# version 03
# baseline (speedup 1.0000x reference)
import math
import contextlib
import numpy as np
import ml_dtypes
import concourse.bass as bass
import concourse.mybir as mybir
from concourse.bass_utils import run_bass_kernel_spmd

F32 = mybir.dt.float32
BF16 = mybir.dt.bfloat16
AF = mybir.ActivationFunctionType
ALU = mybir.AluOpType
AX = mybir.AxisListType
BF = ml_dtypes.bfloat16

NEG30 = -1.0e30
NEG9 = -1.0e9


def full_cfg():
    return dict(B=4, S=2048, D=4096, H0=32, DQ0=192, H1=32, KVH1=8, GRP=4,
                NE=28, RK=8, R0=64, KREM=410, LS0=16.0 / 64.0)


def own_tiles(parity, NT):
    a, b = [], []
    for idx in range(NT // 2):
        lo, hi = idx, NT - 1 - idx
        (a if idx % 2 == 0 else b).extend([lo, hi])
    return sorted(a) if parity == 0 else sorted(b)


def wins(total, w=512):
    return [(s, min(total, s + w)) for s in range(0, total, w)]


class Em:
    def __init__(self, nc):
        self.nc = nc
        self.sem_ctx = nc.semaphore("g")
        self.sem = self.sem_ctx.__enter__()
        self.cum = 0
        self.ops = []

    def emit(self, eng, fn, inc=1, wait=True):
        self.ops.append((eng, fn, inc, self.cum if wait else None))
        self.cum += inc

    def dma(self, out, in_, wait=True, **kw):
        self.emit("sync", lambda e: e.dma_start(out, in_, **kw), inc=16, wait=wait)

    def build_block(self):
        nc = self.nc
        sem = self.sem
        with nc.Block() as block:
            by = {}
            for (eng, fn, inc, wt) in self.ops:
                by.setdefault(eng, []).append((fn, inc, wt))

            def mk(items):
                def sec(e):
                    for fn, inc, wt in items:
                        if wt is not None and wt > 0:
                            e.wait_ge(sem, wt)
                        fn(e).then_inc(sem, inc)
                return sec

            for eng in list(by):
                getattr(block, eng)(mk(by[eng]))
            final = self.cum

            @block.sync
            def _(sync):
                sync.wait_ge(sem, final)
        self.sem_ctx.__exit__(None, None, None)


def ln_transpose(em, st, x_hbm, NT, D, S, xnT, ident_bf, own, xnTo=None, eps=1e-5):
    CK = D // 128
    xs, sqh, red, red2, r2b, xnb, pT = (st["xs"], st["sqh"], st["red"], st["red2"],
                                        st["r2b"], st["xnb"], st["pT"])
    epsb = st["epsb"]
    H = D // 2
    em.emit("vector", lambda e: e.memset(epsb[:, :], float(eps)))
    for ti in range(NT):
        em.dma(xs[:, :], x_hbm[ti * 128:(ti + 1) * 128, :])
        em.emit("vector", lambda e: e.tensor_reduce(red[:, :], xs[:, :], AX.X, ALU.add))
        em.emit("vector", lambda e: e.tensor_scalar(red[:, :], red[:, :], 1.0 / D, None, op0=ALU.mult))
        em.emit("vector", lambda e: e.tensor_scalar(xs[:, :], xs[:, :], red[:, :], None, op0=ALU.subtract))
        em.emit("vector", lambda e: e.tensor_mul(sqh[:, :], xs[:, 0:H], xs[:, 0:H]))
        em.emit("vector", lambda e: e.tensor_reduce(red2[:, :], sqh[:, :], AX.X, ALU.add))
        em.emit("vector", lambda e: e.tensor_mul(sqh[:, :], xs[:, H:D], xs[:, H:D]))
        em.emit("vector", lambda e: e.tensor_reduce(r2b[:, :], sqh[:, :], AX.X, ALU.add))
        em.emit("vector", lambda e: e.tensor_add(red2[:, :], red2[:, :], r2b[:, :]))
        em.emit("vector", lambda e: e.tensor_scalar(red2[:, :], red2[:, :], 1.0 / D, None, op0=ALU.mult))
        em.emit("scalar", lambda e: e.activation(red2[:, :], red2[:, :], AF.Sqrt, bias=epsb[:, :]))
        em.emit("vector", lambda e: e.reciprocal(red2[:, :], red2[:, :]))
        em.emit("vector", lambda e: e.tensor_scalar(xnb[:, :], xs[:, :], red2[:, :], None, op0=ALU.mult))
        for ck in range(CK):
            em.emit("tensor", lambda e, _ck=ck: e.transpose(pT[:, :], xnb[:, _ck * 128:(_ck + 1) * 128], ident_bf[:, :]))
            em.emit("vector", lambda e, _ck=ck, _ti=ti: e.tensor_copy(
                xnT[:, _ck * S + _ti * 128: _ck * S + (_ti + 1) * 128], pT[:, :]))
    if xnTo is not None:
        So = len(own) * 128
        for ck in range(CK):
            for j, ti in enumerate(own):
                em.dma(xnTo[ck * 128:(ck + 1) * 128, j * 128:(j + 1) * 128],
                       xnT[:, ck * S + ti * 128: ck * S + (ti + 1) * 128])


def dma_chunked(em, dst_view, src_view, nck, grp=8):
    for c0 in range(0, nck, grp):
        c1 = min(c0 + grp, nck)
        em.dma(dst_view[:, c0:c1, :], src_view[:, c0:c1, :])


def build_l0a(cfg, parity):
    S, D, H0, DQ0 = cfg["S"], cfg["D"], cfg["H0"], cfg["DQ0"]
    NT, CK = S // 128, D // 128
    own = own_tiles(parity, NT)
    So = len(own) * 128
    scale_q = float(DQ0) ** -0.5

    nc = bass.Bass("TRN2", target_bir_lowering=False)
    x = nc.dram_tensor("x", [S, D], F32, kind="ExternalInput").ap()
    wq = nc.dram_tensor("wq", [D, H0 * 128], BF16, kind="ExternalInput").ap()
    wk = nc.dram_tensor("wk", [D, H0 * 128], BF16, kind="ExternalInput").ap()
    wv = nc.dram_tensor("wv", [D, H0 * 128], BF16, kind="ExternalInput").ap()
    identb = nc.dram_tensor("identb", [128, 128], BF16, kind="ExternalInput").ap()
    tri30 = nc.dram_tensor("tri30", [128, 128], F32, kind="ExternalInput").ap()
    attnT = nc.dram_tensor("attnT", [H0 * 128, So], BF16, kind="ExternalOutput").ap()
    xnTo = nc.dram_tensor("xnTo", [D, So], BF16, kind="ExternalOutput").ap()

    with contextlib.ExitStack() as ctx:
        sb = lambda n, sh, dt: ctx.enter_context(nc.sbuf_tensor(n, sh, dt))
        ps = lambda n, sh, dt: ctx.enter_context(nc.psum_tensor(n, sh, dt))
        xnT = sb("xnT", [128, CK * S], BF16)
        xs = sb("xs", [128, D], F32)
        sqh = sb("sqh", [128, D // 2], F32)
        xnb = sb("xnb", [128, D], BF16)
        wsl = sb("wsl", [128, CK * 128], BF16)
        qT = sb("qT", [128, S], BF16)
        kT = sb("kT", [128, S], BF16)
        vT = sb("vT", [128, S], BF16)
        vtm = sb("vtm", [128, S], BF16)
        idb = sb("idb", [128, 128], BF16)
        tri = sb("tri", [128, 128], F32)
        red = sb("red", [128, 1], F32)
        red2 = sb("red2", [128, 1], F32)
        r2b = sb("r2b", [128, 1], F32)
        epsb = sb("epsb", [128, 1], F32)
        nmax = sb("nmax", [128, 1], F32)
        ssum = sb("ssum", [128, 1], F32)
        p0 = ps("p0", [128, 512], F32)
        pT = ps("pTr", [128, 128], BF16)
        pA = ps("pA", [128, 128], F32)
        sco = xs[:, 0:min(S, 2048)] if S <= 2048 else None
        assert S <= 2048
        sco = xs[:, 0:S]
        prb = xnb[:, 0:S]
        pts = wsl[:, 0:128]

        em = Em(nc)
        em.dma(idb[:, :], identb)
        em.dma(tri[:, :], tri30)
        st = dict(xs=xs, sqh=sqh, red=red, red2=red2, r2b=r2b, xnb=xnb, pT=pT, epsb=epsb)
        ln_transpose(em, st, x, NT, D, S, xnT, idb, own, xnTo)

        def load_wslice(w_hbm, h):
            src = w_hbm[:, h * 128:(h + 1) * 128].rearrange("(ck p) c -> p ck c", p=128)
            dst = wsl[:, :].rearrange("p (ck c) -> p ck c", c=128)
            dma_chunked(em, dst, src, CK)

        def proj(out_sb, scale):
            for (w0, w1) in wins(S):
                for ck in range(CK):
                    em.emit("tensor", lambda e, _ck=ck, _w0=w0, _w1=w1: e.matmul(
                        p0[:, 0:_w1 - _w0], wsl[:, _ck * 128:(_ck + 1) * 128],
                        xnT[:, _ck * S + _w0: _ck * S + _w1],
                        start=(_ck == 0), stop=(_ck == CK - 1)), wait=(ck == 0))
                if scale == 1.0:
                    em.emit("vector", lambda e, _w0=w0, _w1=w1: e.tensor_copy(out_sb[:, _w0:_w1], p0[:, 0:_w1 - _w0]))
                else:
                    em.emit("scalar", lambda e, _w0=w0, _w1=w1: e.mul(out_sb[:, _w0:_w1], p0[:, 0:_w1 - _w0], scale))

        for h in range(H0):
            load_wslice(wq, h)
            proj(qT, scale_q)
            load_wslice(wk, h)
            proj(kT, 1.0)
            load_wslice(wv, h)
            proj(vT, 1.0)
            for ti in range(NT):
                em.emit("tensor", lambda e, _ti=ti: e.transpose(pT[:, :], vT[:, _ti * 128:(_ti + 1) * 128], idb[:, :]))
                em.emit("vector", lambda e, _ti=ti: e.tensor_copy(vtm[:, _ti * 128:(_ti + 1) * 128], pT[:, :]))
            for j, qi in enumerate(own):
                W = (qi + 1) * 128
                for (w0, w1) in wins(W):
                    em.emit("tensor", lambda e, _qi=qi, _w0=w0, _w1=w1: e.matmul(
                        p0[:, 0:_w1 - _w0], qT[:, _qi * 128:(_qi + 1) * 128],
                        kT[:, _w0:_w1], start=True, stop=True))
                    em.emit("vector", lambda e, _w0=w0, _w1=w1: e.tensor_copy(sco[:, _w0:_w1], p0[:, 0:_w1 - _w0]))
                em.emit("vector", lambda e, _qi=qi: e.tensor_add(
                    sco[:, _qi * 128:(_qi + 1) * 128], sco[:, _qi * 128:(_qi + 1) * 128], tri[:, :]))
                em.emit("vector", lambda e, _W=W: e.tensor_reduce(nmax[:, :], sco[:, 0:_W], AX.X, ALU.max, negate=True))
                em.emit("scalar", lambda e, _W=W: e.activation(sco[:, 0:_W], sco[:, 0:_W], AF.Exp, bias=nmax[:, :]))
                em.emit("vector", lambda e, _W=W: e.tensor_reduce(ssum[:, :], sco[:, 0:_W], AX.X, ALU.add))
                em.emit("vector", lambda e: e.reciprocal(ssum[:, :], ssum[:, :]))
                em.emit("vector", lambda e, _W=W: e.tensor_scalar(prb[:, 0:_W], sco[:, 0:_W], ssum[:, :], None, op0=ALU.mult))
                for ti in range(qi + 1):
                    em.emit("tensor", lambda e, _ti=ti: e.transpose(pT[:, :], prb[:, _ti * 128:(_ti + 1) * 128], idb[:, :]))
                    em.emit("vector", lambda e: e.tensor_copy(pts, pT[:, :]))
                    em.emit("tensor", lambda e, _ti=ti, _qi=qi: e.matmul(
                        pA[:, :], vtm[:, _ti * 128:(_ti + 1) * 128], pts,
                        start=(_ti == 0), stop=(_ti == _qi)))
                em.emit("vector", lambda e: e.tensor_copy(xnb[:, S:S + 128] if S + 128 <= D else xnb[:, 0:128], pA[:, :]))
                em.dma(attnT[h * 128:(h + 1) * 128, j * 128:(j + 1) * 128],
                       xnb[:, S:S + 128] if S + 128 <= D else xnb[:, 0:128])
        em.build_block()
    return nc


def build_l0b(cfg, parity):
    S, D, R0, H0, LS0 = cfg["S"], cfg["D"], cfg["R0"], cfg["H0"], cfg["LS0"]
    NT, CK = S // 128, D // 128
    own = own_tiles(parity, NT)
    NTo = len(own)
    So = NTo * 128
    E0 = H0 * 128
    CKE = E0 // 128

    nc = bass.Bass("TRN2", target_bir_lowering=False)
    xo = nc.dram_tensor("xo", [So, D], F32, kind="ExternalInput").ap()
    attnT = nc.dram_tensor("attnT", [E0, So], BF16, kind="ExternalInput").ap()
    xnTo = nc.dram_tensor("xnTo", [D, So], BF16, kind="ExternalInput").ap()
    wo = nc.dram_tensor("wo", [E0, D], BF16, kind="ExternalInput").ap()
    wbT = nc.dram_tensor("wbT", [D, D], BF16, kind="ExternalInput").ap()
    a0T = nc.dram_tensor("a0T", [D, R0], BF16, kind="ExternalInput").ap()
    wlT = nc.dram_tensor("wlT", [D, R0], BF16, kind="ExternalInput").ap()
    b0T = nc.dram_tensor("b0T", [R0, D], BF16, kind="ExternalInput").ap()
    wrT = nc.dram_tensor("wrT", [R0, D], BF16, kind="ExternalInput").ap()
    bb = nc.dram_tensor("bb", [128, D], F32, kind="ExternalInput").ap()
    x1 = nc.dram_tensor("x1", [So, D], F32, kind="ExternalOutput").ap()
    scr = nc.dram_tensor("scr", [So, D], F32, kind="Internal").ap()

    with contextlib.ExitStack() as ctx:
        sb = lambda n, sh, dt: ctx.enter_context(nc.sbuf_tensor(n, sh, dt))
        ps = lambda n, sh, dt: ctx.enter_context(nc.psum_tensor(n, sh, dt))
        act = sb("act", [128, max(CK, CKE) * So], BF16)
        wstr = sb("wstr", [128, CK * 512], BF16)
        aTl = sb("aTl", [128, CK * R0], BF16)
        wTl = sb("wTl", [128, CK * R0], BF16)
        b0s = sb("b0s", [R0, D], BF16)
        wrs = sb("wrs", [R0, D], BF16)
        aT = sb("aT", [R0, So], BF16)
        bbw = sb("bbw", [128, 512], F32)
        xres = sb("xres", [128, 512], F32)
        wop = sb("wop", [128, 512], F32)
        obuf = sb("obuf", [128, 512], F32)
        p0 = ps("p0", [128, 512], F32)
        em = Em(nc)
        actv = act[:, :].rearrange("p (ck c) -> p ck c", c=So)

        dma_chunked(em, actv, attnT.rearrange("(ck p) c -> p ck c", p=128), CKE)
        for (w0, w1) in wins(D):
            dma_chunked(em, wstr[:, 0:CKE * (w1 - w0)].rearrange("p (ck c) -> p ck c", c=w1 - w0),
                        wo[:, w0:w1].rearrange("(ck p) c -> p ck c", p=128), CKE)
            for j in range(NTo):
                for ck in range(CKE):
                    em.emit("tensor", lambda e, _ck=ck, _j=j, _n=w1 - w0: e.matmul(
                        p0[:, 0:_n], act[:, _ck * So + _j * 128: _ck * So + (_j + 1) * 128],
                        wstr[:, _ck * _n:(_ck + 1) * _n],
                        start=(_ck == 0), stop=(_ck == CKE - 1)), wait=(ck == 0))
                em.emit("vector", lambda e, _n=w1 - w0: e.tensor_copy(obuf[:, 0:_n], p0[:, 0:_n]))
                em.dma(scr[j * 128:(j + 1) * 128, w0:w1], obuf[:, 0:w1 - w0])

        dma_chunked(em, actv, xnTo.rearrange("(ck p) c -> p ck c", p=128), CK)
        em.dma(aTl[:, :].rearrange("p (ck c) -> p ck c", c=R0), a0T.rearrange("(ck p) c -> p ck c", p=128))
        em.dma(wTl[:, :].rearrange("p (ck c) -> p ck c", c=R0), wlT.rearrange("(ck p) c -> p ck c", p=128))
        em.dma(b0s[:, :], b0T)
        em.dma(wrs[:, :], wrT)
        for (w0, w1) in wins(So):
            for ck in range(CK):
                em.emit("tensor", lambda e, _ck=ck, _w0=w0, _w1=w1: e.matmul(
                    p0[0:R0, 0:_w1 - _w0], aTl[:, _ck * R0:(_ck + 1) * R0],
                    act[:, _ck * So + _w0: _ck * So + _w1],
                    start=(_ck == 0), stop=False), wait=(ck == 0))
            for ck in range(CK):
                em.emit("tensor", lambda e, _ck=ck, _w0=w0, _w1=w1: e.matmul(
                    p0[0:R0, 0:_w1 - _w0], wTl[:, _ck * R0:(_ck + 1) * R0],
                    act[:, _ck * So + _w0: _ck * So + _w1],
                    start=False, stop=(_ck == CK - 1)), wait=False)
            em.emit("scalar", lambda e, _w0=w0, _w1=w1: e.mul(aT[:, _w0:_w1], p0[0:R0, 0:_w1 - _w0], float(LS0)))
        for (w0, w1) in wins(D):
            n = w1 - w0
            dma_chunked(em, wstr[:, 0:CK * n].rearrange("p (ck c) -> p ck c", c=n),
                        wbT[:, w0:w1].rearrange("(ck p) c -> p ck c", p=128), CK)
            em.dma(bbw[:, 0:n], bb[:, w0:w1])
            for j in range(NTo):
                for ck in range(CK):
                    em.emit("tensor", lambda e, _ck=ck, _j=j, _n=n: e.matmul(
                        p0[:, 0:_n], act[:, _ck * So + _j * 128: _ck * So + (_j + 1) * 128],
                        wstr[:, _ck * _n:(_ck + 1) * _n],
                        start=(_ck == 0), stop=False), wait=(ck == 0))
                em.emit("tensor", lambda e, _j=j, _w0=w0, _w1=w1: e.matmul(
                    p0[:, 0:_w1 - _w0], aT[:, _j * 128:(_j + 1) * 128], b0s[:, _w0:_w1],
                    start=False, stop=False), wait=False)
                em.emit("tensor", lambda e, _j=j, _w0=w0, _w1=w1: e.matmul(
                    p0[:, 0:_w1 - _w0], aT[:, _j * 128:(_j + 1) * 128], wrs[:, _w0:_w1],
                    start=False, stop=True), wait=False)
                em.dma(xres[:, 0:n], xo[j * 128:(j + 1) * 128, w0:w1])
                em.dma(wop[:, 0:n], scr[j * 128:(j + 1) * 128, w0:w1])
                em.emit("vector", lambda e, _n=n: e.tensor_add(obuf[:, 0:_n], p0[:, 0:_n], bbw[:, 0:_n]))
                em.emit("vector", lambda e, _n=n: e.tensor_add(obuf[:, 0:_n], obuf[:, 0:_n], xres[:, 0:_n]))
                em.emit("vector", lambda e, _n=n: e.tensor_add(obuf[:, 0:_n], obuf[:, 0:_n], wop[:, 0:_n]))
                em.dma(x1[j * 128:(j + 1) * 128, w0:w1], obuf[:, 0:n])
        em.build_block()
    return nc


def build_l1a(cfg, parity):
    S, D, H1, KVH1, GRP = cfg["S"], cfg["D"], cfg["H1"], cfg["KVH1"], cfg["GRP"]
    NT, CK = S // 128, D // 128
    own = own_tiles(parity, NT)
    So = len(own) * 128
    scale_s = float(128.0) ** -0.5

    nc = bass.Bass("TRN2", target_bir_lowering=False)
    x = nc.dram_tensor("x", [S, D], F32, kind="ExternalInput").ap()
    wqT = nc.dram_tensor("wqT", [D, H1 * 128], BF16, kind="ExternalInput").ap()
    wkT = nc.dram_tensor("wkT", [D, KVH1 * 128], BF16, kind="ExternalInput").ap()
    wvT = nc.dram_tensor("wvT", [D, KVH1 * 128], BF16, kind="ExternalInput").ap()
    hp1 = nc.dram_tensor("hp1", [H1 * 128, 128], F32, kind="ExternalInput").ap()
    hp2 = nc.dram_tensor("hp2", [H1 * 128, 128], F32, kind="ExternalInput").ap()
    hb1 = nc.dram_tensor("hb1", [128, H1], F32, kind="ExternalInput").ap()
    hb2 = nc.dram_tensor("hb2", [128, H1], F32, kind="ExternalInput").ap()
    cosA = nc.dram_tensor("cosA", [128, S], F32, kind="ExternalInput").ap()
    sinA = nc.dram_tensor("sinA", [128, S], F32, kind="ExternalInput").ap()
    identb = nc.dram_tensor("identb", [128, 128], BF16, kind="ExternalInput").ap()
    xnTo = nc.dram_tensor("xnTo", [D, So], BF16, kind="ExternalOutput").ap()
    qrT = nc.dram_tensor("qrT", [H1 * 128, So], BF16, kind="ExternalOutput").ap()
    qhT = nc.dram_tensor("qhT", [H1 * 128, So], BF16, kind="ExternalOutput").ap()
    krT = nc.dram_tensor("krT", [KVH1 * 128, S], BF16, kind="ExternalOutput").ap()
    khT = nc.dram_tensor("khT", [H1 * 128, S], BF16, kind="ExternalOutput").ap()
    vtmo = nc.dram_tensor("vtmo", [KVH1 * 128, S], BF16, kind="ExternalOutput").ap()

    with contextlib.ExitStack() as ctx:
        sb = lambda n, sh, dt: ctx.enter_context(nc.sbuf_tensor(n, sh, dt))
        ps = lambda n, sh, dt: ctx.enter_context(nc.psum_tensor(n, sh, dt))
        xnT = sb("xnT", [128, CK * S], BF16)
        xs = sb("xs", [128, D], F32)
        sqh = sb("sqh", [128, max(D // 2, S)], F32)
        xnb = sb("xnb", [128, D], BF16)
        wsl = sb("wsl", [128, CK * 128], BF16)
        kf = sb("kf", [128, S], F32)
        krf = sb("krf", [128, S], F32)
        vT = sb("vT", [128, S], BF16)
        vstg = sb("vstg", [128, 128], BF16)
        assert So <= S // 2 or S >= 2 * So
        qf = kf[:, 0:So]
        qrf = kf[:, So:2 * So]
        coss = sb("coss", [128, S], F32)
        sins = sb("sins", [128, S], F32)
        hsb = sb("hsb", [128, 128], F32)
        hsb2 = sb("hsb2", [128, 128], F32)
        hb1s = sb("hb1s", [128, H1], F32)
        hb2s = sb("hb2s", [128, H1], F32)
        idb = sb("idb", [128, 128], BF16)
        red = sb("red", [128, 1], F32)
        red2 = sb("red2", [128, 1], F32)
        r2b = sb("r2b", [128, 1], F32)
        epsb = sb("epsb", [128, 1], F32)
        p0 = ps("p0", [128, 512], F32)
        pT = ps("pTr", [128, 128], BF16)
        rot = sqh[:, 0:S]
        if D >= 2 * S:
            yb = xs[:, 0:S]
            zb = xs[:, S:2 * S]
        else:
            yb = sb("ybb", [128, S], F32)
            zb = sb("zbb", [128, S], F32)
        krb = xnb[:, 0:S]
        outb = xnb[:, S:2 * S] if D >= 2 * S else sb("outbb", [128, S], BF16)

        em = Em(nc)
        em.dma(idb[:, :], identb)
        em.dma(coss[:, :], cosA)
        em.dma(sins[:, :], sinA)
        em.dma(hb1s[:, :], hb1)
        em.dma(hb2s[:, :], hb2)
        st = dict(xs=xs, sqh=sqh[:, 0:D // 2], red=red, red2=red2, r2b=r2b, xnb=xnb, pT=pT, epsb=epsb)
        ln_transpose(em, st, x, NT, D, S, xnT, idb, own, xnTo)

        def load_wslice(w_hbm, h):
            src = w_hbm[:, h * 128:(h + 1) * 128].rearrange("(ck p) c -> p ck c", p=128)
            dst = wsl[:, :].rearrange("p (ck c) -> p ck c", c=128)
            dma_chunked(em, dst, src, CK)

        def proj_full(out_sb):
            for (w0, w1) in wins(S):
                for ck in range(CK):
                    em.emit("tensor", lambda e, _ck=ck, _w0=w0, _w1=w1: e.matmul(
                        p0[:, 0:_w1 - _w0], wsl[:, _ck * 128:(_ck + 1) * 128],
                        xnT[:, _ck * S + _w0: _ck * S + _w1],
                        start=(_ck == 0), stop=(_ck == CK - 1)), wait=(ck == 0))
                em.emit("vector", lambda e, _w0=w0, _w1=w1: e.tensor_copy(out_sb[:, _w0:_w1], p0[:, 0:_w1 - _w0]))

        def proj_own(out_sb):
            for j, ti in enumerate(own):
                for ck in range(CK):
                    em.emit("tensor", lambda e, _ck=ck, _ti=ti: e.matmul(
                        p0[:, 0:128], wsl[:, _ck * 128:(_ck + 1) * 128],
                        xnT[:, _ck * S + _ti * 128: _ck * S + (_ti + 1) * 128],
                        start=(_ck == 0), stop=(_ck == CK - 1)), wait=(ck == 0))
                em.emit("vector", lambda e, _j=j: e.tensor_copy(out_sb[:, _j * 128:(_j + 1) * 128], p0[:, 0:128]))

        def rope_seg(dst, src, c0, c1, cs0):
            n = c1 - c0
            em.emit("scalar", lambda e: e.mul(rot[0:64, c0:c1], src[64:128, c0:c1], -1.0))
            em.emit("scalar", lambda e: e.copy(rot[64:128, c0:c1], src[0:64, c0:c1]))
            em.emit("vector", lambda e: e.tensor_mul(dst[:, c0:c1], src[:, c0:c1], coss[:, cs0:cs0 + n]))
            em.emit("vector", lambda e: e.tensor_mul(rot[:, c0:c1], rot[:, c0:c1], sins[:, cs0:cs0 + n]))
            em.emit("vector", lambda e: e.tensor_add(dst[:, c0:c1], dst[:, c0:c1], rot[:, c0:c1]))

        def hash_sign(src_f, W, h, out_bf):
            em.dma(hsb[:, :], hp1[h * 128:(h + 1) * 128, :])
            em.dma(hsb2[:, :], hp2[h * 128:(h + 1) * 128, :])
            for (w0, w1) in wins(W):
                em.emit("tensor", lambda e, _w0=w0, _w1=w1: e.matmul(
                    p0[:, 0:_w1 - _w0], hsb[:, :], src_f[:, _w0:_w1], start=True, stop=True))
                em.emit("scalar", lambda e, _w0=w0, _w1=w1, _h=h: e.activation(
                    yb[:, _w0:_w1], p0[:, 0:_w1 - _w0], AF.Silu, bias=hb1s[:, _h:_h + 1]))
            em.emit("vector", lambda e, _W=W: e.tensor_add(yb[:, 0:_W], yb[:, 0:_W], src_f[:, 0:_W]))
            for (w0, w1) in wins(W):
                em.emit("tensor", lambda e, _w0=w0, _w1=w1: e.matmul(
                    p0[:, 0:_w1 - _w0], hsb2[:, :], yb[:, _w0:_w1], start=True, stop=True))
                em.emit("vector", lambda e, _w0=w0, _w1=w1, _h=h: e.scalar_tensor_tensor(
                    zb[:, _w0:_w1], p0[:, 0:_w1 - _w0], hb2s[:, _h:_h + 1],
                    yb[:, _w0:_w1], op0=ALU.add, op1=ALU.add))
            em.emit("scalar", lambda e, _W=W: e.activation(out_bf[:, 0:_W], zb[:, 0:_W], AF.Sign))

        for kv in range(KVH1):
            load_wslice(wkT, kv)
            proj_full(kf)
            rope_seg(krf, kf, 0, S, 0)
            em.emit("vector", lambda e: e.tensor_copy(krb, krf[:, :]))
            em.dma(krT[kv * 128:(kv + 1) * 128, :], krb)
            load_wslice(wvT, kv)
            proj_full(vT)
            for ti in range(NT):
                em.emit("tensor", lambda e, _ti=ti: e.transpose(pT[:, :], vT[:, _ti * 128:(_ti + 1) * 128], idb[:, :]))
                em.emit("vector", lambda e: e.tensor_copy(vstg[:, :], pT[:, :]))
                em.dma(vtmo[kv * 128:(kv + 1) * 128, ti * 128:(ti + 1) * 128], vstg[:, :])
            for g in range(GRP):
                h = kv * GRP + g
                hash_sign(krf, S, h, outb)
                em.dma(khT[h * 128:(h + 1) * 128, :], outb[:, 0:S])
                load_wslice(wqT, h)
                proj_own(qf)
                for j, ti in enumerate(own):
                    rope_seg(qrf, qf, j * 128, (j + 1) * 128, ti * 128)
                hash_sign(qrf, So, h, outb)
                em.dma(qhT[h * 128:(h + 1) * 128, :], outb[:, 0:So])
                em.emit("scalar", lambda e: e.mul(outb[:, 0:So], qrf[:, 0:So], scale_s))
                em.dma(qrT[h * 128:(h + 1) * 128, :], outb[:, 0:So])
        em.build_block()
    return nc


def build_l1b(cfg, parity):
    S, H1, KVH1, GRP, KREM = cfg["S"], cfg["H1"], cfg["KVH1"], cfg["GRP"], cfg["KREM"]
    NT = S // 128
    own = own_tiles(parity, NT)
    So = len(own) * 128
    span = 2 * (128 * S + S)
    nsteps = int(math.ceil(math.log2(span)))
    lo_init = -float(128 * S + S)

    nc = bass.Bass("TRN2", target_bir_lowering=False)
    qrT = nc.dram_tensor("qrT", [H1 * 128, So], BF16, kind="ExternalInput").ap()
    qhT = nc.dram_tensor("qhT", [H1 * 128, So], BF16, kind="ExternalInput").ap()
    krT = nc.dram_tensor("krT", [KVH1 * 128, S], BF16, kind="ExternalInput").ap()
    khT = nc.dram_tensor("khT", [H1 * 128, S], BF16, kind="ExternalInput").ap()
    vtmi = nc.dram_tensor("vtmi", [KVH1 * 128, S], BF16, kind="ExternalInput").ap()
    identb = nc.dram_tensor("identb", [128, 128], BF16, kind="ExternalInput").ap()
    tri9 = nc.dram_tensor("tri9", [128, 128], F32, kind="ExternalInput").ap()
    tri30 = nc.dram_tensor("tri30", [128, 128], F32, kind="ExternalInput").ap()
    revbc = nc.dram_tensor("revbc", [128, S], F32, kind="ExternalInput").ap()
    attn1T = nc.dram_tensor("attn1T", [H1 * 128, So], BF16, kind="ExternalOutput").ap()

    with contextlib.ExitStack() as ctx:
        sb = lambda n, sh, dt: ctx.enter_context(nc.sbuf_tensor(n, sh, dt))
        ps = lambda n, sh, dt: ctx.enter_context(nc.psum_tensor(n, sh, dt))
        qr = sb("qr", [128, So], BF16)
        qh = sb("qh", [128, So], BF16)
        kr = sb("kr", [128, S], BF16)
        kh = sb("kh", [128, S], BF16)
        vtm = sb("vtm", [128, S], BF16)
        ub = sb("ub", [128, S], F32)
        sgn = sb("sgn", [128, S], F32)
        sco = sb("sco", [128, S], F32)
        sel = sb("sel", [128, S], mybir.dt.uint32)
        rev = sb("rev", [128, S], F32)
        negb = sb("negb", [128, S], F32)
        prb = sb("prb", [128, S], BF16)
        pts = sb("pts", [128, 128], BF16)
        ast = sb("ast", [128, So], BF16)
        idb = sb("idb", [128, 128], BF16)
        t9 = sb("t9", [128, 128], F32)
        t30 = sb("t30", [128, 128], F32)
        nmb = sb("nmb", [128, 1], F32)
        cnt = sb("cnt", [128, 1], F32)
        tmp1 = sb("tmp1", [128, 1], F32)
        kap = sb("kap", [128, 1], F32)
        nmax = sb("nmax", [128, 1], F32)
        ssum = sb("ssum", [128, 1], F32)
        p0 = ps("p0", [128, 512], F32)
        pT = ps("pTr", [128, 128], BF16)
        pA = ps("pA", [128, 128], F32)

        em = Em(nc)
        em.dma(idb[:, :], identb)
        em.dma(t9[:, :], tri9)
        em.dma(t30[:, :], tri30)
        em.dma(rev[:, :], revbc)
        em.emit("vector", lambda e: e.memset(negb[:, :], NEG30))
        for h in range(H1):
            kv = h // GRP
            em.dma(qr[:, :], qrT[h * 128:(h + 1) * 128, :])
            em.dma(qh[:, :], qhT[h * 128:(h + 1) * 128, :])
            em.dma(kr[:, :], krT[kv * 128:(kv + 1) * 128, :])
            em.dma(kh[:, :], khT[h * 128:(h + 1) * 128, :])
            em.dma(vtm[:, :], vtmi[kv * 128:(kv + 1) * 128, :])
            for j, qi in enumerate(own):
                W = (qi + 1) * 128
                for (w0, w1) in wins(W):
                    em.emit("tensor", lambda e, _j=j, _w0=w0, _w1=w1: e.matmul(
                        p0[:, 0:_w1 - _w0], qh[:, _j * 128:(_j + 1) * 128],
                        kh[:, _w0:_w1], start=True, stop=True))
                    em.emit("vector", lambda e, _w0=w0, _w1=w1: e.tensor_copy(ub[:, _w0:_w1], p0[:, 0:_w1 - _w0]))
                em.emit("vector", lambda e, _qi=qi: e.tensor_add(
                    ub[:, _qi * 128:(_qi + 1) * 128], ub[:, _qi * 128:(_qi + 1) * 128], t9[:, :]))
                em.emit("vector", lambda e, _W=W: e.tensor_scalar(ub[:, 0:_W], ub[:, 0:_W], float(S), None, op0=ALU.mult))
                em.emit("vector", lambda e, _W=W: e.tensor_add(ub[:, 0:_W], ub[:, 0:_W], rev[:, 0:_W]))
                step0 = float(2 ** (nsteps - 1))
                em.emit("vector", lambda e, _s=step0: e.memset(nmb[:, :], -lo_init - _s + 0.5))
                thr = float(2 * KREM - W)
                for it in range(nsteps):
                    step = float(2 ** (nsteps - 1 - it))
                    nstep = float(2 ** (nsteps - 2 - it)) if it < nsteps - 1 else 0.5
                    em.emit("scalar", lambda e, _W=W: e.activation(sgn[:, 0:_W], ub[:, 0:_W], AF.Sign, bias=nmb[:, :]))
                    em.emit("vector", lambda e, _W=W: e.tensor_reduce(cnt[:, :], sgn[:, 0:_W], AX.X, ALU.add))
                    em.emit("vector", lambda e, _t=thr, _s=step: e.tensor_scalar(
                        tmp1[:, :], cnt[:, :], _t, -_s, op0=ALU.is_ge, op1=ALU.mult))
                    em.emit("vector", lambda e, _ns=nstep: e.scalar_tensor_tensor(
                        nmb[:, :], tmp1[:, :], _ns, nmb[:, :], op0=ALU.add, op1=ALU.add))
                em.emit("vector", lambda e: e.tensor_scalar(kap[:, :], nmb[:, :], -1.0, None, op0=ALU.mult))
                em.emit("vector", lambda e, _W=W: e.tensor_scalar(sel[:, 0:_W], ub[:, 0:_W], kap[:, :], None, op0=ALU.is_lt))
                for (w0, w1) in wins(W):
                    em.emit("tensor", lambda e, _j=j, _w0=w0, _w1=w1: e.matmul(
                        p0[:, 0:_w1 - _w0], qr[:, _j * 128:(_j + 1) * 128],
                        kr[:, _w0:_w1], start=True, stop=True))
                    em.emit("vector", lambda e, _w0=w0, _w1=w1: e.tensor_copy(sco[:, _w0:_w1], p0[:, 0:_w1 - _w0]))
                em.emit("vector", lambda e, _qi=qi: e.tensor_add(
                    sco[:, _qi * 128:(_qi + 1) * 128], sco[:, _qi * 128:(_qi + 1) * 128], t30[:, :]))
                em.emit("vector", lambda e, _W=W: e.copy_predicated(sco[:, 0:_W], sel[:, 0:_W], negb[:, 0:_W]))
                em.emit("vector", lambda e, _W=W: e.tensor_reduce(nmax[:, :], sco[:, 0:_W], AX.X, ALU.max, negate=True))
                em.emit("scalar", lambda e, _W=W: e.activation(sco[:, 0:_W], sco[:, 0:_W], AF.Exp, bias=nmax[:, :]))
                em.emit("vector", lambda e, _W=W: e.tensor_reduce(ssum[:, :], sco[:, 0:_W], AX.X, ALU.add))
                em.emit("vector", lambda e: e.reciprocal(ssum[:, :], ssum[:, :]))
                em.emit("vector", lambda e, _W=W: e.tensor_scalar(prb[:, 0:_W], sco[:, 0:_W], ssum[:, :], None, op0=ALU.mult))
                for ti in range(qi + 1):
                    em.emit("tensor", lambda e, _ti=ti: e.transpose(pT[:, :], prb[:, _ti * 128:(_ti + 1) * 128], idb[:, :]))
                    em.emit("vector", lambda e: e.tensor_copy(pts[:, :], pT[:, :]))
                    em.emit("tensor", lambda e, _ti=ti, _qi=qi: e.matmul(
                        pA[:, :], vtm[:, _ti * 128:(_ti + 1) * 128], pts[:, :],
                        start=(_ti == 0), stop=(_ti == _qi)))
                em.emit("vector", lambda e, _j=j: e.tensor_copy(ast[:, _j * 128:(_j + 1) * 128], pA[:, :]))
            em.dma(attn1T[h * 128:(h + 1) * 128, :], ast[:, :])
        em.build_block()
    return nc


def build_l1c(cfg, parity):
    S, D, NE, RK, H1 = cfg["S"], cfg["D"], cfg["NE"], cfg["RK"], cfg["H1"]
    NT, CK = S // 128, D // 128
    own = own_tiles(parity, NT)
    NTo = len(own)
    So = NTo * 128
    ER = NE * RK
    NKC = (ER + 127) // 128
    ERp = NKC * 128

    nc = bass.Bass("TRN2", target_bir_lowering=False)
    xo = nc.dram_tensor("xo", [So, D], F32, kind="ExternalInput").ap()
    attnT = nc.dram_tensor("attnT", [H1 * 128, So], BF16, kind="ExternalInput").ap()
    xnTo = nc.dram_tensor("xnTo", [D, So], BF16, kind="ExternalInput").ap()
    woT = nc.dram_tensor("woT", [D, D], BF16, kind="ExternalInput").ap()
    wbT = nc.dram_tensor("wbT", [D, D], BF16, kind="ExternalInput").ap()
    gwT = nc.dram_tensor("gwT", [D, NE], BF16, kind="ExternalInput").ap()
    afT = nc.dram_tensor("afT", [D, ER], BF16, kind="ExternalInput").ap()
    bfl = nc.dram_tensor("bfl", [ERp, D], BF16, kind="ExternalInput").ap()
    repc = nc.dram_tensor("repc", [NE, ER], F32, kind="ExternalInput").ap()
    identf = nc.dram_tensor("identf", [128, 128], F32, kind="ExternalInput").ap()
    out = nc.dram_tensor("out", [So, D], F32, kind="ExternalOutput").ap()
    scr = nc.dram_tensor("scr", [So, D], F32, kind="Internal").ap()

    with contextlib.ExitStack() as ctx:
        sb = lambda n, sh, dt: ctx.enter_context(nc.sbuf_tensor(n, sh, dt))
        ps = lambda n, sh, dt: ctx.enter_context(nc.psum_tensor(n, sh, dt))
        act = sb("act", [128, CK * So], BF16)
        wstr = sb("wstr", [128, CK * 512], BF16)
        gws = sb("gws", [128, CK * NE], BF16)
        afs = sb("afs", [128, CK * ER], BF16)
        bstr = sb("bstr", [128, NKC * 512], BF16)
        hT = sb("hT", [128, NTo * NKC * 128], BF16)
        reps = sb("reps", [NE, ER], F32)
        idf = sb("idf", [128, 128], F32)
        lg = sb("lg", [128, NE], F32)
        lg2 = sb("lg2", [128, NE], F32)
        eq1 = sb("eq1", [128, NE], F32)
        eq2 = sb("eq2", [128, NE], F32)
        gate = sb("gate", [128, NE], F32)
        negb = sb("negb", [128, NE], F32)
        gtT = sb("gtT", [NE, 128], F32)
        hb = sb("hb", [128, ER], F32)
        grep = sb("grep", [128, ER], F32)
        m1 = sb("m1", [128, 1], F32)
        m2 = sb("m2", [128, 1], F32)
        dm = sb("dm", [128, 1], F32)
        p1s = sb("p1s", [128, 1], F32)
        p2s = sb("p2s", [128, 1], F32)
        xres = sb("xres", [128, 512], F32)
        wop = sb("wop", [128, 512], F32)
        obuf = sb("obuf", [128, 512], F32)
        p0 = ps("p0", [128, 512], F32)
        pf = ps("pf", [128, 128], F32)
        em = Em(nc)
        actv = act[:, :].rearrange("p (ck c) -> p ck c", c=So)

        dma_chunked(em, actv, attnT.rearrange("(ck p) c -> p ck c", p=128), CK)
        for (w0, w1) in wins(D):
            n = w1 - w0
            dma_chunked(em, wstr[:, 0:CK * n].rearrange("p (ck c) -> p ck c", c=n),
                        woT[:, w0:w1].rearrange("(ck p) c -> p ck c", p=128), CK)
            for j in range(NTo):
                for ck in range(CK):
                    em.emit("tensor", lambda e, _ck=ck, _j=j, _n=n: e.matmul(
                        p0[:, 0:_n], act[:, _ck * So + _j * 128: _ck * So + (_j + 1) * 128],
                        wstr[:, _ck * _n:(_ck + 1) * _n],
                        start=(_ck == 0), stop=(_ck == CK - 1)), wait=(ck == 0))
                em.emit("vector", lambda e, _n=n: e.tensor_copy(obuf[:, 0:_n], p0[:, 0:_n]))
                em.dma(scr[j * 128:(j + 1) * 128, w0:w1], obuf[:, 0:n])

        dma_chunked(em, actv, xnTo.rearrange("(ck p) c -> p ck c", p=128), CK)
        em.dma(gws[:, :].rearrange("p (ck c) -> p ck c", c=NE), gwT.rearrange("(ck p) c -> p ck c", p=128))
        em.dma(afs[:, :].rearrange("p (ck c) -> p ck c", c=ER), afT.rearrange("(ck p) c -> p ck c", p=128))
        em.dma(reps[:, :], repc)
        em.dma(idf[:, :], identf)
        em.emit("vector", lambda e: e.memset(negb[:, :], NEG30))
        for j in range(NTo):
            for ck in range(CK):
                em.emit("tensor", lambda e, _ck=ck, _j=j: e.matmul(
                    p0[:, 0:NE], act[:, _ck * So + _j * 128: _ck * So + (_j + 1) * 128],
                    gws[:, _ck * NE:(_ck + 1) * NE],
                    start=(_ck == 0), stop=(_ck == CK - 1)), wait=(ck == 0))
            em.emit("vector", lambda e: e.tensor_copy(lg[:, :], p0[:, 0:NE]))
            em.emit("vector", lambda e: e.tensor_reduce(m1[:, :], lg[:, :], AX.X, ALU.max))
            em.emit("vector", lambda e: e.tensor_scalar(eq1[:, :], lg[:, :], m1[:, :], None, op0=ALU.is_ge))
            em.emit("vector", lambda e: e.tensor_scalar(lg2[:, :], eq1[:, :], NEG30, None, op0=ALU.mult))
            em.emit("vector", lambda e: e.tensor_add(lg2[:, :], lg2[:, :], lg[:, :]))
            em.emit("vector", lambda e: e.tensor_reduce(m2[:, :], lg2[:, :], AX.X, ALU.max))
            em.emit("vector", lambda e: e.tensor_scalar(eq2[:, :], lg2[:, :], m2[:, :], None, op0=ALU.is_ge))
            em.emit("vector", lambda e: e.tensor_sub(dm[:, :], m1[:, :], m2[:, :]))
            em.emit("scalar", lambda e: e.activation(p1s[:, :], dm[:, :], AF.Sigmoid))
            em.emit("vector", lambda e: e.tensor_sub(dm[:, :], m2[:, :], m1[:, :]))
            em.emit("scalar", lambda e: e.activation(p2s[:, :], dm[:, :], AF.Sigmoid))
            em.emit("vector", lambda e: e.tensor_scalar(eq1[:, :], eq1[:, :], p1s[:, :], 2.0, op0=ALU.mult, op1=ALU.mult))
            em.emit("vector", lambda e: e.tensor_scalar(eq2[:, :], eq2[:, :], p2s[:, :], 2.0, op0=ALU.mult, op1=ALU.mult))
            em.emit("vector", lambda e: e.tensor_add(gate[:, :], eq1[:, :], eq2[:, :]))
            for ck in range(CK):
                em.emit("tensor", lambda e, _ck=ck, _j=j: e.matmul(
                    p0[:, 0:ER], act[:, _ck * So + _j * 128: _ck * So + (_j + 1) * 128],
                    afs[:, _ck * ER:(_ck + 1) * ER],
                    start=(_ck == 0), stop=(_ck == CK - 1)), wait=(ck == 0))
            em.emit("vector", lambda e: e.tensor_copy(hb[:, :], p0[:, 0:ER]))
            em.emit("tensor", lambda e: e.transpose(pf[0:NE, :], gate[:, 0:NE], idf[:, :]))
            em.emit("vector", lambda e: e.tensor_copy(gtT[:, :], pf[0:NE, :]))
            em.emit("tensor", lambda e: e.matmul(p0[:, 0:ER], gtT[:, :], reps[:, :], start=True, stop=True))
            em.emit("vector", lambda e: e.tensor_tensor(grep[:, :], hb[:, :], p0[:, 0:ER], op=ALU.mult))
            for kc in range(NKC):
                c0 = kc * 128
                c1 = min(ER, c0 + 128)
                em.emit("tensor", lambda e, _c0=c0, _c1=c1: e.transpose(pf[0:_c1 - _c0, :], grep[:, _c0:_c1], idf[:, :]))
                em.emit("vector", lambda e, _j=j, _kc=kc, _c0=c0, _c1=c1: e.tensor_copy(
                    hT[0:_c1 - _c0, (_j * NKC + _kc) * 128:(_j * NKC + _kc) * 128 + 128], pf[0:_c1 - _c0, :]))
        for (w0, w1) in wins(D):
            n = w1 - w0
            dma_chunked(em, wstr[:, 0:CK * n].rearrange("p (ck c) -> p ck c", c=n),
                        wbT[:, w0:w1].rearrange("(ck p) c -> p ck c", p=128), CK)
            em.dma(bstr[:, 0:NKC * n].rearrange("p (kc c) -> p kc c", c=n),
                   bfl[:, w0:w1].rearrange("(kc p) c -> p kc c", p=128))
            for j in range(NTo):
                for ck in range(CK):
                    em.emit("tensor", lambda e, _ck=ck, _j=j, _n=n: e.matmul(
                        p0[:, 0:_n], act[:, _ck * So + _j * 128: _ck * So + (_j + 1) * 128],
                        wstr[:, _ck * _n:(_ck + 1) * _n],
                        start=(_ck == 0), stop=False), wait=(ck == 0))
                for kc in range(NKC):
                    c0 = kc * 128
                    c1 = min(ER, c0 + 128)
                    em.emit("tensor", lambda e, _j=j, _kc=kc, _c0=c0, _c1=c1, _n=n: e.matmul(
                        p0[:, 0:_n], hT[0:_c1 - _c0, (_j * NKC + _kc) * 128:(_j * NKC + _kc) * 128 + 128],
                        bstr[0:_c1 - _c0, _kc * _n:(_kc + 1) * _n],
                        start=False, stop=(_kc == NKC - 1)), wait=False)
                em.dma(xres[:, 0:n], xo[j * 128:(j + 1) * 128, w0:w1])
                em.dma(wop[:, 0:n], scr[j * 128:(j + 1) * 128, w0:w1])
                em.emit("vector", lambda e, _n=n: e.tensor_add(obuf[:, 0:_n], p0[:, 0:_n], xres[:, 0:_n]))
                em.emit("vector", lambda e, _n=n: e.tensor_add(obuf[:, 0:_n], obuf[:, 0:_n], wop[:, 0:_n]))
                em.dma(out[j * 128:(j + 1) * 128, w0:w1], obuf[:, 0:n])
        em.build_block()
    return nc


def build_prep(rows, cols=8192):
    nc = bass.Bass("TRN2", target_bir_lowering=False)
    src = nc.dram_tensor("src", [rows, cols], F32, kind="ExternalInput").ap()
    dst = nc.dram_tensor("dst", [rows, cols], BF16, kind="ExternalOutput").ap()
    with contextlib.ExitStack() as ctx:
        xs = ctx.enter_context(nc.sbuf_tensor("xs", [128, cols], F32))
        xb = ctx.enter_context(nc.sbuf_tensor("xb", [128, cols], BF16))
        em = Em(nc)
        for t in range(rows // 128):
            em.dma(xs[:, :], src[t * 128:(t + 1) * 128, :])
            em.emit("scalar", lambda e: e.copy(xb[:, :], xs[:, :]))
            em.dma(dst[t * 128:(t + 1) * 128, :], xb[:, :])
        em.build_block()
    return nc



def rope_tables(S, Dh=128, base=10000.0):
    inv = 1.0 / base ** (np.arange(0, Dh, 2, dtype=np.float64) / Dh)
    t = np.arange(S, dtype=np.float64)
    fr = np.outer(t, inv)
    emb = np.concatenate([fr, fr], -1)
    return np.cos(emb).astype(np.float32), np.sin(emb).astype(np.float32)


def make_consts(cfg):
    S = cfg["S"]
    c = {}
    c["identb"] = np.eye(128, dtype=np.float32).astype(BF)
    c["identf"] = np.eye(128, dtype=np.float32)
    tri = np.triu(np.ones((128, 128), np.float32), 1)
    c["tri30"] = (tri * NEG30).astype(np.float32)
    c["tri9"] = (tri * NEG9).astype(np.float32)
    c["revbc"] = np.ascontiguousarray(
        np.tile((S - 1 - np.arange(S, dtype=np.float32))[None, :], (128, 1)))
    NE, RK = cfg["NE"], cfg["RK"]
    rep = np.zeros((NE, NE * RK), np.float32)
    for e in range(NE):
        rep[e, e * RK:(e + 1) * RK] = 1.0
    c["repc"] = rep
    cos, sin = rope_tables(S)
    c["cosA"] = np.ascontiguousarray(cos.T)
    c["sinA"] = np.ascontiguousarray(sin.T)
    return c


def prep_weights(cfg, ins):
    D, H0, DQ0, NE, RK = cfg["D"], cfg["H0"], cfg["DQ0"], cfg["NE"], cfg["RK"]
    ER = NE * RK
    NKC = (ER + 127) // 128
    ERp = NKC * 128
    C = np.ascontiguousarray
    w = {}
    wq = np.asarray(ins["L0_attn_wq"], np.float32)
    w["l0_wq"] = C(wq.reshape(D, H0, DQ0)[:, :, :128].reshape(D, H0 * 128))
    wkv = np.asarray(ins["L0_attn_wkv"], np.float32)
    w["l0_wk"] = C(wkv[:, :H0 * 128])
    w["l0_wv"] = C(wkv[:, H0 * 128:2 * H0 * 128])
    w["l0_wo"] = C(np.asarray(ins["L0_attn_wo"], np.float32))
    w["l0_wbT"] = C(np.asarray(ins["L0_ffn_W_base"], np.float32).T)
    w["l0_a0T"] = C(np.asarray(ins["L0_ffn_A"], np.float32).T)
    w["l0_wlT"] = C(np.asarray(ins["L0_ffn_WL"], np.float32).T)
    w["l0_b0T"] = C(np.asarray(ins["L0_ffn_B"], np.float32).T)
    w["l0_wrT"] = C(np.asarray(ins["L0_ffn_WR"], np.float32).T)
    w["l1_wqT"] = C(np.asarray(ins["L1_attn_Wq"], np.float32).T)
    w["l1_wkT"] = C(np.asarray(ins["L1_attn_Wk"], np.float32).T)
    w["l1_wvT"] = C(np.asarray(ins["L1_attn_Wv"], np.float32).T)
    w["l1_woT"] = C(np.asarray(ins["L1_attn_Wo"], np.float32).T)
    w["l1_wbT"] = C(np.asarray(ins["L1_ffn_base_W"], np.float32).T)
    w["l1_gwT"] = C(np.asarray(ins["L1_ffn_gate_W"], np.float32).T)
    la = np.asarray(ins["L1_ffn_lora_A"], np.float32).reshape(ER, D)
    w["l1_afT"] = C(la.T)
    lb = np.asarray(ins["L1_ffn_lora_B"], np.float32)
    bfl = lb.transpose(0, 2, 1).reshape(ER, D)
    if ERp != ER:
        bfl = np.concatenate([bfl, np.zeros((ERp - ER, D), np.float32)], 0)
    w["l1_bfl"] = C(bfl)
    return w


def convert_bf16_on_device(w, ncores, runner):
    order = sorted(w.keys())
    flats = [w[k].reshape(-1) for k in order]
    sizes = [f.size for f in flats]
    total = int(np.sum(sizes))
    cols = 8192
    per = -(-total // (ncores * 128 * cols)) * 128 * cols
    buf = np.zeros(ncores * per, np.float32)
    buf[:total] = np.concatenate(flats)
    shards = buf.reshape(ncores, per // cols, cols)
    nc = build_prep(per // cols, cols)
    res = runner(nc, [{"src": np.ascontiguousarray(s)} for s in shards], list(range(ncores)))
    outf = np.concatenate([np.asarray(r["dst"]).reshape(-1) for r in res])
    wb = {}
    off = 0
    for k, sz in zip(order, sizes):
        wb[k] = np.ascontiguousarray(outf[off:off + sz].reshape(w[k].shape))
        off += sz
    return wb


def _hw_runner(nc, in_maps, core_ids):
    return run_bass_kernel_spmd(nc, in_maps, core_ids=core_ids).results


def kernel(**ins):
    return run_all(full_cfg(), ins, ncores=8, runner=_hw_runner)


def run_all(cfg, ins, ncores, runner):
    B, S, D, H1 = cfg["B"], cfg["S"], cfg["D"], cfg["H1"]
    NT = S // 128
    x0 = np.ascontiguousarray(np.asarray(ins["L0_attn_hidden_states"], np.float32).reshape(B, S, D))
    consts = make_consts(cfg)
    w = prep_weights(cfg, ins)
    wb = convert_bf16_on_device(w, ncores, runner)
    bb_bc = np.ascontiguousarray(np.tile(np.asarray(ins["L0_ffn_b_base"], np.float32)[None, :], (128, 1)))
    hb1 = np.ascontiguousarray(np.asarray(ins["L1_attn_hash_bias1"], np.float32).reshape(H1, 128).T)
    hb2 = np.ascontiguousarray(np.asarray(ins["L1_attn_hash_bias2"], np.float32).reshape(H1, 128).T)
    hp1 = np.ascontiguousarray(np.asarray(ins["L1_attn_hash_proj1"], np.float32).reshape(H1 * 128, 128))
    hp2 = np.ascontiguousarray(np.asarray(ins["L1_attn_hash_proj2"], np.float32).reshape(H1 * 128, 128))

    cores = list(range(ncores))
    parities = [c % 2 for c in cores]
    bids = [c // 2 for c in cores]
    owns = [own_tiles(p, NT) for p in parities]

    def gather_rows(arr_b, ot):
        return np.ascontiguousarray(np.concatenate([arr_b[ti * 128:(ti + 1) * 128] for ti in ot], 0))

    def run_pair(builder, mapper):
        ncs = [builder(cfg, 0), builder(cfg, 1)]
        ev = [c for c in cores if c % 2 == 0]
        od = [c for c in cores if c % 2 == 1]
        r0 = runner(ncs[0], [mapper(c) for c in ev], ev)
        r1 = runner(ncs[1], [mapper(c) for c in od], od)
        res = {}
        for i, c in enumerate(ev):
            res[c] = r0[i]
        for i, c in enumerate(od):
            res[c] = r1[i]
        return res

    resa = run_pair(build_l0a, lambda c: {
        "x": x0[bids[c]], "wq": wb["l0_wq"], "wk": wb["l0_wk"], "wv": wb["l0_wv"],
        "identb": consts["identb"], "tri30": consts["tri30"]})
    resb = run_pair(build_l0b, lambda c: {
        "xo": gather_rows(x0[bids[c]], owns[c]),
        "attnT": np.asarray(resa[c]["attnT"]), "xnTo": np.asarray(resa[c]["xnTo"]),
        "wo": wb["l0_wo"], "wbT": wb["l0_wbT"], "a0T": wb["l0_a0T"], "wlT": wb["l0_wlT"],
        "b0T": wb["l0_b0T"], "wrT": wb["l0_wrT"], "bb": bb_bc})
    x1 = np.zeros((B, S, D), np.float32)
    for c in cores:
        xc = np.asarray(resb[c]["x1"])
        for j, ti in enumerate(owns[c]):
            x1[bids[c], ti * 128:(ti + 1) * 128] = xc[j * 128:(j + 1) * 128]
    res1a = run_pair(build_l1a, lambda c: {
        "x": x1[bids[c]], "wqT": wb["l1_wqT"], "wkT": wb["l1_wkT"], "wvT": wb["l1_wvT"],
        "hp1": hp1, "hp2": hp2, "hb1": hb1, "hb2": hb2,
        "cosA": consts["cosA"], "sinA": consts["sinA"], "identb": consts["identb"]})
    res1b = run_pair(build_l1b, lambda c: {
        "qrT": np.asarray(res1a[c]["qrT"]), "qhT": np.asarray(res1a[c]["qhT"]),
        "krT": np.asarray(res1a[c]["krT"]), "khT": np.asarray(res1a[c]["khT"]),
        "vtmi": np.asarray(res1a[c]["vtmo"]), "identb": consts["identb"],
        "tri9": consts["tri9"], "tri30": consts["tri30"], "revbc": consts["revbc"]})
    res1c = run_pair(build_l1c, lambda c: {
        "xo": gather_rows(x1[bids[c]], owns[c]),
        "attnT": np.asarray(res1b[c]["attn1T"]), "xnTo": np.asarray(res1a[c]["xnTo"]),
        "woT": wb["l1_woT"], "wbT": wb["l1_wbT"], "gwT": wb["l1_gwT"],
        "afT": wb["l1_afT"], "bfl": wb["l1_bfl"],
        "repc": consts["repc"], "identf": consts["identf"]})
    outf = np.zeros((B, S, D), np.float32)
    for c in cores:
        oc = np.asarray(res1c[c]["out"])
        for j, ti in enumerate(owns[c]):
            outf[bids[c], ti * 128:(ti + 1) * 128] = oc[j * 128:(j + 1) * 128]
    return outf



def build_fused(cfg):
    B, S, D, H0, DQ0 = cfg["B"], cfg["S"], cfg["D"], cfg["H0"], cfg["DQ0"]
    H1, KVH1, GRP, NE, RK, R0, KREM, LS0 = (cfg["H1"], cfg["KVH1"], cfg["GRP"],
                                            cfg["NE"], cfg["RK"], cfg["R0"],
                                            cfg["KREM"], cfg["LS0"])
    NT, CK = S // 128, D // 128
    E0 = H0 * 128
    CKE = E0 // 128
    ER = NE * RK
    NKC = (ER + 127) // 128
    ERp = NKC * 128
    scale_q = float(DQ0) ** -0.5
    scale_s = float(128.0) ** -0.5
    span = 2 * (128 * S + S)
    nsteps = int(math.ceil(math.log2(span)))
    lo_init = -float(128 * S + S)

    WSHAPES = dict(l0_wq=(D, E0), l0_wk=(D, E0), l0_wv=(D, E0), l0_wo=(E0, D),
                   l0_wbT=(D, D), l0_a0T=(D, R0), l0_wlT=(D, R0), l0_b0T=(R0, D),
                   l0_wrT=(R0, D), l1_wqT=(D, H1 * 128), l1_wkT=(D, KVH1 * 128),
                   l1_wvT=(D, KVH1 * 128), l1_woT=(D, D), l1_wbT=(D, D),
                   l1_gwT=(D, NE), l1_afT=(D, ER), l1_bfl=(ERp, D))

    nc = bass.Bass("TRN2", target_bir_lowering=False)
    x = nc.dram_tensor("x", [S, D], F32, kind="ExternalInput").ap()
    wfin = {k: nc.dram_tensor("wf_" + k, list(sh), F32, kind="ExternalInput").ap()
            for k, sh in WSHAPES.items()}
    wb = {k: nc.dram_tensor("wb_" + k, list(sh), BF16, kind="Internal").ap()
          for k, sh in WSHAPES.items()}
    hp1 = nc.dram_tensor("hp1", [H1 * 128, 128], F32, kind="ExternalInput").ap()
    hp2 = nc.dram_tensor("hp2", [H1 * 128, 128], F32, kind="ExternalInput").ap()
    hb1 = nc.dram_tensor("hb1", [128, H1], F32, kind="ExternalInput").ap()
    hb2 = nc.dram_tensor("hb2", [128, H1], F32, kind="ExternalInput").ap()
    cosA = nc.dram_tensor("cosA", [128, S], F32, kind="ExternalInput").ap()
    sinA = nc.dram_tensor("sinA", [128, S], F32, kind="ExternalInput").ap()
    identb = nc.dram_tensor("identb", [128, 128], BF16, kind="ExternalInput").ap()
    identf = nc.dram_tensor("identf", [128, 128], F32, kind="ExternalInput").ap()
    tri30h = nc.dram_tensor("tri30", [128, 128], F32, kind="ExternalInput").ap()
    tri9h = nc.dram_tensor("tri9", [128, 128], F32, kind="ExternalInput").ap()
    revbc = nc.dram_tensor("revbc", [128, S], F32, kind="ExternalInput").ap()
    repch = nc.dram_tensor("repc", [NE, ER], F32, kind="ExternalInput").ap()
    bbh = nc.dram_tensor("bb", [128, D], F32, kind="ExternalInput").ap()
    out = nc.dram_tensor("out", [S, D], F32, kind="ExternalOutput").ap()
    xnTo = nc.dram_tensor("xnTo_s", [D, S], BF16, kind="Internal").ap()
    attnT = nc.dram_tensor("attnT_s", [E0, S], BF16, kind="Internal").ap()
    scrA = nc.dram_tensor("scrA", [S, D], F32, kind="Internal").ap()
    x1 = nc.dram_tensor("x1_s", [S, D], F32, kind="Internal").ap()
    xn1To = nc.dram_tensor("xn1To_s", [D, S], BF16, kind="Internal").ap()
    qrT = nc.dram_tensor("qrT_s", [H1 * 128, S], BF16, kind="Internal").ap()
    qhT = nc.dram_tensor("qhT_s", [H1 * 128, S], BF16, kind="Internal").ap()
    krT = nc.dram_tensor("krT_s", [KVH1 * 128, S], BF16, kind="Internal").ap()
    khT = nc.dram_tensor("khT_s", [H1 * 128, S], BF16, kind="Internal").ap()
    vtmo = nc.dram_tensor("vtm_s", [KVH1 * 128, S], BF16, kind="Internal").ap()
    attn1T = nc.dram_tensor("attn1T_s", [H1 * 128, S], BF16, kind="Internal").ap()
    scrC = nc.dram_tensor("scrC", [S, D], F32, kind="Internal").ap()

    with contextlib.ExitStack() as octx:
        osb = lambda n, sh, dt: octx.enter_context(nc.sbuf_tensor(n, sh, dt))
        ops_ = lambda n, sh, dt: octx.enter_context(nc.psum_tensor(n, sh, dt))
        idb = osb("idb", [128, 128], BF16)
        idf = osb("idf", [128, 128], F32)
        p0 = ops_("p0", [128, 512], F32)
        pT = ops_("pTr", [128, 128], BF16)
        pA = ops_("pA", [128, 128], F32)
        pf = ops_("pf", [128, 128], F32)
        em = Em(nc)
        em.dma(idb[:, :], identb)
        em.dma(idf[:, :], identf)

        def _stage1():
            F = 1024
            with contextlib.ExitStack() as sctx:
                cs = lambda n, sh, dt: sctx.enter_context(nc.sbuf_tensor(n, sh, dt))
                cxf = cs("cxf", [128, F], F32)
                cxb = cs("cxb", [128, F], BF16)
                for k, sh in WSHAPES.items():
                    Z = sh[0] * sh[1]
                    assert Z % F == 0, (k, sh)
                    nr = Z // F
                    src = wfin[k].rearrange("a b -> (a b)").rearrange("(n f) -> n f", f=F)
                    dst = wb[k].rearrange("a b -> (a b)").rearrange("(n f) -> n f", f=F)
                    for r0 in range(0, nr, 128):
                        r1 = min(nr, r0 + 128)
                        em.dma(cxf[0:r1 - r0, :], src[r0:r1, :])
                        em.emit("scalar", lambda e, _r=r1 - r0: e.copy(cxb[0:_r, :], cxf[0:_r, :]))
                        em.dma(dst[r0:r1, :], cxb[0:r1 - r0, :])

        _stage1()
        def _stage2():
            with contextlib.ExitStack() as sctx:
                cs = lambda n, sh, dt: sctx.enter_context(nc.sbuf_tensor(n, sh, dt))
                xnT = cs("xnT", [128, CK * S], BF16)
                xs = cs("xs", [128, D], F32)
                sqh = cs("sqh", [128, D // 2], F32)
                xnb = cs("xnb", [128, D], BF16)
                wsl = cs("wsl", [128, CK * 128], BF16)
                qT = cs("qT", [128, S], BF16)
                kT = cs("kT", [128, S], BF16)
                vT = cs("vT", [128, S], BF16)
                vtm = cs("vtm", [128, S], BF16)
                tri = cs("tri", [128, 128], F32)
                red = cs("red", [128, 1], F32)
                red2 = cs("red2", [128, 1], F32)
                r2b = cs("r2b", [128, 1], F32)
                epsb = cs("epsb", [128, 1], F32)
                nmax = cs("nmax", [128, 1], F32)
                ssum = cs("ssum", [128, 1], F32)
                astg = cs("astg", [128, 128], BF16)
                sco = xs[:, 0:S]
                prb = xnb[:, 0:S]
                pts = wsl[:, 0:128]
                em.dma(tri[:, :], tri30h)
                st = dict(xs=xs, sqh=sqh, red=red, red2=red2, r2b=r2b, xnb=xnb, pT=pT, epsb=epsb)
                allt = list(range(NT))
                ln_transpose(em, st, x, NT, D, S, xnT, idb, allt, xnTo)

                def load_wslice(w_hbm, h):
                    src = w_hbm[:, h * 128:(h + 1) * 128].rearrange("(ck p) c -> p ck c", p=128)
                    dst = wsl[:, :].rearrange("p (ck c) -> p ck c", c=128)
                    dma_chunked(em, dst, src, CK)

                def proj(out_sb, scale):
                    for (w0, w1) in wins(S):
                        for ck in range(CK):
                            em.emit("tensor", lambda e, _ck=ck, _w0=w0, _w1=w1: e.matmul(
                                p0[:, 0:_w1 - _w0], wsl[:, _ck * 128:(_ck + 1) * 128],
                                xnT[:, _ck * S + _w0: _ck * S + _w1],
                                start=(_ck == 0), stop=(_ck == CK - 1)), wait=(ck == 0))
                        if scale == 1.0:
                            em.emit("vector", lambda e, _w0=w0, _w1=w1: e.tensor_copy(out_sb[:, _w0:_w1], p0[:, 0:_w1 - _w0]))
                        else:
                            em.emit("scalar", lambda e, _w0=w0, _w1=w1: e.mul(out_sb[:, _w0:_w1], p0[:, 0:_w1 - _w0], scale))

                for h in range(H0):
                    load_wslice(wb["l0_wq"], h)
                    proj(qT, scale_q)
                    load_wslice(wb["l0_wk"], h)
                    proj(kT, 1.0)
                    load_wslice(wb["l0_wv"], h)
                    proj(vT, 1.0)
                    for ti in range(NT):
                        em.emit("tensor", lambda e, _ti=ti: e.transpose(pT[:, :], vT[:, _ti * 128:(_ti + 1) * 128], idb[:, :]))
                        em.emit("vector", lambda e, _ti=ti: e.tensor_copy(vtm[:, _ti * 128:(_ti + 1) * 128], pT[:, :]))
                    for qi in range(NT):
                        W = (qi + 1) * 128
                        for (w0, w1) in wins(W):
                            em.emit("tensor", lambda e, _qi=qi, _w0=w0, _w1=w1: e.matmul(
                                p0[:, 0:_w1 - _w0], qT[:, _qi * 128:(_qi + 1) * 128],
                                kT[:, _w0:_w1], start=True, stop=True))
                            em.emit("vector", lambda e, _w0=w0, _w1=w1: e.tensor_copy(sco[:, _w0:_w1], p0[:, 0:_w1 - _w0]))
                        em.emit("vector", lambda e, _qi=qi: e.tensor_add(
                            sco[:, _qi * 128:(_qi + 1) * 128], sco[:, _qi * 128:(_qi + 1) * 128], tri[:, :]))
                        em.emit("vector", lambda e, _W=W: e.tensor_reduce(nmax[:, :], sco[:, 0:_W], AX.X, ALU.max, negate=True))
                        em.emit("scalar", lambda e, _W=W: e.activation(sco[:, 0:_W], sco[:, 0:_W], AF.Exp, bias=nmax[:, :]))
                        em.emit("vector", lambda e, _W=W: e.tensor_reduce(ssum[:, :], sco[:, 0:_W], AX.X, ALU.add))
                        em.emit("vector", lambda e: e.reciprocal(ssum[:, :], ssum[:, :]))
                        em.emit("vector", lambda e, _W=W: e.tensor_scalar(prb[:, 0:_W], sco[:, 0:_W], ssum[:, :], None, op0=ALU.mult))
                        for ti in range(qi + 1):
                            em.emit("tensor", lambda e, _ti=ti: e.transpose(pT[:, :], prb[:, _ti * 128:(_ti + 1) * 128], idb[:, :]))
                            em.emit("vector", lambda e: e.tensor_copy(pts, pT[:, :]))
                            em.emit("tensor", lambda e, _ti=ti, _qi=qi: e.matmul(
                                pA[:, :], vtm[:, _ti * 128:(_ti + 1) * 128], pts,
                                start=(_ti == 0), stop=(_ti == _qi)))
                        em.emit("vector", lambda e: e.tensor_copy(astg[:, :], pA[:, :]))
                        em.dma(attnT[h * 128:(h + 1) * 128, qi * 128:(qi + 1) * 128], astg[:, :])

        _stage2()
        def _stage3():
            with contextlib.ExitStack() as sctx:
                cs = lambda n, sh, dt: sctx.enter_context(nc.sbuf_tensor(n, sh, dt))
                act = cs("act", [128, max(CK, CKE) * S], BF16)
                wstr = cs("wstr", [128, CK * 512], BF16)
                aTl = cs("aTl", [128, CK * R0], BF16)
                wTl = cs("wTl", [128, CK * R0], BF16)
                b0s = cs("b0s", [R0, D], BF16)
                wrs = cs("wrs", [R0, D], BF16)
                aT = cs("aT", [R0, S], BF16)
                bbw = cs("bbw", [128, 512], F32)
                xres = cs("xres", [128, 512], F32)
                wop = cs("wop", [128, 512], F32)
                obuf = cs("obuf", [128, 512], F32)
                actv = act[:, :].rearrange("p (ck c) -> p ck c", c=S)
                dma_chunked(em, actv, attnT.rearrange("(ck p) c -> p ck c", p=128), CKE)
                for (w0, w1) in wins(D):
                    n = w1 - w0
                    dma_chunked(em, wstr[:, 0:CKE * n].rearrange("p (ck c) -> p ck c", c=n),
                                wb["l0_wo"][:, w0:w1].rearrange("(ck p) c -> p ck c", p=128), CKE)
                    for j in range(NT):
                        for ck in range(CKE):
                            em.emit("tensor", lambda e, _ck=ck, _j=j, _n=n: e.matmul(
                                p0[:, 0:_n], act[:, _ck * S + _j * 128: _ck * S + (_j + 1) * 128],
                                wstr[:, _ck * _n:(_ck + 1) * _n],
                                start=(_ck == 0), stop=(_ck == CKE - 1)), wait=(ck == 0))
                        em.emit("vector", lambda e, _n=n: e.tensor_copy(obuf[:, 0:_n], p0[:, 0:_n]))
                        em.dma(scrA[j * 128:(j + 1) * 128, w0:w1], obuf[:, 0:n])
                dma_chunked(em, actv, xnTo.rearrange("(ck p) c -> p ck c", p=128), CK)
                em.dma(aTl[:, :].rearrange("p (ck c) -> p ck c", c=R0), wb["l0_a0T"].rearrange("(ck p) c -> p ck c", p=128))
                em.dma(wTl[:, :].rearrange("p (ck c) -> p ck c", c=R0), wb["l0_wlT"].rearrange("(ck p) c -> p ck c", p=128))
                em.dma(b0s[:, :], wb["l0_b0T"])
                em.dma(wrs[:, :], wb["l0_wrT"])
                for (w0, w1) in wins(S):
                    for ck in range(CK):
                        em.emit("tensor", lambda e, _ck=ck, _w0=w0, _w1=w1: e.matmul(
                            p0[0:R0, 0:_w1 - _w0], aTl[:, _ck * R0:(_ck + 1) * R0],
                            act[:, _ck * S + _w0: _ck * S + _w1],
                            start=(_ck == 0), stop=False), wait=(ck == 0))
                    for ck in range(CK):
                        em.emit("tensor", lambda e, _ck=ck, _w0=w0, _w1=w1: e.matmul(
                            p0[0:R0, 0:_w1 - _w0], wTl[:, _ck * R0:(_ck + 1) * R0],
                            act[:, _ck * S + _w0: _ck * S + _w1],
                            start=False, stop=(_ck == CK - 1)), wait=False)
                    em.emit("scalar", lambda e, _w0=w0, _w1=w1: e.mul(aT[:, _w0:_w1], p0[0:R0, 0:_w1 - _w0], float(LS0)))
                for (w0, w1) in wins(D):
                    n = w1 - w0
                    dma_chunked(em, wstr[:, 0:CK * n].rearrange("p (ck c) -> p ck c", c=n),
                                wb["l0_wbT"][:, w0:w1].rearrange("(ck p) c -> p ck c", p=128), CK)
                    em.dma(bbw[:, 0:n], bbh[:, w0:w1])
                    for j in range(NT):
                        for ck in range(CK):
                            em.emit("tensor", lambda e, _ck=ck, _j=j, _n=n: e.matmul(
                                p0[:, 0:_n], act[:, _ck * S + _j * 128: _ck * S + (_j + 1) * 128],
                                wstr[:, _ck * _n:(_ck + 1) * _n],
                                start=(_ck == 0), stop=False), wait=(ck == 0))
                        em.emit("tensor", lambda e, _j=j, _w0=w0, _w1=w1: e.matmul(
                            p0[:, 0:_w1 - _w0], aT[:, _j * 128:(_j + 1) * 128], b0s[:, _w0:_w1],
                            start=False, stop=False), wait=False)
                        em.emit("tensor", lambda e, _j=j, _w0=w0, _w1=w1: e.matmul(
                            p0[:, 0:_w1 - _w0], aT[:, _j * 128:(_j + 1) * 128], wrs[:, _w0:_w1],
                            start=False, stop=True), wait=False)
                        em.dma(xres[:, 0:n], x[j * 128:(j + 1) * 128, w0:w1])
                        em.dma(wop[:, 0:n], scrA[j * 128:(j + 1) * 128, w0:w1])
                        em.emit("vector", lambda e, _n=n: e.tensor_add(obuf[:, 0:_n], p0[:, 0:_n], bbw[:, 0:_n]))
                        em.emit("vector", lambda e, _n=n: e.tensor_add(obuf[:, 0:_n], obuf[:, 0:_n], xres[:, 0:_n]))
                        em.emit("vector", lambda e, _n=n: e.tensor_add(obuf[:, 0:_n], obuf[:, 0:_n], wop[:, 0:_n]))
                        em.dma(x1[j * 128:(j + 1) * 128, w0:w1], obuf[:, 0:n])

        _stage3()
        def _stage4():
            with contextlib.ExitStack() as sctx:
                cs = lambda n, sh, dt: sctx.enter_context(nc.sbuf_tensor(n, sh, dt))
                xnT = cs("xnT3", [128, CK * S], BF16)
                xs = cs("xs3", [128, D], F32)
                sqh = cs("sqh3", [128, max(D // 2, S)], F32)
                xnb = cs("xnb3", [128, D], BF16)
                wsl = cs("wsl3", [128, CK * 128], BF16)
                kf = cs("kf3", [128, S], F32)
                krf = cs("krf3", [128, S], F32)
                vT = cs("vT3", [128, S], BF16)
                vstg = cs("vstg3", [128, 128], BF16)
                coss = cs("coss3", [128, S], F32)
                sins = cs("sins3", [128, S], F32)
                hsb = cs("hsb3", [128, 128], F32)
                hsb2 = cs("hsb23", [128, 128], F32)
                hb1s = cs("hb1s3", [128, H1], F32)
                hb2s = cs("hb2s3", [128, H1], F32)
                red = cs("red3", [128, 1], F32)
                red2 = cs("red23", [128, 1], F32)
                r2b = cs("r2b3", [128, 1], F32)
                epsb = cs("epsb3", [128, 1], F32)
                rot = sqh[:, 0:S]
                if D >= 2 * S:
                    yb = xs[:, 0:S]
                    zb = xs[:, S:2 * S]
                    qf = kf[:, 0:S]
                    qrf = xs[:, S:2 * S]
                    krb = xnb[:, 0:S]
                    outb = xnb[:, S:2 * S]
                else:
                    yb = cs("ybb3", [128, S], F32)
                    zb = cs("zbb3", [128, S], F32)
                    qf = kf[:, 0:S]
                    qrf = cs("qrf3", [128, S], F32)
                    krb = xnb[:, 0:S]
                    outb = cs("outbb3", [128, S], BF16)
                em.dma(coss[:, :], cosA)
                em.dma(sins[:, :], sinA)
                em.dma(hb1s[:, :], hb1)
                em.dma(hb2s[:, :], hb2)
                st = dict(xs=xs, sqh=sqh[:, 0:D // 2], red=red, red2=red2, r2b=r2b, xnb=xnb, pT=pT, epsb=epsb)
                ln_transpose(em, st, x1, NT, D, S, xnT, idb, list(range(NT)), xn1To)

                def load_wslice3(w_hbm, h):
                    src = w_hbm[:, h * 128:(h + 1) * 128].rearrange("(ck p) c -> p ck c", p=128)
                    dst = wsl[:, :].rearrange("p (ck c) -> p ck c", c=128)
                    dma_chunked(em, dst, src, CK)

                def proj_full3(out_sb):
                    for (w0, w1) in wins(S):
                        for ck in range(CK):
                            em.emit("tensor", lambda e, _ck=ck, _w0=w0, _w1=w1: e.matmul(
                                p0[:, 0:_w1 - _w0], wsl[:, _ck * 128:(_ck + 1) * 128],
                                xnT[:, _ck * S + _w0: _ck * S + _w1],
                                start=(_ck == 0), stop=(_ck == CK - 1)), wait=(ck == 0))
                        em.emit("vector", lambda e, _w0=w0, _w1=w1: e.tensor_copy(out_sb[:, _w0:_w1], p0[:, 0:_w1 - _w0]))

                def rope3(dst, src):
                    em.emit("scalar", lambda e: e.mul(rot[0:64, 0:S], src[64:128, 0:S], -1.0))
                    em.emit("scalar", lambda e: e.copy(rot[64:128, 0:S], src[0:64, 0:S]))
                    em.emit("vector", lambda e: e.tensor_mul(dst[:, 0:S], src[:, 0:S], coss[:, 0:S]))
                    em.emit("vector", lambda e: e.tensor_mul(rot[:, 0:S], rot[:, 0:S], sins[:, 0:S]))
                    em.emit("vector", lambda e: e.tensor_add(dst[:, 0:S], dst[:, 0:S], rot[:, 0:S]))

                def hash_sign3(src_f, h, out_bf):
                    em.dma(hsb[:, :], hp1[h * 128:(h + 1) * 128, :])
                    em.dma(hsb2[:, :], hp2[h * 128:(h + 1) * 128, :])
                    for (w0, w1) in wins(S):
                        em.emit("tensor", lambda e, _w0=w0, _w1=w1: e.matmul(
                            p0[:, 0:_w1 - _w0], hsb[:, :], src_f[:, _w0:_w1], start=True, stop=True))
                        em.emit("scalar", lambda e, _w0=w0, _w1=w1, _h=h: e.activation(
                            yb[:, _w0:_w1], p0[:, 0:_w1 - _w0], AF.Silu, bias=hb1s[:, _h:_h + 1]))
                    em.emit("vector", lambda e: e.tensor_add(yb[:, 0:S], yb[:, 0:S], src_f[:, 0:S]))
                    for (w0, w1) in wins(S):
                        em.emit("tensor", lambda e, _w0=w0, _w1=w1: e.matmul(
                            p0[:, 0:_w1 - _w0], hsb2[:, :], yb[:, _w0:_w1], start=True, stop=True))
                        em.emit("vector", lambda e, _w0=w0, _w1=w1, _h=h: e.scalar_tensor_tensor(
                            zb[:, _w0:_w1], p0[:, 0:_w1 - _w0], hb2s[:, _h:_h + 1],
                            yb[:, _w0:_w1], op0=ALU.add, op1=ALU.add))
                    em.emit("scalar", lambda e: e.activation(out_bf[:, 0:S], zb[:, 0:S], AF.Sign))

                for kv in range(KVH1):
                    load_wslice3(wb["l1_wkT"], kv)
                    proj_full3(kf)
                    rope3(krf, kf)
                    em.emit("vector", lambda e: e.tensor_copy(krb, krf[:, :]))
                    em.dma(krT[kv * 128:(kv + 1) * 128, :], krb)
                    load_wslice3(wb["l1_wvT"], kv)
                    proj_full3(vT)
                    for ti in range(NT):
                        em.emit("tensor", lambda e, _ti=ti: e.transpose(pT[:, :], vT[:, _ti * 128:(_ti + 1) * 128], idb[:, :]))
                        em.emit("vector", lambda e: e.tensor_copy(vstg[:, :], pT[:, :]))
                        em.dma(vtmo[kv * 128:(kv + 1) * 128, ti * 128:(ti + 1) * 128], vstg[:, :])
                    for g in range(GRP):
                        h = kv * GRP + g
                        hash_sign3(krf, h, outb)
                        em.dma(khT[h * 128:(h + 1) * 128, :], outb[:, 0:S])
                        load_wslice3(wb["l1_wqT"], h)
                        proj_full3(qf)
                        rope3(qrf, qf)
                        em.emit("scalar", lambda e: e.mul(outb[:, 0:S], qrf[:, 0:S], scale_s))
                        em.dma(qrT[h * 128:(h + 1) * 128, :], outb[:, 0:S])
                        hash_sign3(qrf, h, outb)
                        em.dma(qhT[h * 128:(h + 1) * 128, :], outb[:, 0:S])

        _stage4()
        def _stage5():
            with contextlib.ExitStack() as sctx:
                cs = lambda n, sh, dt: sctx.enter_context(nc.sbuf_tensor(n, sh, dt))
                qr = cs("qr4", [128, S], BF16)
                qh = cs("qh4", [128, S], BF16)
                kr = cs("kr4", [128, S], BF16)
                kh = cs("kh4", [128, S], BF16)
                vtm = cs("vtm4", [128, S], BF16)
                ub = cs("ub4", [128, S], F32)
                sgn = cs("sgn4", [128, S], F32)
                sco = cs("sco4", [128, S], F32)
                sel = cs("sel4", [128, S], mybir.dt.uint32)
                rev = cs("rev4", [128, S], F32)
                negb = cs("negb4", [128, S], F32)
                prb = cs("prb4", [128, S], BF16)
                pts = cs("pts4", [128, 128], BF16)
                astg = cs("astg4", [128, 128], BF16)
                t9 = cs("t94", [128, 128], F32)
                t30 = cs("t304", [128, 128], F32)
                nmb = cs("nmb4", [128, 1], F32)
                cnt = cs("cnt4", [128, 1], F32)
                tmp1 = cs("tmp14", [128, 1], F32)
                kap = cs("kap4", [128, 1], F32)
                nmax = cs("nmax4", [128, 1], F32)
                ssum = cs("ssum4", [128, 1], F32)
                em.dma(t9[:, :], tri9h)
                em.dma(t30[:, :], tri30h)
                em.dma(rev[:, :], revbc)
                em.emit("vector", lambda e: e.memset(negb[:, :], NEG30))
                for h in range(H1):
                    kv = h // GRP
                    em.dma(qr[:, :], qrT[h * 128:(h + 1) * 128, :])
                    em.dma(qh[:, :], qhT[h * 128:(h + 1) * 128, :])
                    em.dma(kr[:, :], krT[kv * 128:(kv + 1) * 128, :])
                    em.dma(kh[:, :], khT[h * 128:(h + 1) * 128, :])
                    em.dma(vtm[:, :], vtmo[kv * 128:(kv + 1) * 128, :])
                    for qi in range(NT):
                        W = (qi + 1) * 128
                        for (w0, w1) in wins(W):
                            em.emit("tensor", lambda e, _qi=qi, _w0=w0, _w1=w1: e.matmul(
                                p0[:, 0:_w1 - _w0], qh[:, _qi * 128:(_qi + 1) * 128],
                                kh[:, _w0:_w1], start=True, stop=True))
                            em.emit("vector", lambda e, _w0=w0, _w1=w1: e.tensor_copy(ub[:, _w0:_w1], p0[:, 0:_w1 - _w0]))
                        em.emit("vector", lambda e, _qi=qi: e.tensor_add(
                            ub[:, _qi * 128:(_qi + 1) * 128], ub[:, _qi * 128:(_qi + 1) * 128], t9[:, :]))
                        em.emit("vector", lambda e, _W=W: e.tensor_scalar(ub[:, 0:_W], ub[:, 0:_W], float(S), None, op0=ALU.mult))
                        em.emit("vector", lambda e, _W=W: e.tensor_add(ub[:, 0:_W], ub[:, 0:_W], rev[:, 0:_W]))
                        step0 = float(2 ** (nsteps - 1))
                        em.emit("vector", lambda e, _s=step0: e.memset(nmb[:, :], -lo_init - _s + 0.5))
                        thr = float(2 * KREM - W)
                        for it in range(nsteps):
                            step = float(2 ** (nsteps - 1 - it))
                            nstep = float(2 ** (nsteps - 2 - it)) if it < nsteps - 1 else 0.5
                            em.emit("scalar", lambda e, _W=W: e.activation(sgn[:, 0:_W], ub[:, 0:_W], AF.Sign, bias=nmb[:, :]))
                            em.emit("vector", lambda e, _W=W: e.tensor_reduce(cnt[:, :], sgn[:, 0:_W], AX.X, ALU.add))
                            em.emit("vector", lambda e, _t=thr, _s=step: e.tensor_scalar(
                                tmp1[:, :], cnt[:, :], _t, -_s, op0=ALU.is_ge, op1=ALU.mult))
                            em.emit("vector", lambda e, _ns=nstep: e.scalar_tensor_tensor(
                                nmb[:, :], tmp1[:, :], _ns, nmb[:, :], op0=ALU.add, op1=ALU.add))
                        em.emit("vector", lambda e: e.tensor_scalar(kap[:, :], nmb[:, :], -1.0, None, op0=ALU.mult))
                        em.emit("vector", lambda e, _W=W: e.tensor_scalar(sel[:, 0:_W], ub[:, 0:_W], kap[:, :], None, op0=ALU.is_lt))
                        for (w0, w1) in wins(W):
                            em.emit("tensor", lambda e, _qi=qi, _w0=w0, _w1=w1: e.matmul(
                                p0[:, 0:_w1 - _w0], qr[:, _qi * 128:(_qi + 1) * 128],
                                kr[:, _w0:_w1], start=True, stop=True))
                            em.emit("vector", lambda e, _w0=w0, _w1=w1: e.tensor_copy(sco[:, _w0:_w1], p0[:, 0:_w1 - _w0]))
                        em.emit("vector", lambda e, _qi=qi: e.tensor_add(
                            sco[:, _qi * 128:(_qi + 1) * 128], sco[:, _qi * 128:(_qi + 1) * 128], t30[:, :]))
                        em.emit("vector", lambda e, _W=W: e.copy_predicated(sco[:, 0:_W], sel[:, 0:_W], negb[:, 0:_W]))
                        em.emit("vector", lambda e, _W=W: e.tensor_reduce(nmax[:, :], sco[:, 0:_W], AX.X, ALU.max, negate=True))
                        em.emit("scalar", lambda e, _W=W: e.activation(sco[:, 0:_W], sco[:, 0:_W], AF.Exp, bias=nmax[:, :]))
                        em.emit("vector", lambda e, _W=W: e.tensor_reduce(ssum[:, :], sco[:, 0:_W], AX.X, ALU.add))
                        em.emit("vector", lambda e: e.reciprocal(ssum[:, :], ssum[:, :]))
                        em.emit("vector", lambda e, _W=W: e.tensor_scalar(prb[:, 0:_W], sco[:, 0:_W], ssum[:, :], None, op0=ALU.mult))
                        for ti in range(qi + 1):
                            em.emit("tensor", lambda e, _ti=ti: e.transpose(pT[:, :], prb[:, _ti * 128:(_ti + 1) * 128], idb[:, :]))
                            em.emit("vector", lambda e: e.tensor_copy(pts[:, :], pT[:, :]))
                            em.emit("tensor", lambda e, _ti=ti, _qi=qi: e.matmul(
                                pA[:, :], vtm[:, _ti * 128:(_ti + 1) * 128], pts[:, :],
                                start=(_ti == 0), stop=(_ti == _qi)))
                        em.emit("vector", lambda e: e.tensor_copy(astg[:, :], pA[:, :]))
                        em.dma(attn1T[h * 128:(h + 1) * 128, qi * 128:(qi + 1) * 128], astg[:, :])

        _stage5()
        def _stage6():
            with contextlib.ExitStack() as sctx:
                cs = lambda n, sh, dt: sctx.enter_context(nc.sbuf_tensor(n, sh, dt))
                act = cs("act5", [128, CK * S], BF16)
                wstr = cs("wstr5", [128, CK * 512], BF16)
                gws = cs("gws5", [128, CK * NE], BF16)
                afs = cs("afs5", [128, CK * ER], BF16)
                bstr = cs("bstr5", [128, NKC * 512], BF16)
                hT = cs("hT5", [128, NT * NKC * 128], BF16)
                reps = cs("reps5", [NE, ER], F32)
                lg = cs("lg5", [128, NE], F32)
                lg2 = cs("lg25", [128, NE], F32)
                eq1 = cs("eq15", [128, NE], F32)
                eq2 = cs("eq25", [128, NE], F32)
                gate = cs("gate5", [128, NE], F32)
                gtT = cs("gtT5", [NE, 128], F32)
                hb = cs("hb5", [128, ER], F32)
                grep = cs("grep5", [128, ER], F32)
                m1 = cs("m15", [128, 1], F32)
                m2 = cs("m25", [128, 1], F32)
                dm = cs("dm5", [128, 1], F32)
                p1s = cs("p1s5", [128, 1], F32)
                p2s = cs("p2s5", [128, 1], F32)
                xres = cs("xres5", [128, 512], F32)
                wop = cs("wop5", [128, 512], F32)
                obuf = cs("obuf5", [128, 512], F32)
                actv = act[:, :].rearrange("p (ck c) -> p ck c", c=S)
                dma_chunked(em, actv, attn1T.rearrange("(ck p) c -> p ck c", p=128), CK)
                for (w0, w1) in wins(D):
                    n = w1 - w0
                    dma_chunked(em, wstr[:, 0:CK * n].rearrange("p (ck c) -> p ck c", c=n),
                                wb["l1_woT"][:, w0:w1].rearrange("(ck p) c -> p ck c", p=128), CK)
                    for j in range(NT):
                        for ck in range(CK):
                            em.emit("tensor", lambda e, _ck=ck, _j=j, _n=n: e.matmul(
                                p0[:, 0:_n], act[:, _ck * S + _j * 128: _ck * S + (_j + 1) * 128],
                                wstr[:, _ck * _n:(_ck + 1) * _n],
                                start=(_ck == 0), stop=(_ck == CK - 1)), wait=(ck == 0))
                        em.emit("vector", lambda e, _n=n: e.tensor_copy(obuf[:, 0:_n], p0[:, 0:_n]))
                        em.dma(scrC[j * 128:(j + 1) * 128, w0:w1], obuf[:, 0:n])
                dma_chunked(em, actv, xn1To.rearrange("(ck p) c -> p ck c", p=128), CK)
                em.dma(gws[:, :].rearrange("p (ck c) -> p ck c", c=NE), wb["l1_gwT"].rearrange("(ck p) c -> p ck c", p=128))
                em.dma(afs[:, :].rearrange("p (ck c) -> p ck c", c=ER), wb["l1_afT"].rearrange("(ck p) c -> p ck c", p=128))
                em.dma(reps[:, :], repch)
                for j in range(NT):
                    for ck in range(CK):
                        em.emit("tensor", lambda e, _ck=ck, _j=j: e.matmul(
                            p0[:, 0:NE], act[:, _ck * S + _j * 128: _ck * S + (_j + 1) * 128],
                            gws[:, _ck * NE:(_ck + 1) * NE],
                            start=(_ck == 0), stop=(_ck == CK - 1)), wait=(ck == 0))
                    em.emit("vector", lambda e: e.tensor_copy(lg[:, :], p0[:, 0:NE]))
                    em.emit("vector", lambda e: e.tensor_reduce(m1[:, :], lg[:, :], AX.X, ALU.max))
                    em.emit("vector", lambda e: e.tensor_scalar(eq1[:, :], lg[:, :], m1[:, :], None, op0=ALU.is_ge))
                    em.emit("vector", lambda e: e.tensor_scalar(lg2[:, :], eq1[:, :], NEG30, None, op0=ALU.mult))
                    em.emit("vector", lambda e: e.tensor_add(lg2[:, :], lg2[:, :], lg[:, :]))
                    em.emit("vector", lambda e: e.tensor_reduce(m2[:, :], lg2[:, :], AX.X, ALU.max))
                    em.emit("vector", lambda e: e.tensor_scalar(eq2[:, :], lg2[:, :], m2[:, :], None, op0=ALU.is_ge))
                    em.emit("vector", lambda e: e.tensor_sub(dm[:, :], m1[:, :], m2[:, :]))
                    em.emit("scalar", lambda e: e.activation(p1s[:, :], dm[:, :], AF.Sigmoid))
                    em.emit("vector", lambda e: e.tensor_sub(dm[:, :], m2[:, :], m1[:, :]))
                    em.emit("scalar", lambda e: e.activation(p2s[:, :], dm[:, :], AF.Sigmoid))
                    em.emit("vector", lambda e: e.tensor_scalar(eq1[:, :], eq1[:, :], p1s[:, :], 2.0, op0=ALU.mult, op1=ALU.mult))
                    em.emit("vector", lambda e: e.tensor_scalar(eq2[:, :], eq2[:, :], p2s[:, :], 2.0, op0=ALU.mult, op1=ALU.mult))
                    em.emit("vector", lambda e: e.tensor_add(gate[:, :], eq1[:, :], eq2[:, :]))
                    for ck in range(CK):
                        em.emit("tensor", lambda e, _ck=ck, _j=j: e.matmul(
                            p0[:, 0:ER], act[:, _ck * S + _j * 128: _ck * S + (_j + 1) * 128],
                            afs[:, _ck * ER:(_ck + 1) * ER],
                            start=(_ck == 0), stop=(_ck == CK - 1)), wait=(ck == 0))
                    em.emit("vector", lambda e: e.tensor_copy(hb[:, :], p0[:, 0:ER]))
                    em.emit("tensor", lambda e: e.transpose(pf[0:NE, :], gate[:, 0:NE], idf[:, :]))
                    em.emit("vector", lambda e: e.tensor_copy(gtT[:, :], pf[0:NE, :]))
                    em.emit("tensor", lambda e: e.matmul(p0[:, 0:ER], gtT[:, :], reps[:, :], start=True, stop=True))
                    em.emit("vector", lambda e: e.tensor_tensor(grep[:, :], hb[:, :], p0[:, 0:ER], op=ALU.mult))
                    for kc in range(NKC):
                        c0 = kc * 128
                        c1 = min(ER, c0 + 128)
                        em.emit("tensor", lambda e, _c0=c0, _c1=c1: e.transpose(pf[0:_c1 - _c0, :], grep[:, _c0:_c1], idf[:, :]))
                        em.emit("vector", lambda e, _j=j, _kc=kc, _c0=c0, _c1=c1: e.tensor_copy(
                            hT[0:_c1 - _c0, (_j * NKC + _kc) * 128:(_j * NKC + _kc) * 128 + 128], pf[0:_c1 - _c0, :]))
                for (w0, w1) in wins(D):
                    n = w1 - w0
                    dma_chunked(em, wstr[:, 0:CK * n].rearrange("p (ck c) -> p ck c", c=n),
                                wb["l1_wbT"][:, w0:w1].rearrange("(ck p) c -> p ck c", p=128), CK)
                    em.dma(bstr[:, 0:NKC * n].rearrange("p (kc c) -> p kc c", c=n),
                           wb["l1_bfl"][:, w0:w1].rearrange("(kc p) c -> p kc c", p=128))
                    for j in range(NT):
                        for ck in range(CK):
                            em.emit("tensor", lambda e, _ck=ck, _j=j, _n=n: e.matmul(
                                p0[:, 0:_n], act[:, _ck * S + _j * 128: _ck * S + (_j + 1) * 128],
                                wstr[:, _ck * _n:(_ck + 1) * _n],
                                start=(_ck == 0), stop=False), wait=(ck == 0))
                        for kc in range(NKC):
                            c0 = kc * 128
                            c1 = min(ER, c0 + 128)
                            em.emit("tensor", lambda e, _j=j, _kc=kc, _c0=c0, _c1=c1, _n=n: e.matmul(
                                p0[:, 0:_n], hT[0:_c1 - _c0, (_j * NKC + _kc) * 128:(_j * NKC + _kc) * 128 + 128],
                                bstr[0:_c1 - _c0, _kc * _n:(_kc + 1) * _n],
                                start=False, stop=(_kc == NKC - 1)), wait=False)
                        em.dma(xres[:, 0:n], x1[j * 128:(j + 1) * 128, w0:w1])
                        em.dma(wop[:, 0:n], scrC[j * 128:(j + 1) * 128, w0:w1])
                        em.emit("vector", lambda e, _n=n: e.tensor_add(obuf[:, 0:_n], p0[:, 0:_n], xres[:, 0:_n]))
                        em.emit("vector", lambda e, _n=n: e.tensor_add(obuf[:, 0:_n], obuf[:, 0:_n], wop[:, 0:_n]))
                        em.dma(out[j * 128:(j + 1) * 128, w0:w1], obuf[:, 0:n])
        _stage6()
        em.build_block()
    return nc


def run_fused(cfg, ins, ncores, runner):
    B, S, D, H1 = cfg["B"], cfg["S"], cfg["D"], cfg["H1"]
    x0 = np.ascontiguousarray(np.asarray(ins["L0_attn_hidden_states"], np.float32).reshape(B, S, D))
    consts = make_consts(cfg)
    w = prep_weights(cfg, ins)
    bb_bc = np.ascontiguousarray(np.tile(np.asarray(ins["L0_ffn_b_base"], np.float32)[None, :], (128, 1)))
    hb1 = np.ascontiguousarray(np.asarray(ins["L1_attn_hash_bias1"], np.float32).reshape(H1, 128).T)
    hb2 = np.ascontiguousarray(np.asarray(ins["L1_attn_hash_bias2"], np.float32).reshape(H1, 128).T)
    hp1 = np.ascontiguousarray(np.asarray(ins["L1_attn_hash_proj1"], np.float32).reshape(H1 * 128, 128))
    hp2 = np.ascontiguousarray(np.asarray(ins["L1_attn_hash_proj2"], np.float32).reshape(H1 * 128, 128))
    nc = build_fused(cfg)
    base = {("wf_" + k): v for k, v in w.items()}
    base.update({"hp1": hp1, "hp2": hp2, "hb1": hb1, "hb2": hb2,
                 "cosA": consts["cosA"], "sinA": consts["sinA"],
                 "identb": consts["identb"], "identf": consts["identf"],
                 "tri30": consts["tri30"], "tri9": consts["tri9"],
                 "revbc": consts["revbc"], "repc": consts["repc"], "bb": bb_bc})
    in_maps = []
    for c in range(ncores):
        m = dict(base)
        m["x"] = x0[(c // 2) % B]
        in_maps.append(m)
    res = runner(nc, in_maps, list(range(ncores)))
    outf = np.zeros((B, S, D), np.float32)
    for b in range(B):
        outf[b] = np.asarray(res[2 * b]["out"]) if 2 * b < ncores else np.asarray(res[b]["out"])
    return outf


def kernel(**ins):
    return run_fused(full_cfg(), ins, ncores=8, runner=_hw_runner)


# version 04 (final)
# speedup vs baseline: 1.2278x; 1.2278x over previous
import math
import contextlib
import numpy as np
import ml_dtypes
import concourse.bass as bass
import concourse.mybir as mybir
from concourse.bass_utils import run_bass_kernel_spmd

F32 = mybir.dt.float32
BF16 = mybir.dt.bfloat16
AF = mybir.ActivationFunctionType
ALU = mybir.AluOpType
AX = mybir.AxisListType
BF = ml_dtypes.bfloat16

NEG30 = -1.0e30
NEG9 = -1.0e9


def full_cfg():
    return dict(B=4, S=2048, D=4096, H0=32, DQ0=192, H1=32, KVH1=8, GRP=4,
                NE=28, RK=8, R0=64, KREM=410, LS0=16.0 / 64.0)


def own_tiles(parity, NT):
    a, b = [], []
    for idx in range(NT // 2):
        lo, hi = idx, NT - 1 - idx
        (a if idx % 2 == 0 else b).extend([lo, hi])
    return sorted(a) if parity == 0 else sorted(b)


def wins(total, w=512):
    return [(s, min(total, s + w)) for s in range(0, total, w)]


class Em:
    def __init__(self, nc):
        self.nc = nc
        self.sem_ctx = nc.semaphore("g")
        self.sem = self.sem_ctx.__enter__()
        self.cum = 0
        self.ops = []

    def emit(self, eng, fn, inc=1, wait=True):
        self.ops.append((eng, fn, inc, self.cum if wait else None))
        self.cum += inc

    def dma(self, out, in_, wait=True, **kw):
        self.emit("sync", lambda e: e.dma_start(out, in_, **kw), inc=16, wait=wait)

    def build_block(self):
        nc = self.nc
        sem = self.sem
        with nc.Block() as block:
            by = {}
            for (eng, fn, inc, wt) in self.ops:
                by.setdefault(eng, []).append((fn, inc, wt))

            def mk(items):
                def sec(e):
                    for fn, inc, wt in items:
                        if wt is not None and wt > 0:
                            e.wait_ge(sem, wt)
                        fn(e).then_inc(sem, inc)
                return sec

            for eng in list(by):
                getattr(block, eng)(mk(by[eng]))
            final = self.cum

            @block.sync
            def _(sync):
                sync.wait_ge(sem, final)
        self.sem_ctx.__exit__(None, None, None)


def ln_transpose(em, st, x_hbm, NT, D, S, xnT, ident_bf, own, xnTo=None, eps=1e-5):
    CK = D // 128
    xs, sqh, red, red2, r2b, xnb, pT = (st["xs"], st["sqh"], st["red"], st["red2"],
                                        st["r2b"], st["xnb"], st["pT"])
    epsb = st["epsb"]
    H = D // 2
    em.emit("vector", lambda e: e.memset(epsb[:, :], float(eps)))
    for ti in range(NT):
        em.dma(xs[:, :], x_hbm[ti * 128:(ti + 1) * 128, :])
        em.emit("vector", lambda e: e.tensor_reduce(red[:, :], xs[:, :], AX.X, ALU.add))
        em.emit("vector", lambda e: e.tensor_scalar(red[:, :], red[:, :], 1.0 / D, None, op0=ALU.mult))
        em.emit("vector", lambda e: e.tensor_scalar(xs[:, :], xs[:, :], red[:, :], None, op0=ALU.subtract))
        em.emit("vector", lambda e: e.tensor_mul(sqh[:, :], xs[:, 0:H], xs[:, 0:H]))
        em.emit("vector", lambda e: e.tensor_reduce(red2[:, :], sqh[:, :], AX.X, ALU.add))
        em.emit("vector", lambda e: e.tensor_mul(sqh[:, :], xs[:, H:D], xs[:, H:D]))
        em.emit("vector", lambda e: e.tensor_reduce(r2b[:, :], sqh[:, :], AX.X, ALU.add))
        em.emit("vector", lambda e: e.tensor_add(red2[:, :], red2[:, :], r2b[:, :]))
        em.emit("vector", lambda e: e.tensor_scalar(red2[:, :], red2[:, :], 1.0 / D, None, op0=ALU.mult))
        em.emit("scalar", lambda e: e.activation(red2[:, :], red2[:, :], AF.Sqrt, bias=epsb[:, :]))
        em.emit("vector", lambda e: e.reciprocal(red2[:, :], red2[:, :]))
        em.emit("vector", lambda e: e.tensor_scalar(xnb[:, :], xs[:, :], red2[:, :], None, op0=ALU.mult))
        for ck in range(CK):
            em.emit("tensor", lambda e, _ck=ck: e.transpose(pT[:, :], xnb[:, _ck * 128:(_ck + 1) * 128], ident_bf[:, :]))
            em.emit("vector", lambda e, _ck=ck, _ti=ti: e.tensor_copy(
                xnT[:, _ck * S + _ti * 128: _ck * S + (_ti + 1) * 128], pT[:, :]))
    if xnTo is not None:
        So = len(own) * 128
        for ck in range(CK):
            for j, ti in enumerate(own):
                em.dma(xnTo[ck * 128:(ck + 1) * 128, j * 128:(j + 1) * 128],
                       xnT[:, ck * S + ti * 128: ck * S + (ti + 1) * 128])


def dma_chunked(em, dst_view, src_view, nck, grp=8):
    for c0 in range(0, nck, grp):
        c1 = min(c0 + grp, nck)
        em.dma(dst_view[:, c0:c1, :], src_view[:, c0:c1, :], wait=(c0 == 0))


def build_l0a(cfg, parity):
    S, D, H0, DQ0 = cfg["S"], cfg["D"], cfg["H0"], cfg["DQ0"]
    NT, CK = S // 128, D // 128
    own = own_tiles(parity, NT)
    So = len(own) * 128
    scale_q = float(DQ0) ** -0.5

    nc = bass.Bass("TRN2", target_bir_lowering=False)
    x = nc.dram_tensor("x", [S, D], F32, kind="ExternalInput").ap()
    wq = nc.dram_tensor("wq", [D, H0 * 128], BF16, kind="ExternalInput").ap()
    wk = nc.dram_tensor("wk", [D, H0 * 128], BF16, kind="ExternalInput").ap()
    wv = nc.dram_tensor("wv", [D, H0 * 128], BF16, kind="ExternalInput").ap()
    identb = nc.dram_tensor("identb", [128, 128], BF16, kind="ExternalInput").ap()
    tri30 = nc.dram_tensor("tri30", [128, 128], F32, kind="ExternalInput").ap()
    attnT = nc.dram_tensor("attnT", [H0 * 128, So], BF16, kind="ExternalOutput").ap()
    xnTo = nc.dram_tensor("xnTo", [D, So], BF16, kind="ExternalOutput").ap()

    with contextlib.ExitStack() as ctx:
        sb = lambda n, sh, dt: ctx.enter_context(nc.sbuf_tensor(n, sh, dt))
        ps = lambda n, sh, dt: ctx.enter_context(nc.psum_tensor(n, sh, dt))
        xnT = sb("xnT", [128, CK * S], BF16)
        xs = sb("xs", [128, D], F32)
        sqh = sb("sqh", [128, D // 2], F32)
        xnb = sb("xnb", [128, D], BF16)
        wsl = sb("wsl", [128, CK * 128], BF16)
        qT = sb("qT", [128, S], BF16)
        kT = sb("kT", [128, S], BF16)
        vT = sb("vT", [128, S], BF16)
        vtm = sb("vtm", [128, S], BF16)
        idb = sb("idb", [128, 128], BF16)
        tri = sb("tri", [128, 128], F32)
        red = sb("red", [128, 1], F32)
        red2 = sb("red2", [128, 1], F32)
        r2b = sb("r2b", [128, 1], F32)
        epsb = sb("epsb", [128, 1], F32)
        nmax = sb("nmax", [128, 1], F32)
        ssum = sb("ssum", [128, 1], F32)
        p0 = ps("p0", [128, 512], F32)
        pT = ps("pTr", [128, 128], BF16)
        pA = ps("pA", [128, 128], F32)
        sco = xs[:, 0:min(S, 2048)] if S <= 2048 else None
        assert S <= 2048
        sco = xs[:, 0:S]
        prb = xnb[:, 0:S]
        pts = wsl[:, 0:128]

        em = Em(nc)
        em.dma(idb[:, :], identb)
        em.dma(tri[:, :], tri30)
        st = dict(xs=xs, sqh=sqh, red=red, red2=red2, r2b=r2b, xnb=xnb, pT=pT, epsb=epsb)
        ln_transpose(em, st, x, NT, D, S, xnT, idb, own, xnTo)

        def load_wslice(w_hbm, h):
            src = w_hbm[:, h * 128:(h + 1) * 128].rearrange("(ck p) c -> p ck c", p=128)
            dst = wsl[:, :].rearrange("p (ck c) -> p ck c", c=128)
            dma_chunked(em, dst, src, CK)

        def proj(out_sb, scale):
            for (w0, w1) in wins(S):
                for ck in range(CK):
                    em.emit("tensor", lambda e, _ck=ck, _w0=w0, _w1=w1: e.matmul(
                        p0[:, 0:_w1 - _w0], wsl[:, _ck * 128:(_ck + 1) * 128],
                        xnT[:, _ck * S + _w0: _ck * S + _w1],
                        start=(_ck == 0), stop=(_ck == CK - 1)), wait=(ck == 0))
                if scale == 1.0:
                    em.emit("vector", lambda e, _w0=w0, _w1=w1: e.tensor_copy(out_sb[:, _w0:_w1], p0[:, 0:_w1 - _w0]))
                else:
                    em.emit("scalar", lambda e, _w0=w0, _w1=w1: e.mul(out_sb[:, _w0:_w1], p0[:, 0:_w1 - _w0], scale))

        for h in range(H0):
            load_wslice(wq, h)
            proj(qT, scale_q)
            load_wslice(wk, h)
            proj(kT, 1.0)
            load_wslice(wv, h)
            proj(vT, 1.0)
            for ti in range(NT):
                em.emit("tensor", lambda e, _ti=ti: e.transpose(pT[:, :], vT[:, _ti * 128:(_ti + 1) * 128], idb[:, :]))
                em.emit("vector", lambda e, _ti=ti: e.tensor_copy(vtm[:, _ti * 128:(_ti + 1) * 128], pT[:, :]))
            for j, qi in enumerate(own):
                W = (qi + 1) * 128
                for (w0, w1) in wins(W):
                    em.emit("tensor", lambda e, _qi=qi, _w0=w0, _w1=w1: e.matmul(
                        p0[:, 0:_w1 - _w0], qT[:, _qi * 128:(_qi + 1) * 128],
                        kT[:, _w0:_w1], start=True, stop=True))
                    em.emit("vector", lambda e, _w0=w0, _w1=w1: e.tensor_copy(sco[:, _w0:_w1], p0[:, 0:_w1 - _w0]))
                em.emit("vector", lambda e, _qi=qi: e.tensor_add(
                    sco[:, _qi * 128:(_qi + 1) * 128], sco[:, _qi * 128:(_qi + 1) * 128], tri[:, :]))
                em.emit("vector", lambda e, _W=W: e.tensor_reduce(nmax[:, :], sco[:, 0:_W], AX.X, ALU.max, negate=True))
                em.emit("scalar", lambda e, _W=W: e.activation(sco[:, 0:_W], sco[:, 0:_W], AF.Exp, bias=nmax[:, :]))
                em.emit("vector", lambda e, _W=W: e.tensor_reduce(ssum[:, :], sco[:, 0:_W], AX.X, ALU.add))
                em.emit("vector", lambda e: e.reciprocal(ssum[:, :], ssum[:, :]))
                em.emit("vector", lambda e, _W=W: e.tensor_scalar(prb[:, 0:_W], sco[:, 0:_W], ssum[:, :], None, op0=ALU.mult))
                for ti in range(qi + 1):
                    em.emit("tensor", lambda e, _ti=ti: e.transpose(pT[:, :], prb[:, _ti * 128:(_ti + 1) * 128], idb[:, :]))
                    em.emit("vector", lambda e: e.tensor_copy(pts, pT[:, :]))
                    em.emit("tensor", lambda e, _ti=ti, _qi=qi: e.matmul(
                        pA[:, :], vtm[:, _ti * 128:(_ti + 1) * 128], pts,
                        start=(_ti == 0), stop=(_ti == _qi)))
                em.emit("vector", lambda e: e.tensor_copy(xnb[:, S:S + 128] if S + 128 <= D else xnb[:, 0:128], pA[:, :]))
                em.dma(attnT[h * 128:(h + 1) * 128, j * 128:(j + 1) * 128],
                       xnb[:, S:S + 128] if S + 128 <= D else xnb[:, 0:128])
        em.build_block()
    return nc


def build_l0b(cfg, parity):
    S, D, R0, H0, LS0 = cfg["S"], cfg["D"], cfg["R0"], cfg["H0"], cfg["LS0"]
    NT, CK = S // 128, D // 128
    own = own_tiles(parity, NT)
    NTo = len(own)
    So = NTo * 128
    E0 = H0 * 128
    CKE = E0 // 128

    nc = bass.Bass("TRN2", target_bir_lowering=False)
    xo = nc.dram_tensor("xo", [So, D], F32, kind="ExternalInput").ap()
    attnT = nc.dram_tensor("attnT", [E0, So], BF16, kind="ExternalInput").ap()
    xnTo = nc.dram_tensor("xnTo", [D, So], BF16, kind="ExternalInput").ap()
    wo = nc.dram_tensor("wo", [E0, D], BF16, kind="ExternalInput").ap()
    wbT = nc.dram_tensor("wbT", [D, D], BF16, kind="ExternalInput").ap()
    a0T = nc.dram_tensor("a0T", [D, R0], BF16, kind="ExternalInput").ap()
    wlT = nc.dram_tensor("wlT", [D, R0], BF16, kind="ExternalInput").ap()
    b0T = nc.dram_tensor("b0T", [R0, D], BF16, kind="ExternalInput").ap()
    wrT = nc.dram_tensor("wrT", [R0, D], BF16, kind="ExternalInput").ap()
    bb = nc.dram_tensor("bb", [128, D], F32, kind="ExternalInput").ap()
    x1 = nc.dram_tensor("x1", [So, D], F32, kind="ExternalOutput").ap()
    scr = nc.dram_tensor("scr", [So, D], F32, kind="Internal").ap()

    with contextlib.ExitStack() as ctx:
        sb = lambda n, sh, dt: ctx.enter_context(nc.sbuf_tensor(n, sh, dt))
        ps = lambda n, sh, dt: ctx.enter_context(nc.psum_tensor(n, sh, dt))
        act = sb("act", [128, max(CK, CKE) * So], BF16)
        wstr = sb("wstr", [128, CK * 512], BF16)
        aTl = sb("aTl", [128, CK * R0], BF16)
        wTl = sb("wTl", [128, CK * R0], BF16)
        b0s = sb("b0s", [R0, D], BF16)
        wrs = sb("wrs", [R0, D], BF16)
        aT = sb("aT", [R0, So], BF16)
        bbw = sb("bbw", [128, 512], F32)
        xres = sb("xres", [128, 512], F32)
        wop = sb("wop", [128, 512], F32)
        obuf = sb("obuf", [128, 512], F32)
        p0 = ps("p0", [128, 512], F32)
        em = Em(nc)
        actv = act[:, :].rearrange("p (ck c) -> p ck c", c=So)

        dma_chunked(em, actv, attnT.rearrange("(ck p) c -> p ck c", p=128), CKE)
        for (w0, w1) in wins(D):
            dma_chunked(em, wstr[:, 0:CKE * (w1 - w0)].rearrange("p (ck c) -> p ck c", c=w1 - w0),
                        wo[:, w0:w1].rearrange("(ck p) c -> p ck c", p=128), CKE)
            for j in range(NTo):
                for ck in range(CKE):
                    em.emit("tensor", lambda e, _ck=ck, _j=j, _n=w1 - w0: e.matmul(
                        p0[:, 0:_n], act[:, _ck * So + _j * 128: _ck * So + (_j + 1) * 128],
                        wstr[:, _ck * _n:(_ck + 1) * _n],
                        start=(_ck == 0), stop=(_ck == CKE - 1)), wait=(ck == 0))
                em.emit("vector", lambda e, _n=w1 - w0: e.tensor_copy(obuf[:, 0:_n], p0[:, 0:_n]))
                em.dma(scr[j * 128:(j + 1) * 128, w0:w1], obuf[:, 0:w1 - w0])

        dma_chunked(em, actv, xnTo.rearrange("(ck p) c -> p ck c", p=128), CK)
        em.dma(aTl[:, :].rearrange("p (ck c) -> p ck c", c=R0), a0T.rearrange("(ck p) c -> p ck c", p=128))
        em.dma(wTl[:, :].rearrange("p (ck c) -> p ck c", c=R0), wlT.rearrange("(ck p) c -> p ck c", p=128))
        em.dma(b0s[:, :], b0T)
        em.dma(wrs[:, :], wrT)
        for (w0, w1) in wins(So):
            for ck in range(CK):
                em.emit("tensor", lambda e, _ck=ck, _w0=w0, _w1=w1: e.matmul(
                    p0[0:R0, 0:_w1 - _w0], aTl[:, _ck * R0:(_ck + 1) * R0],
                    act[:, _ck * So + _w0: _ck * So + _w1],
                    start=(_ck == 0), stop=False), wait=(ck == 0))
            for ck in range(CK):
                em.emit("tensor", lambda e, _ck=ck, _w0=w0, _w1=w1: e.matmul(
                    p0[0:R0, 0:_w1 - _w0], wTl[:, _ck * R0:(_ck + 1) * R0],
                    act[:, _ck * So + _w0: _ck * So + _w1],
                    start=False, stop=(_ck == CK - 1)), wait=False)
            em.emit("scalar", lambda e, _w0=w0, _w1=w1: e.mul(aT[:, _w0:_w1], p0[0:R0, 0:_w1 - _w0], float(LS0)))
        for (w0, w1) in wins(D):
            n = w1 - w0
            dma_chunked(em, wstr[:, 0:CK * n].rearrange("p (ck c) -> p ck c", c=n),
                        wbT[:, w0:w1].rearrange("(ck p) c -> p ck c", p=128), CK)
            em.dma(bbw[:, 0:n], bb[:, w0:w1])
            for j in range(NTo):
                for ck in range(CK):
                    em.emit("tensor", lambda e, _ck=ck, _j=j, _n=n: e.matmul(
                        p0[:, 0:_n], act[:, _ck * So + _j * 128: _ck * So + (_j + 1) * 128],
                        wstr[:, _ck * _n:(_ck + 1) * _n],
                        start=(_ck == 0), stop=False), wait=(ck == 0))
                em.emit("tensor", lambda e, _j=j, _w0=w0, _w1=w1: e.matmul(
                    p0[:, 0:_w1 - _w0], aT[:, _j * 128:(_j + 1) * 128], b0s[:, _w0:_w1],
                    start=False, stop=False), wait=False)
                em.emit("tensor", lambda e, _j=j, _w0=w0, _w1=w1: e.matmul(
                    p0[:, 0:_w1 - _w0], aT[:, _j * 128:(_j + 1) * 128], wrs[:, _w0:_w1],
                    start=False, stop=True), wait=False)
                em.dma(xres[:, 0:n], xo[j * 128:(j + 1) * 128, w0:w1])
                em.dma(wop[:, 0:n], scr[j * 128:(j + 1) * 128, w0:w1])
                em.emit("vector", lambda e, _n=n: e.tensor_add(obuf[:, 0:_n], p0[:, 0:_n], bbw[:, 0:_n]))
                em.emit("vector", lambda e, _n=n: e.tensor_add(obuf[:, 0:_n], obuf[:, 0:_n], xres[:, 0:_n]))
                em.emit("vector", lambda e, _n=n: e.tensor_add(obuf[:, 0:_n], obuf[:, 0:_n], wop[:, 0:_n]))
                em.dma(x1[j * 128:(j + 1) * 128, w0:w1], obuf[:, 0:n])
        em.build_block()
    return nc


def build_l1a(cfg, parity):
    S, D, H1, KVH1, GRP = cfg["S"], cfg["D"], cfg["H1"], cfg["KVH1"], cfg["GRP"]
    NT, CK = S // 128, D // 128
    own = own_tiles(parity, NT)
    So = len(own) * 128
    scale_s = float(128.0) ** -0.5

    nc = bass.Bass("TRN2", target_bir_lowering=False)
    x = nc.dram_tensor("x", [S, D], F32, kind="ExternalInput").ap()
    wqT = nc.dram_tensor("wqT", [D, H1 * 128], BF16, kind="ExternalInput").ap()
    wkT = nc.dram_tensor("wkT", [D, KVH1 * 128], BF16, kind="ExternalInput").ap()
    wvT = nc.dram_tensor("wvT", [D, KVH1 * 128], BF16, kind="ExternalInput").ap()
    hp1 = nc.dram_tensor("hp1", [H1 * 128, 128], F32, kind="ExternalInput").ap()
    hp2 = nc.dram_tensor("hp2", [H1 * 128, 128], F32, kind="ExternalInput").ap()
    hb1 = nc.dram_tensor("hb1", [128, H1], F32, kind="ExternalInput").ap()
    hb2 = nc.dram_tensor("hb2", [128, H1], F32, kind="ExternalInput").ap()
    cosA = nc.dram_tensor("cosA", [128, S], F32, kind="ExternalInput").ap()
    sinA = nc.dram_tensor("sinA", [128, S], F32, kind="ExternalInput").ap()
    identb = nc.dram_tensor("identb", [128, 128], BF16, kind="ExternalInput").ap()
    xnTo = nc.dram_tensor("xnTo", [D, So], BF16, kind="ExternalOutput").ap()
    qrT = nc.dram_tensor("qrT", [H1 * 128, So], BF16, kind="ExternalOutput").ap()
    qhT = nc.dram_tensor("qhT", [H1 * 128, So], BF16, kind="ExternalOutput").ap()
    krT = nc.dram_tensor("krT", [KVH1 * 128, S], BF16, kind="ExternalOutput").ap()
    khT = nc.dram_tensor("khT", [H1 * 128, S], BF16, kind="ExternalOutput").ap()
    vtmo = nc.dram_tensor("vtmo", [KVH1 * 128, S], BF16, kind="ExternalOutput").ap()

    with contextlib.ExitStack() as ctx:
        sb = lambda n, sh, dt: ctx.enter_context(nc.sbuf_tensor(n, sh, dt))
        ps = lambda n, sh, dt: ctx.enter_context(nc.psum_tensor(n, sh, dt))
        xnT = sb("xnT", [128, CK * S], BF16)
        xs = sb("xs", [128, D], F32)
        sqh = sb("sqh", [128, max(D // 2, S)], F32)
        xnb = sb("xnb", [128, D], BF16)
        wsl = sb("wsl", [128, CK * 128], BF16)
        kf = sb("kf", [128, S], F32)
        krf = sb("krf", [128, S], F32)
        vT = sb("vT", [128, S], BF16)
        vstg = sb("vstg", [128, 128], BF16)
        assert So <= S // 2 or S >= 2 * So
        qf = kf[:, 0:So]
        qrf = kf[:, So:2 * So]
        coss = sb("coss", [128, S], F32)
        sins = sb("sins", [128, S], F32)
        hsb = sb("hsb", [128, 128], F32)
        hsb2 = sb("hsb2", [128, 128], F32)
        hb1s = sb("hb1s", [128, H1], F32)
        hb2s = sb("hb2s", [128, H1], F32)
        idb = sb("idb", [128, 128], BF16)
        red = sb("red", [128, 1], F32)
        red2 = sb("red2", [128, 1], F32)
        r2b = sb("r2b", [128, 1], F32)
        epsb = sb("epsb", [128, 1], F32)
        p0 = ps("p0", [128, 512], F32)
        pT = ps("pTr", [128, 128], BF16)
        rot = sqh[:, 0:S]
        if D >= 2 * S:
            yb = xs[:, 0:S]
            zb = xs[:, S:2 * S]
        else:
            yb = sb("ybb", [128, S], F32)
            zb = sb("zbb", [128, S], F32)
        krb = xnb[:, 0:S]
        outb = xnb[:, S:2 * S] if D >= 2 * S else sb("outbb", [128, S], BF16)

        em = Em(nc)
        em.dma(idb[:, :], identb)
        em.dma(coss[:, :], cosA)
        em.dma(sins[:, :], sinA)
        em.dma(hb1s[:, :], hb1)
        em.dma(hb2s[:, :], hb2)
        st = dict(xs=xs, sqh=sqh[:, 0:D // 2], red=red, red2=red2, r2b=r2b, xnb=xnb, pT=pT, epsb=epsb)
        ln_transpose(em, st, x, NT, D, S, xnT, idb, own, xnTo)

        def load_wslice(w_hbm, h):
            src = w_hbm[:, h * 128:(h + 1) * 128].rearrange("(ck p) c -> p ck c", p=128)
            dst = wsl[:, :].rearrange("p (ck c) -> p ck c", c=128)
            dma_chunked(em, dst, src, CK)

        def proj_full(out_sb):
            for (w0, w1) in wins(S):
                for ck in range(CK):
                    em.emit("tensor", lambda e, _ck=ck, _w0=w0, _w1=w1: e.matmul(
                        p0[:, 0:_w1 - _w0], wsl[:, _ck * 128:(_ck + 1) * 128],
                        xnT[:, _ck * S + _w0: _ck * S + _w1],
                        start=(_ck == 0), stop=(_ck == CK - 1)), wait=(ck == 0))
                em.emit("vector", lambda e, _w0=w0, _w1=w1: e.tensor_copy(out_sb[:, _w0:_w1], p0[:, 0:_w1 - _w0]))

        def proj_own(out_sb):
            for j, ti in enumerate(own):
                for ck in range(CK):
                    em.emit("tensor", lambda e, _ck=ck, _ti=ti: e.matmul(
                        p0[:, 0:128], wsl[:, _ck * 128:(_ck + 1) * 128],
                        xnT[:, _ck * S + _ti * 128: _ck * S + (_ti + 1) * 128],
                        start=(_ck == 0), stop=(_ck == CK - 1)), wait=(ck == 0))
                em.emit("vector", lambda e, _j=j: e.tensor_copy(out_sb[:, _j * 128:(_j + 1) * 128], p0[:, 0:128]))

        def rope_seg(dst, src, c0, c1, cs0):
            n = c1 - c0
            em.emit("scalar", lambda e: e.mul(rot[0:64, c0:c1], src[64:128, c0:c1], -1.0))
            em.emit("scalar", lambda e: e.copy(rot[64:128, c0:c1], src[0:64, c0:c1]))
            em.emit("vector", lambda e: e.tensor_mul(dst[:, c0:c1], src[:, c0:c1], coss[:, cs0:cs0 + n]))
            em.emit("vector", lambda e: e.tensor_mul(rot[:, c0:c1], rot[:, c0:c1], sins[:, cs0:cs0 + n]))
            em.emit("vector", lambda e: e.tensor_add(dst[:, c0:c1], dst[:, c0:c1], rot[:, c0:c1]))

        def hash_sign(src_f, W, h, out_bf):
            em.dma(hsb[:, :], hp1[h * 128:(h + 1) * 128, :])
            em.dma(hsb2[:, :], hp2[h * 128:(h + 1) * 128, :])
            for (w0, w1) in wins(W):
                em.emit("tensor", lambda e, _w0=w0, _w1=w1: e.matmul(
                    p0[:, 0:_w1 - _w0], hsb[:, :], src_f[:, _w0:_w1], start=True, stop=True))
                em.emit("scalar", lambda e, _w0=w0, _w1=w1, _h=h: e.activation(
                    yb[:, _w0:_w1], p0[:, 0:_w1 - _w0], AF.Silu, bias=hb1s[:, _h:_h + 1]))
            em.emit("vector", lambda e, _W=W: e.tensor_add(yb[:, 0:_W], yb[:, 0:_W], src_f[:, 0:_W]))
            for (w0, w1) in wins(W):
                em.emit("tensor", lambda e, _w0=w0, _w1=w1: e.matmul(
                    p0[:, 0:_w1 - _w0], hsb2[:, :], yb[:, _w0:_w1], start=True, stop=True))
                em.emit("vector", lambda e, _w0=w0, _w1=w1, _h=h: e.scalar_tensor_tensor(
                    zb[:, _w0:_w1], p0[:, 0:_w1 - _w0], hb2s[:, _h:_h + 1],
                    yb[:, _w0:_w1], op0=ALU.add, op1=ALU.add))
            em.emit("scalar", lambda e, _W=W: e.activation(out_bf[:, 0:_W], zb[:, 0:_W], AF.Sign))

        for kv in range(KVH1):
            load_wslice(wkT, kv)
            proj_full(kf)
            rope_seg(krf, kf, 0, S, 0)
            em.emit("vector", lambda e: e.tensor_copy(krb, krf[:, :]))
            em.dma(krT[kv * 128:(kv + 1) * 128, :], krb)
            load_wslice(wvT, kv)
            proj_full(vT)
            for ti in range(NT):
                em.emit("tensor", lambda e, _ti=ti: e.transpose(pT[:, :], vT[:, _ti * 128:(_ti + 1) * 128], idb[:, :]))
                em.emit("vector", lambda e: e.tensor_copy(vstg[:, :], pT[:, :]))
                em.dma(vtmo[kv * 128:(kv + 1) * 128, ti * 128:(ti + 1) * 128], vstg[:, :])
            for g in range(GRP):
                h = kv * GRP + g
                hash_sign(krf, S, h, outb)
                em.dma(khT[h * 128:(h + 1) * 128, :], outb[:, 0:S])
                load_wslice(wqT, h)
                proj_own(qf)
                for j, ti in enumerate(own):
                    rope_seg(qrf, qf, j * 128, (j + 1) * 128, ti * 128)
                hash_sign(qrf, So, h, outb)
                em.dma(qhT[h * 128:(h + 1) * 128, :], outb[:, 0:So])
                em.emit("scalar", lambda e: e.mul(outb[:, 0:So], qrf[:, 0:So], scale_s))
                em.dma(qrT[h * 128:(h + 1) * 128, :], outb[:, 0:So])
        em.build_block()
    return nc


def build_l1b(cfg, parity):
    S, H1, KVH1, GRP, KREM = cfg["S"], cfg["H1"], cfg["KVH1"], cfg["GRP"], cfg["KREM"]
    NT = S // 128
    own = own_tiles(parity, NT)
    So = len(own) * 128
    span = 2 * (128 * S + S)
    nsteps = int(math.ceil(math.log2(span)))
    lo_init = -float(128 * S + S)

    nc = bass.Bass("TRN2", target_bir_lowering=False)
    qrT = nc.dram_tensor("qrT", [H1 * 128, So], BF16, kind="ExternalInput").ap()
    qhT = nc.dram_tensor("qhT", [H1 * 128, So], BF16, kind="ExternalInput").ap()
    krT = nc.dram_tensor("krT", [KVH1 * 128, S], BF16, kind="ExternalInput").ap()
    khT = nc.dram_tensor("khT", [H1 * 128, S], BF16, kind="ExternalInput").ap()
    vtmi = nc.dram_tensor("vtmi", [KVH1 * 128, S], BF16, kind="ExternalInput").ap()
    identb = nc.dram_tensor("identb", [128, 128], BF16, kind="ExternalInput").ap()
    tri9 = nc.dram_tensor("tri9", [128, 128], F32, kind="ExternalInput").ap()
    tri30 = nc.dram_tensor("tri30", [128, 128], F32, kind="ExternalInput").ap()
    revbc = nc.dram_tensor("revbc", [128, S], F32, kind="ExternalInput").ap()
    attn1T = nc.dram_tensor("attn1T", [H1 * 128, So], BF16, kind="ExternalOutput").ap()

    with contextlib.ExitStack() as ctx:
        sb = lambda n, sh, dt: ctx.enter_context(nc.sbuf_tensor(n, sh, dt))
        ps = lambda n, sh, dt: ctx.enter_context(nc.psum_tensor(n, sh, dt))
        qr = sb("qr", [128, So], BF16)
        qh = sb("qh", [128, So], BF16)
        kr = sb("kr", [128, S], BF16)
        kh = sb("kh", [128, S], BF16)
        vtm = sb("vtm", [128, S], BF16)
        ub = sb("ub", [128, S], F32)
        sgn = sb("sgn", [128, S], F32)
        sco = sb("sco", [128, S], F32)
        sel = sb("sel", [128, S], mybir.dt.uint32)
        rev = sb("rev", [128, S], F32)
        negb = sb("negb", [128, S], F32)
        prb = sb("prb", [128, S], BF16)
        pts = sb("pts", [128, 128], BF16)
        ast = sb("ast", [128, So], BF16)
        idb = sb("idb", [128, 128], BF16)
        t9 = sb("t9", [128, 128], F32)
        t30 = sb("t30", [128, 128], F32)
        nmb = sb("nmb", [128, 1], F32)
        cnt = sb("cnt", [128, 1], F32)
        tmp1 = sb("tmp1", [128, 1], F32)
        kap = sb("kap", [128, 1], F32)
        nmax = sb("nmax", [128, 1], F32)
        ssum = sb("ssum", [128, 1], F32)
        p0 = ps("p0", [128, 512], F32)
        pT = ps("pTr", [128, 128], BF16)
        pA = ps("pA", [128, 128], F32)

        em = Em(nc)
        em.dma(idb[:, :], identb)
        em.dma(t9[:, :], tri9)
        em.dma(t30[:, :], tri30)
        em.dma(rev[:, :], revbc)
        em.emit("vector", lambda e: e.memset(negb[:, :], NEG30))
        for h in range(H1):
            kv = h // GRP
            em.dma(qr[:, :], qrT[h * 128:(h + 1) * 128, :])
            em.dma(qh[:, :], qhT[h * 128:(h + 1) * 128, :])
            em.dma(kr[:, :], krT[kv * 128:(kv + 1) * 128, :])
            em.dma(kh[:, :], khT[h * 128:(h + 1) * 128, :])
            em.dma(vtm[:, :], vtmi[kv * 128:(kv + 1) * 128, :])
            for j, qi in enumerate(own):
                W = (qi + 1) * 128
                for (w0, w1) in wins(W):
                    em.emit("tensor", lambda e, _j=j, _w0=w0, _w1=w1: e.matmul(
                        p0[:, 0:_w1 - _w0], qh[:, _j * 128:(_j + 1) * 128],
                        kh[:, _w0:_w1], start=True, stop=True))
                    em.emit("vector", lambda e, _w0=w0, _w1=w1: e.tensor_copy(ub[:, _w0:_w1], p0[:, 0:_w1 - _w0]))
                em.emit("vector", lambda e, _qi=qi: e.tensor_add(
                    ub[:, _qi * 128:(_qi + 1) * 128], ub[:, _qi * 128:(_qi + 1) * 128], t9[:, :]))
                em.emit("vector", lambda e, _W=W: e.tensor_scalar(ub[:, 0:_W], ub[:, 0:_W], float(S), None, op0=ALU.mult))
                em.emit("vector", lambda e, _W=W: e.tensor_add(ub[:, 0:_W], ub[:, 0:_W], rev[:, 0:_W]))
                step0 = float(2 ** (nsteps - 1))
                em.emit("vector", lambda e, _s=step0: e.memset(nmb[:, :], -lo_init - _s + 0.5))
                thr = float(2 * KREM - W)
                for it in range(nsteps):
                    step = float(2 ** (nsteps - 1 - it))
                    nstep = float(2 ** (nsteps - 2 - it)) if it < nsteps - 1 else 0.5
                    em.emit("scalar", lambda e, _W=W: e.activation(sgn[:, 0:_W], ub[:, 0:_W], AF.Sign, bias=nmb[:, :]))
                    em.emit("vector", lambda e, _W=W: e.tensor_reduce(cnt[:, :], sgn[:, 0:_W], AX.X, ALU.add))
                    em.emit("vector", lambda e, _t=thr, _s=step: e.tensor_scalar(
                        tmp1[:, :], cnt[:, :], _t, -_s, op0=ALU.is_ge, op1=ALU.mult))
                    em.emit("vector", lambda e, _ns=nstep: e.scalar_tensor_tensor(
                        nmb[:, :], tmp1[:, :], _ns, nmb[:, :], op0=ALU.add, op1=ALU.add))
                em.emit("vector", lambda e: e.tensor_scalar(kap[:, :], nmb[:, :], -1.0, None, op0=ALU.mult))
                em.emit("vector", lambda e, _W=W: e.tensor_scalar(sel[:, 0:_W], ub[:, 0:_W], kap[:, :], None, op0=ALU.is_lt))
                for (w0, w1) in wins(W):
                    em.emit("tensor", lambda e, _j=j, _w0=w0, _w1=w1: e.matmul(
                        p0[:, 0:_w1 - _w0], qr[:, _j * 128:(_j + 1) * 128],
                        kr[:, _w0:_w1], start=True, stop=True))
                    em.emit("vector", lambda e, _w0=w0, _w1=w1: e.tensor_copy(sco[:, _w0:_w1], p0[:, 0:_w1 - _w0]))
                em.emit("vector", lambda e, _qi=qi: e.tensor_add(
                    sco[:, _qi * 128:(_qi + 1) * 128], sco[:, _qi * 128:(_qi + 1) * 128], t30[:, :]))
                em.emit("vector", lambda e, _W=W: e.copy_predicated(sco[:, 0:_W], sel[:, 0:_W], negb[:, 0:_W]))
                em.emit("vector", lambda e, _W=W: e.tensor_reduce(nmax[:, :], sco[:, 0:_W], AX.X, ALU.max, negate=True))
                em.emit("scalar", lambda e, _W=W: e.activation(sco[:, 0:_W], sco[:, 0:_W], AF.Exp, bias=nmax[:, :]))
                em.emit("vector", lambda e, _W=W: e.tensor_reduce(ssum[:, :], sco[:, 0:_W], AX.X, ALU.add))
                em.emit("vector", lambda e: e.reciprocal(ssum[:, :], ssum[:, :]))
                em.emit("vector", lambda e, _W=W: e.tensor_scalar(prb[:, 0:_W], sco[:, 0:_W], ssum[:, :], None, op0=ALU.mult))
                for ti in range(qi + 1):
                    em.emit("tensor", lambda e, _ti=ti: e.transpose(pT[:, :], prb[:, _ti * 128:(_ti + 1) * 128], idb[:, :]))
                    em.emit("vector", lambda e: e.tensor_copy(pts[:, :], pT[:, :]))
                    em.emit("tensor", lambda e, _ti=ti, _qi=qi: e.matmul(
                        pA[:, :], vtm[:, _ti * 128:(_ti + 1) * 128], pts[:, :],
                        start=(_ti == 0), stop=(_ti == _qi)))
                em.emit("vector", lambda e, _j=j: e.tensor_copy(ast[:, _j * 128:(_j + 1) * 128], pA[:, :]))
            em.dma(attn1T[h * 128:(h + 1) * 128, :], ast[:, :])
        em.build_block()
    return nc


def build_l1c(cfg, parity):
    S, D, NE, RK, H1 = cfg["S"], cfg["D"], cfg["NE"], cfg["RK"], cfg["H1"]
    NT, CK = S // 128, D // 128
    own = own_tiles(parity, NT)
    NTo = len(own)
    So = NTo * 128
    ER = NE * RK
    NKC = (ER + 127) // 128
    ERp = NKC * 128

    nc = bass.Bass("TRN2", target_bir_lowering=False)
    xo = nc.dram_tensor("xo", [So, D], F32, kind="ExternalInput").ap()
    attnT = nc.dram_tensor("attnT", [H1 * 128, So], BF16, kind="ExternalInput").ap()
    xnTo = nc.dram_tensor("xnTo", [D, So], BF16, kind="ExternalInput").ap()
    woT = nc.dram_tensor("woT", [D, D], BF16, kind="ExternalInput").ap()
    wbT = nc.dram_tensor("wbT", [D, D], BF16, kind="ExternalInput").ap()
    gwT = nc.dram_tensor("gwT", [D, NE], BF16, kind="ExternalInput").ap()
    afT = nc.dram_tensor("afT", [D, ER], BF16, kind="ExternalInput").ap()
    bfl = nc.dram_tensor("bfl", [ERp, D], BF16, kind="ExternalInput").ap()
    repc = nc.dram_tensor("repc", [NE, ER], F32, kind="ExternalInput").ap()
    identf = nc.dram_tensor("identf", [128, 128], F32, kind="ExternalInput").ap()
    out = nc.dram_tensor("out", [So, D], F32, kind="ExternalOutput").ap()
    scr = nc.dram_tensor("scr", [So, D], F32, kind="Internal").ap()

    with contextlib.ExitStack() as ctx:
        sb = lambda n, sh, dt: ctx.enter_context(nc.sbuf_tensor(n, sh, dt))
        ps = lambda n, sh, dt: ctx.enter_context(nc.psum_tensor(n, sh, dt))
        act = sb("act", [128, CK * So], BF16)
        wstr = sb("wstr", [128, CK * 512], BF16)
        gws = sb("gws", [128, CK * NE], BF16)
        afs = sb("afs", [128, CK * ER], BF16)
        bstr = sb("bstr", [128, NKC * 512], BF16)
        hT = sb("hT", [128, NTo * NKC * 128], BF16)
        reps = sb("reps", [NE, ER], F32)
        idf = sb("idf", [128, 128], F32)
        lg = sb("lg", [128, NE], F32)
        lg2 = sb("lg2", [128, NE], F32)
        eq1 = sb("eq1", [128, NE], F32)
        eq2 = sb("eq2", [128, NE], F32)
        gate = sb("gate", [128, NE], F32)
        negb = sb("negb", [128, NE], F32)
        gtT = sb("gtT", [NE, 128], F32)
        hb = sb("hb", [128, ER], F32)
        grep = sb("grep", [128, ER], F32)
        m1 = sb("m1", [128, 1], F32)
        m2 = sb("m2", [128, 1], F32)
        dm = sb("dm", [128, 1], F32)
        p1s = sb("p1s", [128, 1], F32)
        p2s = sb("p2s", [128, 1], F32)
        xres = sb("xres", [128, 512], F32)
        wop = sb("wop", [128, 512], F32)
        obuf = sb("obuf", [128, 512], F32)
        p0 = ps("p0", [128, 512], F32)
        pf = ps("pf", [128, 128], F32)
        em = Em(nc)
        actv = act[:, :].rearrange("p (ck c) -> p ck c", c=So)

        dma_chunked(em, actv, attnT.rearrange("(ck p) c -> p ck c", p=128), CK)
        for (w0, w1) in wins(D):
            n = w1 - w0
            dma_chunked(em, wstr[:, 0:CK * n].rearrange("p (ck c) -> p ck c", c=n),
                        woT[:, w0:w1].rearrange("(ck p) c -> p ck c", p=128), CK)
            for j in range(NTo):
                for ck in range(CK):
                    em.emit("tensor", lambda e, _ck=ck, _j=j, _n=n: e.matmul(
                        p0[:, 0:_n], act[:, _ck * So + _j * 128: _ck * So + (_j + 1) * 128],
                        wstr[:, _ck * _n:(_ck + 1) * _n],
                        start=(_ck == 0), stop=(_ck == CK - 1)), wait=(ck == 0))
                em.emit("vector", lambda e, _n=n: e.tensor_copy(obuf[:, 0:_n], p0[:, 0:_n]))
                em.dma(scr[j * 128:(j + 1) * 128, w0:w1], obuf[:, 0:n])

        dma_chunked(em, actv, xnTo.rearrange("(ck p) c -> p ck c", p=128), CK)
        em.dma(gws[:, :].rearrange("p (ck c) -> p ck c", c=NE), gwT.rearrange("(ck p) c -> p ck c", p=128))
        em.dma(afs[:, :].rearrange("p (ck c) -> p ck c", c=ER), afT.rearrange("(ck p) c -> p ck c", p=128))
        em.dma(reps[:, :], repc)
        em.dma(idf[:, :], identf)
        em.emit("vector", lambda e: e.memset(negb[:, :], NEG30))
        for j in range(NTo):
            for ck in range(CK):
                em.emit("tensor", lambda e, _ck=ck, _j=j: e.matmul(
                    p0[:, 0:NE], act[:, _ck * So + _j * 128: _ck * So + (_j + 1) * 128],
                    gws[:, _ck * NE:(_ck + 1) * NE],
                    start=(_ck == 0), stop=(_ck == CK - 1)), wait=(ck == 0))
            em.emit("vector", lambda e: e.tensor_copy(lg[:, :], p0[:, 0:NE]))
            em.emit("vector", lambda e: e.tensor_reduce(m1[:, :], lg[:, :], AX.X, ALU.max))
            em.emit("vector", lambda e: e.tensor_scalar(eq1[:, :], lg[:, :], m1[:, :], None, op0=ALU.is_ge))
            em.emit("vector", lambda e: e.tensor_scalar(lg2[:, :], eq1[:, :], NEG30, None, op0=ALU.mult))
            em.emit("vector", lambda e: e.tensor_add(lg2[:, :], lg2[:, :], lg[:, :]))
            em.emit("vector", lambda e: e.tensor_reduce(m2[:, :], lg2[:, :], AX.X, ALU.max))
            em.emit("vector", lambda e: e.tensor_scalar(eq2[:, :], lg2[:, :], m2[:, :], None, op0=ALU.is_ge))
            em.emit("vector", lambda e: e.tensor_sub(dm[:, :], m1[:, :], m2[:, :]))
            em.emit("scalar", lambda e: e.activation(p1s[:, :], dm[:, :], AF.Sigmoid))
            em.emit("vector", lambda e: e.tensor_sub(dm[:, :], m2[:, :], m1[:, :]))
            em.emit("scalar", lambda e: e.activation(p2s[:, :], dm[:, :], AF.Sigmoid))
            em.emit("vector", lambda e: e.tensor_scalar(eq1[:, :], eq1[:, :], p1s[:, :], 2.0, op0=ALU.mult, op1=ALU.mult))
            em.emit("vector", lambda e: e.tensor_scalar(eq2[:, :], eq2[:, :], p2s[:, :], 2.0, op0=ALU.mult, op1=ALU.mult))
            em.emit("vector", lambda e: e.tensor_add(gate[:, :], eq1[:, :], eq2[:, :]))
            for ck in range(CK):
                em.emit("tensor", lambda e, _ck=ck, _j=j: e.matmul(
                    p0[:, 0:ER], act[:, _ck * So + _j * 128: _ck * So + (_j + 1) * 128],
                    afs[:, _ck * ER:(_ck + 1) * ER],
                    start=(_ck == 0), stop=(_ck == CK - 1)), wait=(ck == 0))
            em.emit("vector", lambda e: e.tensor_copy(hb[:, :], p0[:, 0:ER]))
            em.emit("tensor", lambda e: e.transpose(pf[0:NE, :], gate[:, 0:NE], idf[:, :]))
            em.emit("vector", lambda e: e.tensor_copy(gtT[:, :], pf[0:NE, :]))
            em.emit("tensor", lambda e: e.matmul(p0[:, 0:ER], gtT[:, :], reps[:, :], start=True, stop=True))
            em.emit("vector", lambda e: e.tensor_tensor(grep[:, :], hb[:, :], p0[:, 0:ER], op=ALU.mult))
            for kc in range(NKC):
                c0 = kc * 128
                c1 = min(ER, c0 + 128)
                em.emit("tensor", lambda e, _c0=c0, _c1=c1: e.transpose(pf[0:_c1 - _c0, :], grep[:, _c0:_c1], idf[:, :]))
                em.emit("vector", lambda e, _j=j, _kc=kc, _c0=c0, _c1=c1: e.tensor_copy(
                    hT[0:_c1 - _c0, (_j * NKC + _kc) * 128:(_j * NKC + _kc) * 128 + 128], pf[0:_c1 - _c0, :]))
        for (w0, w1) in wins(D):
            n = w1 - w0
            dma_chunked(em, wstr[:, 0:CK * n].rearrange("p (ck c) -> p ck c", c=n),
                        wbT[:, w0:w1].rearrange("(ck p) c -> p ck c", p=128), CK)
            em.dma(bstr[:, 0:NKC * n].rearrange("p (kc c) -> p kc c", c=n),
                   bfl[:, w0:w1].rearrange("(kc p) c -> p kc c", p=128))
            for j in range(NTo):
                for ck in range(CK):
                    em.emit("tensor", lambda e, _ck=ck, _j=j, _n=n: e.matmul(
                        p0[:, 0:_n], act[:, _ck * So + _j * 128: _ck * So + (_j + 1) * 128],
                        wstr[:, _ck * _n:(_ck + 1) * _n],
                        start=(_ck == 0), stop=False), wait=(ck == 0))
                for kc in range(NKC):
                    c0 = kc * 128
                    c1 = min(ER, c0 + 128)
                    em.emit("tensor", lambda e, _j=j, _kc=kc, _c0=c0, _c1=c1, _n=n: e.matmul(
                        p0[:, 0:_n], hT[0:_c1 - _c0, (_j * NKC + _kc) * 128:(_j * NKC + _kc) * 128 + 128],
                        bstr[0:_c1 - _c0, _kc * _n:(_kc + 1) * _n],
                        start=False, stop=(_kc == NKC - 1)), wait=False)
                em.dma(xres[:, 0:n], xo[j * 128:(j + 1) * 128, w0:w1])
                em.dma(wop[:, 0:n], scr[j * 128:(j + 1) * 128, w0:w1])
                em.emit("vector", lambda e, _n=n: e.tensor_add(obuf[:, 0:_n], p0[:, 0:_n], xres[:, 0:_n]))
                em.emit("vector", lambda e, _n=n: e.tensor_add(obuf[:, 0:_n], obuf[:, 0:_n], wop[:, 0:_n]))
                em.dma(out[j * 128:(j + 1) * 128, w0:w1], obuf[:, 0:n])
        em.build_block()
    return nc


def build_prep(rows, cols=8192):
    nc = bass.Bass("TRN2", target_bir_lowering=False)
    src = nc.dram_tensor("src", [rows, cols], F32, kind="ExternalInput").ap()
    dst = nc.dram_tensor("dst", [rows, cols], BF16, kind="ExternalOutput").ap()
    with contextlib.ExitStack() as ctx:
        xs = ctx.enter_context(nc.sbuf_tensor("xs", [128, cols], F32))
        xb = ctx.enter_context(nc.sbuf_tensor("xb", [128, cols], BF16))
        em = Em(nc)
        for t in range(rows // 128):
            em.dma(xs[:, :], src[t * 128:(t + 1) * 128, :])
            em.emit("scalar", lambda e: e.copy(xb[:, :], xs[:, :]))
            em.dma(dst[t * 128:(t + 1) * 128, :], xb[:, :])
        em.build_block()
    return nc



def rope_tables(S, Dh=128, base=10000.0):
    inv = 1.0 / base ** (np.arange(0, Dh, 2, dtype=np.float64) / Dh)
    t = np.arange(S, dtype=np.float64)
    fr = np.outer(t, inv)
    emb = np.concatenate([fr, fr], -1)
    return np.cos(emb).astype(np.float32), np.sin(emb).astype(np.float32)


def make_consts(cfg):
    S = cfg["S"]
    c = {}
    c["identb"] = np.eye(128, dtype=np.float32).astype(BF)
    c["identf"] = np.eye(128, dtype=np.float32)
    tri = np.triu(np.ones((128, 128), np.float32), 1)
    c["tri30"] = (tri * NEG30).astype(np.float32)
    c["tri9"] = (tri * NEG9).astype(np.float32)
    c["revbc"] = np.ascontiguousarray(
        np.tile((S - 1 - np.arange(S, dtype=np.float32))[None, :], (128, 1)))
    NE, RK = cfg["NE"], cfg["RK"]
    rep = np.zeros((NE, NE * RK), np.float32)
    for e in range(NE):
        rep[e, e * RK:(e + 1) * RK] = 1.0
    c["repc"] = rep
    cos, sin = rope_tables(S)
    c["cosA"] = np.ascontiguousarray(cos.T)
    c["sinA"] = np.ascontiguousarray(sin.T)
    return c


def prep_weights(cfg, ins):
    D, H0, DQ0, NE, RK = cfg["D"], cfg["H0"], cfg["DQ0"], cfg["NE"], cfg["RK"]
    ER = NE * RK
    NKC = (ER + 127) // 128
    ERp = NKC * 128
    C = np.ascontiguousarray
    w = {}
    wq = np.asarray(ins["L0_attn_wq"], np.float32)
    w["l0_wq"] = C(wq.reshape(D, H0, DQ0)[:, :, :128].reshape(D, H0 * 128))
    wkv = np.asarray(ins["L0_attn_wkv"], np.float32)
    w["l0_wk"] = C(wkv[:, :H0 * 128])
    w["l0_wv"] = C(wkv[:, H0 * 128:2 * H0 * 128])
    w["l0_wo"] = C(np.asarray(ins["L0_attn_wo"], np.float32))
    w["l0_wbT"] = C(np.asarray(ins["L0_ffn_W_base"], np.float32).T)
    w["l0_a0T"] = C(np.asarray(ins["L0_ffn_A"], np.float32).T)
    w["l0_wlT"] = C(np.asarray(ins["L0_ffn_WL"], np.float32).T)
    w["l0_b0T"] = C(np.asarray(ins["L0_ffn_B"], np.float32).T)
    w["l0_wrT"] = C(np.asarray(ins["L0_ffn_WR"], np.float32).T)
    w["l1_wqT"] = C(np.asarray(ins["L1_attn_Wq"], np.float32).T)
    w["l1_wkT"] = C(np.asarray(ins["L1_attn_Wk"], np.float32).T)
    w["l1_wvT"] = C(np.asarray(ins["L1_attn_Wv"], np.float32).T)
    w["l1_woT"] = C(np.asarray(ins["L1_attn_Wo"], np.float32).T)
    w["l1_wbT"] = C(np.asarray(ins["L1_ffn_base_W"], np.float32).T)
    w["l1_gwT"] = C(np.asarray(ins["L1_ffn_gate_W"], np.float32).T)
    la = np.asarray(ins["L1_ffn_lora_A"], np.float32).reshape(ER, D)
    w["l1_afT"] = C(la.T)
    lb = np.asarray(ins["L1_ffn_lora_B"], np.float32)
    bfl = lb.transpose(0, 2, 1).reshape(ER, D)
    if ERp != ER:
        bfl = np.concatenate([bfl, np.zeros((ERp - ER, D), np.float32)], 0)
    w["l1_bfl"] = C(bfl)
    return w


def convert_bf16_on_device(w, ncores, runner):
    order = sorted(w.keys())
    flats = [w[k].reshape(-1) for k in order]
    sizes = [f.size for f in flats]
    total = int(np.sum(sizes))
    cols = 8192
    per = -(-total // (ncores * 128 * cols)) * 128 * cols
    buf = np.zeros(ncores * per, np.float32)
    buf[:total] = np.concatenate(flats)
    shards = buf.reshape(ncores, per // cols, cols)
    nc = build_prep(per // cols, cols)
    res = runner(nc, [{"src": np.ascontiguousarray(s)} for s in shards], list(range(ncores)))
    outf = np.concatenate([np.asarray(r["dst"]).reshape(-1) for r in res])
    wb = {}
    off = 0
    for k, sz in zip(order, sizes):
        wb[k] = np.ascontiguousarray(outf[off:off + sz].reshape(w[k].shape))
        off += sz
    return wb


def _hw_runner(nc, in_maps, core_ids):
    return run_bass_kernel_spmd(nc, in_maps, core_ids=core_ids).results


def kernel(**ins):
    return run_all(full_cfg(), ins, ncores=8, runner=_hw_runner)


def run_all(cfg, ins, ncores, runner):
    B, S, D, H1 = cfg["B"], cfg["S"], cfg["D"], cfg["H1"]
    NT = S // 128
    x0 = np.ascontiguousarray(np.asarray(ins["L0_attn_hidden_states"], np.float32).reshape(B, S, D))
    consts = make_consts(cfg)
    w = prep_weights(cfg, ins)
    wb = convert_bf16_on_device(w, ncores, runner)
    bb_bc = np.ascontiguousarray(np.tile(np.asarray(ins["L0_ffn_b_base"], np.float32)[None, :], (128, 1)))
    hb1 = np.ascontiguousarray(np.asarray(ins["L1_attn_hash_bias1"], np.float32).reshape(H1, 128).T)
    hb2 = np.ascontiguousarray(np.asarray(ins["L1_attn_hash_bias2"], np.float32).reshape(H1, 128).T)
    hp1 = np.ascontiguousarray(np.asarray(ins["L1_attn_hash_proj1"], np.float32).reshape(H1 * 128, 128))
    hp2 = np.ascontiguousarray(np.asarray(ins["L1_attn_hash_proj2"], np.float32).reshape(H1 * 128, 128))

    cores = list(range(ncores))
    parities = [c % 2 for c in cores]
    bids = [c // 2 for c in cores]
    owns = [own_tiles(p, NT) for p in parities]

    def gather_rows(arr_b, ot):
        return np.ascontiguousarray(np.concatenate([arr_b[ti * 128:(ti + 1) * 128] for ti in ot], 0))

    def run_pair(builder, mapper):
        ncs = [builder(cfg, 0), builder(cfg, 1)]
        ev = [c for c in cores if c % 2 == 0]
        od = [c for c in cores if c % 2 == 1]
        r0 = runner(ncs[0], [mapper(c) for c in ev], ev)
        r1 = runner(ncs[1], [mapper(c) for c in od], od)
        res = {}
        for i, c in enumerate(ev):
            res[c] = r0[i]
        for i, c in enumerate(od):
            res[c] = r1[i]
        return res

    resa = run_pair(build_l0a, lambda c: {
        "x": x0[bids[c]], "wq": wb["l0_wq"], "wk": wb["l0_wk"], "wv": wb["l0_wv"],
        "identb": consts["identb"], "tri30": consts["tri30"]})
    resb = run_pair(build_l0b, lambda c: {
        "xo": gather_rows(x0[bids[c]], owns[c]),
        "attnT": np.asarray(resa[c]["attnT"]), "xnTo": np.asarray(resa[c]["xnTo"]),
        "wo": wb["l0_wo"], "wbT": wb["l0_wbT"], "a0T": wb["l0_a0T"], "wlT": wb["l0_wlT"],
        "b0T": wb["l0_b0T"], "wrT": wb["l0_wrT"], "bb": bb_bc})
    x1 = np.zeros((B, S, D), np.float32)
    for c in cores:
        xc = np.asarray(resb[c]["x1"])
        for j, ti in enumerate(owns[c]):
            x1[bids[c], ti * 128:(ti + 1) * 128] = xc[j * 128:(j + 1) * 128]
    res1a = run_pair(build_l1a, lambda c: {
        "x": x1[bids[c]], "wqT": wb["l1_wqT"], "wkT": wb["l1_wkT"], "wvT": wb["l1_wvT"],
        "hp1": hp1, "hp2": hp2, "hb1": hb1, "hb2": hb2,
        "cosA": consts["cosA"], "sinA": consts["sinA"], "identb": consts["identb"]})
    res1b = run_pair(build_l1b, lambda c: {
        "qrT": np.asarray(res1a[c]["qrT"]), "qhT": np.asarray(res1a[c]["qhT"]),
        "krT": np.asarray(res1a[c]["krT"]), "khT": np.asarray(res1a[c]["khT"]),
        "vtmi": np.asarray(res1a[c]["vtmo"]), "identb": consts["identb"],
        "tri9": consts["tri9"], "tri30": consts["tri30"], "revbc": consts["revbc"]})
    res1c = run_pair(build_l1c, lambda c: {
        "xo": gather_rows(x1[bids[c]], owns[c]),
        "attnT": np.asarray(res1b[c]["attn1T"]), "xnTo": np.asarray(res1a[c]["xnTo"]),
        "woT": wb["l1_woT"], "wbT": wb["l1_wbT"], "gwT": wb["l1_gwT"],
        "afT": wb["l1_afT"], "bfl": wb["l1_bfl"],
        "repc": consts["repc"], "identf": consts["identf"]})
    outf = np.zeros((B, S, D), np.float32)
    for c in cores:
        oc = np.asarray(res1c[c]["out"])
        for j, ti in enumerate(owns[c]):
            outf[bids[c], ti * 128:(ti + 1) * 128] = oc[j * 128:(j + 1) * 128]
    return outf



def build_fused(cfg):
    B, S, D, H0, DQ0 = cfg["B"], cfg["S"], cfg["D"], cfg["H0"], cfg["DQ0"]
    H1, KVH1, GRP, NE, RK, R0, KREM, LS0 = (cfg["H1"], cfg["KVH1"], cfg["GRP"],
                                            cfg["NE"], cfg["RK"], cfg["R0"],
                                            cfg["KREM"], cfg["LS0"])
    NT, CK = S // 128, D // 128
    E0 = H0 * 128
    CKE = E0 // 128
    ER = NE * RK
    NKC = (ER + 127) // 128
    ERp = NKC * 128
    scale_q = float(DQ0) ** -0.5
    scale_s = float(128.0) ** -0.5
    span = 2 * (128 * S + S)
    nsteps = int(math.ceil(math.log2(span)))
    lo_init = -float(128 * S + S)

    WSHAPES = dict(l0_wq=(D, E0), l0_wk=(D, E0), l0_wv=(D, E0), l0_wo=(E0, D),
                   l0_wbT=(D, D), l0_a0T=(D, R0), l0_wlT=(D, R0), l0_b0T=(R0, D),
                   l0_wrT=(R0, D), l1_wqT=(D, H1 * 128), l1_wkT=(D, KVH1 * 128),
                   l1_wvT=(D, KVH1 * 128), l1_woT=(D, D), l1_wbT=(D, D),
                   l1_gwT=(D, NE), l1_afT=(D, ER), l1_bfl=(ERp, D))

    nc = bass.Bass("TRN2", target_bir_lowering=False)
    x = nc.dram_tensor("x", [S, D], F32, kind="ExternalInput").ap()
    wfin = {k: nc.dram_tensor("wf_" + k, list(sh), F32, kind="ExternalInput").ap()
            for k, sh in WSHAPES.items()}
    wb = {k: nc.dram_tensor("wb_" + k, list(sh), BF16, kind="Internal").ap()
          for k, sh in WSHAPES.items()}
    hp1 = nc.dram_tensor("hp1", [H1 * 128, 128], F32, kind="ExternalInput").ap()
    hp2 = nc.dram_tensor("hp2", [H1 * 128, 128], F32, kind="ExternalInput").ap()
    hb1 = nc.dram_tensor("hb1", [128, H1], F32, kind="ExternalInput").ap()
    hb2 = nc.dram_tensor("hb2", [128, H1], F32, kind="ExternalInput").ap()
    cosA = nc.dram_tensor("cosA", [128, S], F32, kind="ExternalInput").ap()
    sinA = nc.dram_tensor("sinA", [128, S], F32, kind="ExternalInput").ap()
    identb = nc.dram_tensor("identb", [128, 128], BF16, kind="ExternalInput").ap()
    identf = nc.dram_tensor("identf", [128, 128], F32, kind="ExternalInput").ap()
    tri30h = nc.dram_tensor("tri30", [128, 128], F32, kind="ExternalInput").ap()
    tri9h = nc.dram_tensor("tri9", [128, 128], F32, kind="ExternalInput").ap()
    revbc = nc.dram_tensor("revbc", [128, S], F32, kind="ExternalInput").ap()
    repch = nc.dram_tensor("repc", [NE, ER], F32, kind="ExternalInput").ap()
    bbh = nc.dram_tensor("bb", [128, D], F32, kind="ExternalInput").ap()
    out = nc.dram_tensor("out", [S, D], F32, kind="ExternalOutput").ap()
    xnTo = nc.dram_tensor("xnTo_s", [D, S], BF16, kind="Internal").ap()
    attnT = nc.dram_tensor("attnT_s", [E0, S], BF16, kind="Internal").ap()
    scrA = nc.dram_tensor("scrA", [S, D], F32, kind="Internal").ap()
    x1 = nc.dram_tensor("x1_s", [S, D], F32, kind="Internal").ap()
    xn1To = nc.dram_tensor("xn1To_s", [D, S], BF16, kind="Internal").ap()
    qrT = nc.dram_tensor("qrT_s", [H1 * 128, S], BF16, kind="Internal").ap()
    qhT = nc.dram_tensor("qhT_s", [H1 * 128, S], BF16, kind="Internal").ap()
    krT = nc.dram_tensor("krT_s", [KVH1 * 128, S], BF16, kind="Internal").ap()
    khT = nc.dram_tensor("khT_s", [H1 * 128, S], BF16, kind="Internal").ap()
    vtmo = nc.dram_tensor("vtm_s", [KVH1 * 128, S], BF16, kind="Internal").ap()
    attn1T = nc.dram_tensor("attn1T_s", [H1 * 128, S], BF16, kind="Internal").ap()
    scrC = nc.dram_tensor("scrC", [S, D], F32, kind="Internal").ap()

    with contextlib.ExitStack() as octx:
        osb = lambda n, sh, dt: octx.enter_context(nc.sbuf_tensor(n, sh, dt))
        ops_ = lambda n, sh, dt: octx.enter_context(nc.psum_tensor(n, sh, dt))
        idb = osb("idb", [128, 128], BF16)
        idf = osb("idf", [128, 128], F32)
        p0 = ops_("p0", [128, 512], F32)
        pT = ops_("pTr", [128, 128], BF16)
        pA = ops_("pA", [128, 128], F32)
        pf = ops_("pf", [128, 128], F32)
        em = Em(nc)
        em.dma(idb[:, :], identb)
        em.dma(idf[:, :], identf)

        def _stage1():
            F = 1024
            with contextlib.ExitStack() as sctx:
                cs = lambda n, sh, dt: sctx.enter_context(nc.sbuf_tensor(n, sh, dt))
                cxf = cs("cxf", [128, F], F32)
                cxb = cs("cxb", [128, F], BF16)
                for k, sh in WSHAPES.items():
                    Z = sh[0] * sh[1]
                    assert Z % F == 0, (k, sh)
                    nr = Z // F
                    src = wfin[k].rearrange("a b -> (a b)").rearrange("(n f) -> n f", f=F)
                    dst = wb[k].rearrange("a b -> (a b)").rearrange("(n f) -> n f", f=F)
                    for r0 in range(0, nr, 128):
                        r1 = min(nr, r0 + 128)
                        em.dma(cxf[0:r1 - r0, :], src[r0:r1, :])
                        em.emit("scalar", lambda e, _r=r1 - r0: e.copy(cxb[0:_r, :], cxf[0:_r, :]))
                        em.dma(dst[r0:r1, :], cxb[0:r1 - r0, :])

        _stage1()
        def _stage2():
            with contextlib.ExitStack() as sctx:
                cs = lambda n, sh, dt: sctx.enter_context(nc.sbuf_tensor(n, sh, dt))
                xnT = cs("xnT", [128, CK * S], BF16)
                xs = cs("xs", [128, D], F32)
                sqh = cs("sqh", [128, D // 2], F32)
                xnb = cs("xnb", [128, D], BF16)
                wsl = cs("wsl", [128, CK * 128], BF16)
                qT = cs("qT", [128, S], BF16)
                kT = cs("kT", [128, S], BF16)
                vT = cs("vT", [128, S], BF16)
                vtm = cs("vtm", [128, S], BF16)
                tri = cs("tri", [128, 128], F32)
                red = cs("red", [128, 1], F32)
                red2 = cs("red2", [128, 1], F32)
                r2b = cs("r2b", [128, 1], F32)
                epsb = cs("epsb", [128, 1], F32)
                nmax = cs("nmax", [128, 1], F32)
                ssum = cs("ssum", [128, 1], F32)
                astg = cs("astg", [128, 128], BF16)
                sco = xs[:, 0:S]
                prb = xnb[:, 0:S]
                pts = wsl[:, 0:128]
                em.dma(tri[:, :], tri30h)
                st = dict(xs=xs, sqh=sqh, red=red, red2=red2, r2b=r2b, xnb=xnb, pT=pT, epsb=epsb)
                allt = list(range(NT))
                ln_transpose(em, st, x, NT, D, S, xnT, idb, allt, xnTo)

                def load_wslice(w_hbm, h):
                    src = w_hbm[:, h * 128:(h + 1) * 128].rearrange("(ck p) c -> p ck c", p=128)
                    dst = wsl[:, :].rearrange("p (ck c) -> p ck c", c=128)
                    dma_chunked(em, dst, src, CK)

                def proj(out_sb, scale):
                    for (w0, w1) in wins(S):
                        for ck in range(CK):
                            em.emit("tensor", lambda e, _ck=ck, _w0=w0, _w1=w1: e.matmul(
                                p0[:, 0:_w1 - _w0], wsl[:, _ck * 128:(_ck + 1) * 128],
                                xnT[:, _ck * S + _w0: _ck * S + _w1],
                                start=(_ck == 0), stop=(_ck == CK - 1)), wait=(ck == 0))
                        if scale == 1.0:
                            em.emit("vector", lambda e, _w0=w0, _w1=w1: e.tensor_copy(out_sb[:, _w0:_w1], p0[:, 0:_w1 - _w0]))
                        else:
                            em.emit("scalar", lambda e, _w0=w0, _w1=w1: e.mul(out_sb[:, _w0:_w1], p0[:, 0:_w1 - _w0], scale))

                for h in range(H0):
                    load_wslice(wb["l0_wq"], h)
                    proj(qT, scale_q)
                    load_wslice(wb["l0_wk"], h)
                    proj(kT, 1.0)
                    load_wslice(wb["l0_wv"], h)
                    proj(vT, 1.0)
                    for ti in range(NT):
                        em.emit("tensor", lambda e, _ti=ti: e.transpose(pT[:, :], vT[:, _ti * 128:(_ti + 1) * 128], idb[:, :]))
                        em.emit("vector", lambda e, _ti=ti: e.tensor_copy(vtm[:, _ti * 128:(_ti + 1) * 128], pT[:, :]))
                    for qi in range(NT):
                        W = (qi + 1) * 128
                        for (w0, w1) in wins(W):
                            em.emit("tensor", lambda e, _qi=qi, _w0=w0, _w1=w1: e.matmul(
                                p0[:, 0:_w1 - _w0], qT[:, _qi * 128:(_qi + 1) * 128],
                                kT[:, _w0:_w1], start=True, stop=True))
                            em.emit("vector", lambda e, _w0=w0, _w1=w1: e.tensor_copy(sco[:, _w0:_w1], p0[:, 0:_w1 - _w0]))
                        em.emit("vector", lambda e, _qi=qi: e.tensor_add(
                            sco[:, _qi * 128:(_qi + 1) * 128], sco[:, _qi * 128:(_qi + 1) * 128], tri[:, :]))
                        em.emit("vector", lambda e, _W=W: e.tensor_reduce(nmax[:, :], sco[:, 0:_W], AX.X, ALU.max, negate=True))
                        em.emit("scalar", lambda e, _W=W: e.activation(sco[:, 0:_W], sco[:, 0:_W], AF.Exp, bias=nmax[:, :]))
                        em.emit("vector", lambda e, _W=W: e.tensor_reduce(ssum[:, :], sco[:, 0:_W], AX.X, ALU.add))
                        em.emit("vector", lambda e: e.reciprocal(ssum[:, :], ssum[:, :]))
                        em.emit("vector", lambda e, _W=W: e.tensor_scalar(prb[:, 0:_W], sco[:, 0:_W], ssum[:, :], None, op0=ALU.mult))
                        for ti in range(qi + 1):
                            em.emit("tensor", lambda e, _ti=ti: e.transpose(pT[:, :], prb[:, _ti * 128:(_ti + 1) * 128], idb[:, :]))
                            em.emit("vector", lambda e: e.tensor_copy(pts, pT[:, :]))
                            em.emit("tensor", lambda e, _ti=ti, _qi=qi: e.matmul(
                                pA[:, :], vtm[:, _ti * 128:(_ti + 1) * 128], pts,
                                start=(_ti == 0), stop=(_ti == _qi)))
                        em.emit("vector", lambda e: e.tensor_copy(astg[:, :], pA[:, :]))
                        em.dma(attnT[h * 128:(h + 1) * 128, qi * 128:(qi + 1) * 128], astg[:, :])

        _stage2()
        def _stage3():
            with contextlib.ExitStack() as sctx:
                cs = lambda n, sh, dt: sctx.enter_context(nc.sbuf_tensor(n, sh, dt))
                act = cs("act", [128, max(CK, CKE) * S], BF16)
                wstr = cs("wstr", [128, CK * 512], BF16)
                aTl = cs("aTl", [128, CK * R0], BF16)
                wTl = cs("wTl", [128, CK * R0], BF16)
                b0s = cs("b0s", [R0, D], BF16)
                wrs = cs("wrs", [R0, D], BF16)
                aT = cs("aT", [R0, S], BF16)
                bbw = cs("bbw", [128, 512], F32)
                xres = cs("xres", [128, 512], F32)
                wop = cs("wop", [128, 512], F32)
                obuf = cs("obuf", [128, 512], F32)
                actv = act[:, :].rearrange("p (ck c) -> p ck c", c=S)
                dma_chunked(em, actv, attnT.rearrange("(ck p) c -> p ck c", p=128), CKE)
                for (w0, w1) in wins(D):
                    n = w1 - w0
                    dma_chunked(em, wstr[:, 0:CKE * n].rearrange("p (ck c) -> p ck c", c=n),
                                wb["l0_wo"][:, w0:w1].rearrange("(ck p) c -> p ck c", p=128), CKE)
                    for j in range(NT):
                        for ck in range(CKE):
                            em.emit("tensor", lambda e, _ck=ck, _j=j, _n=n: e.matmul(
                                p0[:, 0:_n], act[:, _ck * S + _j * 128: _ck * S + (_j + 1) * 128],
                                wstr[:, _ck * _n:(_ck + 1) * _n],
                                start=(_ck == 0), stop=(_ck == CKE - 1)), wait=(ck == 0))
                        em.emit("vector", lambda e, _n=n: e.tensor_copy(obuf[:, 0:_n], p0[:, 0:_n]))
                        em.dma(scrA[j * 128:(j + 1) * 128, w0:w1], obuf[:, 0:n])
                dma_chunked(em, actv, xnTo.rearrange("(ck p) c -> p ck c", p=128), CK)
                em.dma(aTl[:, :].rearrange("p (ck c) -> p ck c", c=R0), wb["l0_a0T"].rearrange("(ck p) c -> p ck c", p=128))
                em.dma(wTl[:, :].rearrange("p (ck c) -> p ck c", c=R0), wb["l0_wlT"].rearrange("(ck p) c -> p ck c", p=128))
                em.dma(b0s[:, :], wb["l0_b0T"])
                em.dma(wrs[:, :], wb["l0_wrT"])
                for (w0, w1) in wins(S):
                    for ck in range(CK):
                        em.emit("tensor", lambda e, _ck=ck, _w0=w0, _w1=w1: e.matmul(
                            p0[0:R0, 0:_w1 - _w0], aTl[:, _ck * R0:(_ck + 1) * R0],
                            act[:, _ck * S + _w0: _ck * S + _w1],
                            start=(_ck == 0), stop=False), wait=(ck == 0))
                    for ck in range(CK):
                        em.emit("tensor", lambda e, _ck=ck, _w0=w0, _w1=w1: e.matmul(
                            p0[0:R0, 0:_w1 - _w0], wTl[:, _ck * R0:(_ck + 1) * R0],
                            act[:, _ck * S + _w0: _ck * S + _w1],
                            start=False, stop=(_ck == CK - 1)), wait=False)
                    em.emit("scalar", lambda e, _w0=w0, _w1=w1: e.mul(aT[:, _w0:_w1], p0[0:R0, 0:_w1 - _w0], float(LS0)))
                for (w0, w1) in wins(D):
                    n = w1 - w0
                    dma_chunked(em, wstr[:, 0:CK * n].rearrange("p (ck c) -> p ck c", c=n),
                                wb["l0_wbT"][:, w0:w1].rearrange("(ck p) c -> p ck c", p=128), CK)
                    em.dma(bbw[:, 0:n], bbh[:, w0:w1])
                    for j in range(NT):
                        for ck in range(CK):
                            em.emit("tensor", lambda e, _ck=ck, _j=j, _n=n: e.matmul(
                                p0[:, 0:_n], act[:, _ck * S + _j * 128: _ck * S + (_j + 1) * 128],
                                wstr[:, _ck * _n:(_ck + 1) * _n],
                                start=(_ck == 0), stop=False), wait=(ck == 0))
                        em.emit("tensor", lambda e, _j=j, _w0=w0, _w1=w1: e.matmul(
                            p0[:, 0:_w1 - _w0], aT[:, _j * 128:(_j + 1) * 128], b0s[:, _w0:_w1],
                            start=False, stop=False), wait=False)
                        em.emit("tensor", lambda e, _j=j, _w0=w0, _w1=w1: e.matmul(
                            p0[:, 0:_w1 - _w0], aT[:, _j * 128:(_j + 1) * 128], wrs[:, _w0:_w1],
                            start=False, stop=True), wait=False)
                        em.dma(xres[:, 0:n], x[j * 128:(j + 1) * 128, w0:w1])
                        em.dma(wop[:, 0:n], scrA[j * 128:(j + 1) * 128, w0:w1])
                        em.emit("vector", lambda e, _n=n: e.tensor_add(obuf[:, 0:_n], p0[:, 0:_n], bbw[:, 0:_n]))
                        em.emit("vector", lambda e, _n=n: e.tensor_add(obuf[:, 0:_n], obuf[:, 0:_n], xres[:, 0:_n]))
                        em.emit("vector", lambda e, _n=n: e.tensor_add(obuf[:, 0:_n], obuf[:, 0:_n], wop[:, 0:_n]))
                        em.dma(x1[j * 128:(j + 1) * 128, w0:w1], obuf[:, 0:n])

        _stage3()
        def _stage4():
            with contextlib.ExitStack() as sctx:
                cs = lambda n, sh, dt: sctx.enter_context(nc.sbuf_tensor(n, sh, dt))
                xnT = cs("xnT3", [128, CK * S], BF16)
                xs = cs("xs3", [128, D], F32)
                sqh = cs("sqh3", [128, max(D // 2, S)], F32)
                xnb = cs("xnb3", [128, D], BF16)
                wsl = cs("wsl3", [128, CK * 128], BF16)
                kf = cs("kf3", [128, S], F32)
                krf = cs("krf3", [128, S], F32)
                vT = cs("vT3", [128, S], BF16)
                vstg = cs("vstg3", [128, 128], BF16)
                coss = cs("coss3", [128, S], F32)
                sins = cs("sins3", [128, S], F32)
                hsb = cs("hsb3", [128, 128], F32)
                hsb2 = cs("hsb23", [128, 128], F32)
                hb1s = cs("hb1s3", [128, H1], F32)
                hb2s = cs("hb2s3", [128, H1], F32)
                red = cs("red3", [128, 1], F32)
                red2 = cs("red23", [128, 1], F32)
                r2b = cs("r2b3", [128, 1], F32)
                epsb = cs("epsb3", [128, 1], F32)
                rot = sqh[:, 0:S]
                if D >= 2 * S:
                    yb = xs[:, 0:S]
                    zb = xs[:, S:2 * S]
                    qf = kf[:, 0:S]
                    qrf = xs[:, S:2 * S]
                    krb = xnb[:, 0:S]
                    outb = xnb[:, S:2 * S]
                else:
                    yb = cs("ybb3", [128, S], F32)
                    zb = cs("zbb3", [128, S], F32)
                    qf = kf[:, 0:S]
                    qrf = cs("qrf3", [128, S], F32)
                    krb = xnb[:, 0:S]
                    outb = cs("outbb3", [128, S], BF16)
                em.dma(coss[:, :], cosA)
                em.dma(sins[:, :], sinA)
                em.dma(hb1s[:, :], hb1)
                em.dma(hb2s[:, :], hb2)
                st = dict(xs=xs, sqh=sqh[:, 0:D // 2], red=red, red2=red2, r2b=r2b, xnb=xnb, pT=pT, epsb=epsb)
                ln_transpose(em, st, x1, NT, D, S, xnT, idb, list(range(NT)), xn1To)

                def load_wslice3(w_hbm, h):
                    src = w_hbm[:, h * 128:(h + 1) * 128].rearrange("(ck p) c -> p ck c", p=128)
                    dst = wsl[:, :].rearrange("p (ck c) -> p ck c", c=128)
                    dma_chunked(em, dst, src, CK)

                def proj_full3(out_sb):
                    for (w0, w1) in wins(S):
                        for ck in range(CK):
                            em.emit("tensor", lambda e, _ck=ck, _w0=w0, _w1=w1: e.matmul(
                                p0[:, 0:_w1 - _w0], wsl[:, _ck * 128:(_ck + 1) * 128],
                                xnT[:, _ck * S + _w0: _ck * S + _w1],
                                start=(_ck == 0), stop=(_ck == CK - 1)), wait=(ck == 0))
                        em.emit("vector", lambda e, _w0=w0, _w1=w1: e.tensor_copy(out_sb[:, _w0:_w1], p0[:, 0:_w1 - _w0]))

                def rope3(dst, src):
                    em.emit("scalar", lambda e: e.mul(rot[0:64, 0:S], src[64:128, 0:S], -1.0))
                    em.emit("scalar", lambda e: e.copy(rot[64:128, 0:S], src[0:64, 0:S]))
                    em.emit("vector", lambda e: e.tensor_mul(dst[:, 0:S], src[:, 0:S], coss[:, 0:S]))
                    em.emit("vector", lambda e: e.tensor_mul(rot[:, 0:S], rot[:, 0:S], sins[:, 0:S]))
                    em.emit("vector", lambda e: e.tensor_add(dst[:, 0:S], dst[:, 0:S], rot[:, 0:S]))

                def hash_sign3(src_f, h, out_bf):
                    em.dma(hsb[:, :], hp1[h * 128:(h + 1) * 128, :])
                    em.dma(hsb2[:, :], hp2[h * 128:(h + 1) * 128, :])
                    for (w0, w1) in wins(S):
                        em.emit("tensor", lambda e, _w0=w0, _w1=w1: e.matmul(
                            p0[:, 0:_w1 - _w0], hsb[:, :], src_f[:, _w0:_w1], start=True, stop=True))
                        em.emit("scalar", lambda e, _w0=w0, _w1=w1, _h=h: e.activation(
                            yb[:, _w0:_w1], p0[:, 0:_w1 - _w0], AF.Silu, bias=hb1s[:, _h:_h + 1]))
                    em.emit("vector", lambda e: e.tensor_add(yb[:, 0:S], yb[:, 0:S], src_f[:, 0:S]))
                    for (w0, w1) in wins(S):
                        em.emit("tensor", lambda e, _w0=w0, _w1=w1: e.matmul(
                            p0[:, 0:_w1 - _w0], hsb2[:, :], yb[:, _w0:_w1], start=True, stop=True))
                        em.emit("vector", lambda e, _w0=w0, _w1=w1, _h=h: e.scalar_tensor_tensor(
                            zb[:, _w0:_w1], p0[:, 0:_w1 - _w0], hb2s[:, _h:_h + 1],
                            yb[:, _w0:_w1], op0=ALU.add, op1=ALU.add))
                    em.emit("scalar", lambda e: e.activation(out_bf[:, 0:S], zb[:, 0:S], AF.Sign))

                for kv in range(KVH1):
                    load_wslice3(wb["l1_wkT"], kv)
                    proj_full3(kf)
                    rope3(krf, kf)
                    em.emit("vector", lambda e: e.tensor_copy(krb, krf[:, :]))
                    em.dma(krT[kv * 128:(kv + 1) * 128, :], krb)
                    load_wslice3(wb["l1_wvT"], kv)
                    proj_full3(vT)
                    for ti in range(NT):
                        em.emit("tensor", lambda e, _ti=ti: e.transpose(pT[:, :], vT[:, _ti * 128:(_ti + 1) * 128], idb[:, :]))
                        em.emit("vector", lambda e: e.tensor_copy(vstg[:, :], pT[:, :]))
                        em.dma(vtmo[kv * 128:(kv + 1) * 128, ti * 128:(ti + 1) * 128], vstg[:, :])
                    for g in range(GRP):
                        h = kv * GRP + g
                        hash_sign3(krf, h, outb)
                        em.dma(khT[h * 128:(h + 1) * 128, :], outb[:, 0:S])
                        load_wslice3(wb["l1_wqT"], h)
                        proj_full3(qf)
                        rope3(qrf, qf)
                        em.emit("scalar", lambda e: e.mul(outb[:, 0:S], qrf[:, 0:S], scale_s))
                        em.dma(qrT[h * 128:(h + 1) * 128, :], outb[:, 0:S])
                        hash_sign3(qrf, h, outb)
                        em.dma(qhT[h * 128:(h + 1) * 128, :], outb[:, 0:S])

        _stage4()
        def _stage5():
            with contextlib.ExitStack() as sctx:
                cs = lambda n, sh, dt: sctx.enter_context(nc.sbuf_tensor(n, sh, dt))
                qr = cs("qr4", [128, S], BF16)
                qh = cs("qh4", [128, S], BF16)
                kr = cs("kr4", [128, S], BF16)
                kh = cs("kh4", [128, S], BF16)
                vtm = cs("vtm4", [128, S], BF16)
                ub = cs("ub4", [128, S], F32)
                sgn = cs("sgn4", [128, S], F32)
                sco = cs("sco4", [128, S], F32)
                sel = cs("sel4", [128, S], mybir.dt.uint32)
                rev = cs("rev4", [128, S], F32)
                negb = cs("negb4", [128, S], F32)
                prb = cs("prb4", [128, S], BF16)
                pts = cs("pts4", [128, 128], BF16)
                astg = cs("astg4", [128, 128], BF16)
                t9 = cs("t94", [128, 128], F32)
                t30 = cs("t304", [128, 128], F32)
                nmb = cs("nmb4", [128, 1], F32)
                cnt = cs("cnt4", [128, 1], F32)
                tmp1 = cs("tmp14", [128, 1], F32)
                kap = cs("kap4", [128, 1], F32)
                nmax = cs("nmax4", [128, 1], F32)
                ssum = cs("ssum4", [128, 1], F32)
                em.dma(t9[:, :], tri9h)
                em.dma(t30[:, :], tri30h)
                em.dma(rev[:, :], revbc)
                em.emit("vector", lambda e: e.memset(negb[:, :], NEG30))
                for h in range(H1):
                    kv = h // GRP
                    em.dma(qr[:, :], qrT[h * 128:(h + 1) * 128, :])
                    em.dma(qh[:, :], qhT[h * 128:(h + 1) * 128, :], wait=False)
                    em.dma(kr[:, :], krT[kv * 128:(kv + 1) * 128, :], wait=False)
                    em.dma(kh[:, :], khT[h * 128:(h + 1) * 128, :], wait=False)
                    em.dma(vtm[:, :], vtmo[kv * 128:(kv + 1) * 128, :], wait=False)
                    for qi in range(NT):
                        W = (qi + 1) * 128
                        need_sel = W > KREM
                        if need_sel:
                            for (w0, w1) in wins(W):
                                em.emit("tensor", lambda e, _qi=qi, _w0=w0, _w1=w1: e.matmul(
                                    p0[:, 0:_w1 - _w0], qh[:, _qi * 128:(_qi + 1) * 128],
                                    kh[:, _w0:_w1], start=True, stop=True))
                                em.emit("vector", lambda e, _w0=w0, _w1=w1: e.tensor_copy(ub[:, _w0:_w1], p0[:, 0:_w1 - _w0]))
                            em.emit("vector", lambda e, _qi=qi: e.tensor_add(
                                ub[:, _qi * 128:(_qi + 1) * 128], ub[:, _qi * 128:(_qi + 1) * 128], t9[:, :]))
                            em.emit("vector", lambda e, _W=W: e.tensor_scalar(ub[:, 0:_W], ub[:, 0:_W], float(S), None, op0=ALU.mult))
                            em.emit("vector", lambda e, _W=W: e.tensor_add(ub[:, 0:_W], ub[:, 0:_W], rev[:, 0:_W]))
                            step0 = float(2 ** (nsteps - 1))
                            em.emit("vector", lambda e, _s=step0: e.memset(nmb[:, :], -lo_init - _s + 0.5))
                            thr = float(2 * KREM - W)
                            for it in range(nsteps):
                                step = float(2 ** (nsteps - 1 - it))
                                nstep = float(2 ** (nsteps - 2 - it)) if it < nsteps - 1 else 0.5
                                em.emit("scalar", lambda e, _W=W: e.activation(sgn[:, 0:_W], ub[:, 0:_W], AF.Sign, bias=nmb[:, :]))
                                em.emit("vector", lambda e, _W=W: e.tensor_reduce(cnt[:, :], sgn[:, 0:_W], AX.X, ALU.add))
                                em.emit("vector", lambda e, _t=thr, _s=step: e.tensor_scalar(
                                    tmp1[:, :], cnt[:, :], _t, -_s, op0=ALU.is_ge, op1=ALU.mult))
                                em.emit("vector", lambda e, _ns=nstep: e.scalar_tensor_tensor(
                                    nmb[:, :], tmp1[:, :], _ns, nmb[:, :], op0=ALU.add, op1=ALU.add))
                            em.emit("vector", lambda e: e.tensor_scalar(kap[:, :], nmb[:, :], -1.0, None, op0=ALU.mult))
                            em.emit("vector", lambda e, _W=W: e.tensor_scalar(sel[:, 0:_W], ub[:, 0:_W], kap[:, :], None, op0=ALU.is_lt))
                        for (w0, w1) in wins(W):
                            em.emit("tensor", lambda e, _qi=qi, _w0=w0, _w1=w1: e.matmul(
                                p0[:, 0:_w1 - _w0], qr[:, _qi * 128:(_qi + 1) * 128],
                                kr[:, _w0:_w1], start=True, stop=True))
                            em.emit("vector", lambda e, _w0=w0, _w1=w1: e.tensor_copy(sco[:, _w0:_w1], p0[:, 0:_w1 - _w0]))
                        em.emit("vector", lambda e, _qi=qi: e.tensor_add(
                            sco[:, _qi * 128:(_qi + 1) * 128], sco[:, _qi * 128:(_qi + 1) * 128], t30[:, :]))
                        if need_sel:
                            em.emit("vector", lambda e, _W=W: e.copy_predicated(sco[:, 0:_W], sel[:, 0:_W], negb[:, 0:_W]))
                        em.emit("vector", lambda e, _W=W: e.tensor_reduce(nmax[:, :], sco[:, 0:_W], AX.X, ALU.max, negate=True))
                        em.emit("scalar", lambda e, _W=W: e.activation(sco[:, 0:_W], sco[:, 0:_W], AF.Exp, bias=nmax[:, :]))
                        em.emit("vector", lambda e, _W=W: e.tensor_reduce(ssum[:, :], sco[:, 0:_W], AX.X, ALU.add))
                        em.emit("vector", lambda e: e.reciprocal(ssum[:, :], ssum[:, :]))
                        em.emit("vector", lambda e, _W=W: e.tensor_scalar(prb[:, 0:_W], sco[:, 0:_W], ssum[:, :], None, op0=ALU.mult))
                        for ti in range(qi + 1):
                            em.emit("tensor", lambda e, _ti=ti: e.transpose(pT[:, :], prb[:, _ti * 128:(_ti + 1) * 128], idb[:, :]))
                            em.emit("vector", lambda e: e.tensor_copy(pts[:, :], pT[:, :]))
                            em.emit("tensor", lambda e, _ti=ti, _qi=qi: e.matmul(
                                pA[:, :], vtm[:, _ti * 128:(_ti + 1) * 128], pts[:, :],
                                start=(_ti == 0), stop=(_ti == _qi)))
                        em.emit("vector", lambda e: e.tensor_copy(astg[:, :], pA[:, :]))
                        em.dma(attn1T[h * 128:(h + 1) * 128, qi * 128:(qi + 1) * 128], astg[:, :])

        _stage5()
        def _stage6():
            with contextlib.ExitStack() as sctx:
                cs = lambda n, sh, dt: sctx.enter_context(nc.sbuf_tensor(n, sh, dt))
                act = cs("act5", [128, CK * S], BF16)
                wstr = cs("wstr5", [128, CK * 512], BF16)
                gws = cs("gws5", [128, CK * NE], BF16)
                afs = cs("afs5", [128, CK * ER], BF16)
                bstr = cs("bstr5", [128, NKC * 512], BF16)
                hT = cs("hT5", [128, NT * NKC * 128], BF16)
                reps = cs("reps5", [NE, ER], F32)
                lg = cs("lg5", [128, NE], F32)
                lg2 = cs("lg25", [128, NE], F32)
                eq1 = cs("eq15", [128, NE], F32)
                eq2 = cs("eq25", [128, NE], F32)
                gate = cs("gate5", [128, NE], F32)
                gtT = cs("gtT5", [NE, 128], F32)
                hb = cs("hb5", [128, ER], F32)
                grep = cs("grep5", [128, ER], F32)
                m1 = cs("m15", [128, 1], F32)
                m2 = cs("m25", [128, 1], F32)
                dm = cs("dm5", [128, 1], F32)
                p1s = cs("p1s5", [128, 1], F32)
                p2s = cs("p2s5", [128, 1], F32)
                xres = cs("xres5", [128, 512], F32)
                wop = cs("wop5", [128, 512], F32)
                obuf = cs("obuf5", [128, 512], F32)
                actv = act[:, :].rearrange("p (ck c) -> p ck c", c=S)
                dma_chunked(em, actv, attn1T.rearrange("(ck p) c -> p ck c", p=128), CK)
                for (w0, w1) in wins(D):
                    n = w1 - w0
                    dma_chunked(em, wstr[:, 0:CK * n].rearrange("p (ck c) -> p ck c", c=n),
                                wb["l1_woT"][:, w0:w1].rearrange("(ck p) c -> p ck c", p=128), CK)
                    for j in range(NT):
                        for ck in range(CK):
                            em.emit("tensor", lambda e, _ck=ck, _j=j, _n=n: e.matmul(
                                p0[:, 0:_n], act[:, _ck * S + _j * 128: _ck * S + (_j + 1) * 128],
                                wstr[:, _ck * _n:(_ck + 1) * _n],
                                start=(_ck == 0), stop=(_ck == CK - 1)), wait=(ck == 0))
                        em.emit("vector", lambda e, _n=n: e.tensor_copy(obuf[:, 0:_n], p0[:, 0:_n]))
                        em.dma(scrC[j * 128:(j + 1) * 128, w0:w1], obuf[:, 0:n])
                dma_chunked(em, actv, xn1To.rearrange("(ck p) c -> p ck c", p=128), CK)
                em.dma(gws[:, :].rearrange("p (ck c) -> p ck c", c=NE), wb["l1_gwT"].rearrange("(ck p) c -> p ck c", p=128))
                em.dma(afs[:, :].rearrange("p (ck c) -> p ck c", c=ER), wb["l1_afT"].rearrange("(ck p) c -> p ck c", p=128))
                em.dma(reps[:, :], repch)
                for j in range(NT):
                    for ck in range(CK):
                        em.emit("tensor", lambda e, _ck=ck, _j=j: e.matmul(
                            p0[:, 0:NE], act[:, _ck * S + _j * 128: _ck * S + (_j + 1) * 128],
                            gws[:, _ck * NE:(_ck + 1) * NE],
                            start=(_ck == 0), stop=(_ck == CK - 1)), wait=(ck == 0))
                    em.emit("vector", lambda e: e.tensor_copy(lg[:, :], p0[:, 0:NE]))
                    em.emit("vector", lambda e: e.tensor_reduce(m1[:, :], lg[:, :], AX.X, ALU.max))
                    em.emit("vector", lambda e: e.tensor_scalar(eq1[:, :], lg[:, :], m1[:, :], None, op0=ALU.is_ge))
                    em.emit("vector", lambda e: e.tensor_scalar(lg2[:, :], eq1[:, :], NEG30, None, op0=ALU.mult))
                    em.emit("vector", lambda e: e.tensor_add(lg2[:, :], lg2[:, :], lg[:, :]))
                    em.emit("vector", lambda e: e.tensor_reduce(m2[:, :], lg2[:, :], AX.X, ALU.max))
                    em.emit("vector", lambda e: e.tensor_scalar(eq2[:, :], lg2[:, :], m2[:, :], None, op0=ALU.is_ge))
                    em.emit("vector", lambda e: e.tensor_sub(dm[:, :], m1[:, :], m2[:, :]))
                    em.emit("scalar", lambda e: e.activation(p1s[:, :], dm[:, :], AF.Sigmoid))
                    em.emit("vector", lambda e: e.tensor_sub(dm[:, :], m2[:, :], m1[:, :]))
                    em.emit("scalar", lambda e: e.activation(p2s[:, :], dm[:, :], AF.Sigmoid))
                    em.emit("vector", lambda e: e.tensor_scalar(eq1[:, :], eq1[:, :], p1s[:, :], 2.0, op0=ALU.mult, op1=ALU.mult))
                    em.emit("vector", lambda e: e.tensor_scalar(eq2[:, :], eq2[:, :], p2s[:, :], 2.0, op0=ALU.mult, op1=ALU.mult))
                    em.emit("vector", lambda e: e.tensor_add(gate[:, :], eq1[:, :], eq2[:, :]))
                    for ck in range(CK):
                        em.emit("tensor", lambda e, _ck=ck, _j=j: e.matmul(
                            p0[:, 0:ER], act[:, _ck * S + _j * 128: _ck * S + (_j + 1) * 128],
                            afs[:, _ck * ER:(_ck + 1) * ER],
                            start=(_ck == 0), stop=(_ck == CK - 1)), wait=(ck == 0))
                    em.emit("vector", lambda e: e.tensor_copy(hb[:, :], p0[:, 0:ER]))
                    em.emit("tensor", lambda e: e.transpose(pf[0:NE, :], gate[:, 0:NE], idf[:, :]))
                    em.emit("vector", lambda e: e.tensor_copy(gtT[:, :], pf[0:NE, :]))
                    em.emit("tensor", lambda e: e.matmul(p0[:, 0:ER], gtT[:, :], reps[:, :], start=True, stop=True))
                    em.emit("vector", lambda e: e.tensor_tensor(grep[:, :], hb[:, :], p0[:, 0:ER], op=ALU.mult))
                    for kc in range(NKC):
                        c0 = kc * 128
                        c1 = min(ER, c0 + 128)
                        em.emit("tensor", lambda e, _c0=c0, _c1=c1: e.transpose(pf[0:_c1 - _c0, :], grep[:, _c0:_c1], idf[:, :]))
                        em.emit("vector", lambda e, _j=j, _kc=kc, _c0=c0, _c1=c1: e.tensor_copy(
                            hT[0:_c1 - _c0, (_j * NKC + _kc) * 128:(_j * NKC + _kc) * 128 + 128], pf[0:_c1 - _c0, :]))
                for (w0, w1) in wins(D):
                    n = w1 - w0
                    dma_chunked(em, wstr[:, 0:CK * n].rearrange("p (ck c) -> p ck c", c=n),
                                wb["l1_wbT"][:, w0:w1].rearrange("(ck p) c -> p ck c", p=128), CK)
                    em.dma(bstr[:, 0:NKC * n].rearrange("p (kc c) -> p kc c", c=n),
                           wb["l1_bfl"][:, w0:w1].rearrange("(kc p) c -> p kc c", p=128))
                    for j in range(NT):
                        for ck in range(CK):
                            em.emit("tensor", lambda e, _ck=ck, _j=j, _n=n: e.matmul(
                                p0[:, 0:_n], act[:, _ck * S + _j * 128: _ck * S + (_j + 1) * 128],
                                wstr[:, _ck * _n:(_ck + 1) * _n],
                                start=(_ck == 0), stop=False), wait=(ck == 0))
                        for kc in range(NKC):
                            c0 = kc * 128
                            c1 = min(ER, c0 + 128)
                            em.emit("tensor", lambda e, _j=j, _kc=kc, _c0=c0, _c1=c1, _n=n: e.matmul(
                                p0[:, 0:_n], hT[0:_c1 - _c0, (_j * NKC + _kc) * 128:(_j * NKC + _kc) * 128 + 128],
                                bstr[0:_c1 - _c0, _kc * _n:(_kc + 1) * _n],
                                start=False, stop=(_kc == NKC - 1)), wait=False)
                        em.dma(xres[:, 0:n], x1[j * 128:(j + 1) * 128, w0:w1])
                        em.dma(wop[:, 0:n], scrC[j * 128:(j + 1) * 128, w0:w1])
                        em.emit("vector", lambda e, _n=n: e.tensor_add(obuf[:, 0:_n], p0[:, 0:_n], xres[:, 0:_n]))
                        em.emit("vector", lambda e, _n=n: e.tensor_add(obuf[:, 0:_n], obuf[:, 0:_n], wop[:, 0:_n]))
                        em.dma(out[j * 128:(j + 1) * 128, w0:w1], obuf[:, 0:n])
        _stage6()
        em.build_block()
    return nc


def run_fused(cfg, ins, ncores, runner):
    B, S, D, H1 = cfg["B"], cfg["S"], cfg["D"], cfg["H1"]
    x0 = np.ascontiguousarray(np.asarray(ins["L0_attn_hidden_states"], np.float32).reshape(B, S, D))
    consts = make_consts(cfg)
    w = prep_weights(cfg, ins)
    bb_bc = np.ascontiguousarray(np.tile(np.asarray(ins["L0_ffn_b_base"], np.float32)[None, :], (128, 1)))
    hb1 = np.ascontiguousarray(np.asarray(ins["L1_attn_hash_bias1"], np.float32).reshape(H1, 128).T)
    hb2 = np.ascontiguousarray(np.asarray(ins["L1_attn_hash_bias2"], np.float32).reshape(H1, 128).T)
    hp1 = np.ascontiguousarray(np.asarray(ins["L1_attn_hash_proj1"], np.float32).reshape(H1 * 128, 128))
    hp2 = np.ascontiguousarray(np.asarray(ins["L1_attn_hash_proj2"], np.float32).reshape(H1 * 128, 128))
    nc = build_fused(cfg)
    base = {("wf_" + k): v for k, v in w.items()}
    base.update({"hp1": hp1, "hp2": hp2, "hb1": hb1, "hb2": hb2,
                 "cosA": consts["cosA"], "sinA": consts["sinA"],
                 "identb": consts["identb"], "identf": consts["identf"],
                 "tri30": consts["tri30"], "tri9": consts["tri9"],
                 "revbc": consts["revbc"], "repc": consts["repc"], "bb": bb_bc})
    in_maps = []
    for c in range(ncores):
        m = dict(base)
        m["x"] = x0[(c // 2) % B]
        in_maps.append(m)
    res = runner(nc, in_maps, list(range(ncores)))
    outf = np.zeros((B, S, D), np.float32)
    for b in range(B):
        outf[b] = np.asarray(res[2 * b]["out"]) if 2 * b < ncores else np.asarray(res[b]["out"])
    return outf


def kernel(**ins):
    return run_fused(full_cfg(), ins, ncores=8, runner=_hw_runner)
